# version 35
# baseline (speedup 1.0000x reference)
import contextlib

import numpy as np
import ml_dtypes

import concourse.bass as bass
import concourse.mybir as mybir
from concourse.bass_utils import run_bass_kernel_spmd

F32 = mybir.dt.float32
BF16 = mybir.dt.bfloat16
F16 = mybir.dt.float16
I32 = mybir.dt.int32
AF = mybir.ActivationFunctionType
OP = mybir.AluOpType

D = 2048
S = 4096
T = 2048
NCH = 16
HA, DA = 16, 128
HB, DB, FB = 32, 64, 64
I_FFN = 5632
GU_OUT = 2 * I_FFN
GOT = GU_OUT // 128
DOT = D // 128
ICH = I_FFN // 128
RSQRT_DA = float(1.0 / np.sqrt(128.0))
LN_EPS = 1e-5
RMS_EPS = 1e-6
ATTB_EPS = 1e-12

ENG = ("sync", "gpsimd", "vector", "scalar", "tensor")


class Sem:
    def __init__(self, handle, name):
        self.h = handle
        self.name = name
        self.v = 0


class Arena:

    def __init__(self, ap, nbytes):
        self.ap = ap
        self.size = nbytes
        self.off = 0
        self.marks = []

    def push(self):
        self.marks.append(self.off)

    def pop(self):
        self.off = self.marks.pop()

    def _carve(self, prow, dtype, shape):
        n = int(np.prod(shape)) * mybir.dt.size(dtype)
        self.off = (self.off + 63) & ~63
        assert self.off + n <= self.size, f"arena overflow: {self.off}+{n} > {self.size}"
        v = self.ap[0:prow, self.off:self.off + n].bitcast(dtype)
        self.off += n
        if len(shape) > 1:
            names = " ".join(f"d{i}" for i in range(len(shape)))
            v = v.rearrange(f"p ({names}) -> p {names}",
                            **{f"d{i}": s for i, s in enumerate(shape)})
        return v

    def take(self, dtype, *shape):
        return self._carve(128, dtype, shape)

    def row(self, dtype, n):
        return self._carve(1, dtype, (n,))


class Sched:
    def __init__(self):
        self.steps = []
        self.race_edges = False
        self.engine_sems = {}

    def op(self, eng, fn, waits=(), inc=None, inc_n=1):
        incs = []
        ws = list(waits)
        auto = None
        if self.race_edges and fn is not None and eng in self.engine_sems:
            es = self.engine_sems[eng]
            if es.v > 0:
                ws.append((es, es.v))
            if inc is not es:
                auto = es
        if inc is not None:
            inc.v += inc_n
            incs.append((inc, inc_n))
        tok = inc.v if inc is not None else None
        if auto is not None:
            auto.v += 1
            incs.append((auto, 1))
        ws2 = tuple((s, v) for (s, v) in ws if v > 0)
        self.steps.append((eng, fn, ws2, tuple(incs)))
        return tok

    def run(self, eng_name, eng):
        for e, fn, waits, incs in self.steps:
            if e != eng_name:
                continue
            for sem, val in waits:
                eng.wait_ge(sem.h, val)
            instr = fn(eng) if fn is not None else None
            if incs:
                instr.then_inc(incs[0][0].h, incs[0][1])
                for sem, n in incs[1:]:
                    eng.nop().then_inc(sem.h, n)


def build(stop_after="full", race_edges=False):
    nc = bass.Bass("TRN2", target_bir_lowering=False, debug=False)

    def din(name, shape, dt):
        return nc.dram_tensor(name, list(shape), dt, kind="ExternalInput").ap()

    xT = din("xT", (D, S), F32)
    Wq_a = din("Wq_a", (D, D), F32)
    bq_a = din("bq_a", (128, NCH), F32)
    Wk_a = din("Wk_a", (D, DA), F32)
    bk_a = din("bk_a", (DA, 1), F32)
    Wv_a = din("Wv_a", (D, DA), F32)
    bv_a = din("bv_a", (1, DA), F32)
    Wo_a = din("Wo_a", (D, D), F32)
    bo_a = din("bo_a", (128, NCH), F32)
    Wq_b = din("Wq_b", (D, D), F32)
    Wk_b = din("Wk_b", (D, D), F32)
    Wv_b = din("Wv_b", (D, D), F32)
    Wo_b = din("Wo_b", (D, D), F32)
    Wfq = din("Wfq", (HB, DB, FB), F32)
    Wfk = din("Wfk", (HB, DB, FB), F32)
    gup_sh = din("gup_sh", (11 * 128, D // 8), I32)
    gus_sh = din("gus_sh", (D // 128, 11 * 128), F16)
    dp_sh = din("dp_sh", (2 * 128, I_FFN // 8), I32)
    dsc_sh = din("dsc_sh", (ICH, 2 * 128), F16)
    pfx8 = din("pfx8", (128, 8), F32)

    outT = nc.dram_tensor("outT", [D, T], F32, kind="ExternalOutput").ap()
    dbgT = None
    if stop_after == "full_dbg":
        dbgT = nc.dram_tensor("dbgT", [D, T], F32, kind="ExternalOutput").ap()

    def dint(name, shape, dt):
        return nc.dram_tensor(name, list(shape), dt, kind="Internal").ap()

    res1 = dint("res1", (D, S), F32)
    res2 = dint("res2", (D, S), F32)
    res3 = dint("res3", (D, T), F32)
    lnsp = dint("lnsp", (D, S), BF16)
    WbT = dint("WbT", (3, 16, NCH, 128, 128), BF16)
    WoTb = dint("WoTb", (NCH, DOT, 128, 128), BF16)
    Wqa_bf = dint("Wqa_bf", (D, D), BF16)
    Woa_bf = dint("Woa_bf", (D, D), BF16)
    GU_SH = 11 * NCH * 128
    D_SH = 2 * ICH * 128
    cwI = dint("cwI", (128, GU_SH + D_SH), BF16)
    cwO = dint("cwO", (8, 128, GU_SH + D_SH), BF16)
    Yall2 = dint("Yall2", (16, T, 128), BF16)
    fqsp = dint("fqsp", (16, 128, T), BF16)
    fksp = dint("fksp", (16, 128, T), BF16)
    fknsp = dint("fknsp", (16, T, 128), BF16)
    v2sp = dint("v2sp", (16, T, 130), BF16)
    cstI = dint("cstI", (128, 16 * 65), F32)
    cstO = dint("cstO", (8, 128, 16 * 65), F32)
    ckvI = dint("ckvI", (128, 4096), BF16)
    ckvO = dint("ckvO", (2, 128, 4096), BF16)

    sched = Sched()
    stack = contextlib.ExitStack()
    ec = stack.enter_context
    wb_cp_tok = [(0, 0)]

    def sb(name, shape, dt, **kw):
        return ec(nc.sbuf_tensor(name, list(shape), dt, **kw))

    def psum(name, shape, dt):
        return ec(nc.psum_tensor(name, list(shape), dt))

    def sem(name):
        return Sem(ec(nc.semaphore(name)), name)

    ones1 = sb("ones1", (1, 128), F32)
    onescol = sb("onescol", (128, 1), BF16)
    epscol = sb("epscol", (128, 1), BF16)
    maskT = sb("maskT", (128, 128), BF16)
    ident = sb("ident", (128, 128), BF16)
    bvb = sb("bvb", (128, DA), BF16)
    bq_sb = sb("bq_sb", (128, NCH), F32)
    bo_sb = sb("bo_sb", (128, NCH), F32)
    bk_sb = sb("bk_sb", (128, 1), F32)
    wk_sb = sb("wk_sb", (128, NCH, DA), BF16)
    wv_sb = sb("wv_sb", (128, NCH, DA), BF16)
    pfx_sb = sb("pfx_sb", (128, 8), F32)
    rows = sb("rows", (1, 3584), F32)
    rows2 = sb("rows2", (1, 1024), F32)
    bv_row = sb("bv_row", (1, DA), F32)
    ident_f32 = sb("ident_f32", (128, 128), F32)
    trd = sb("trd", (128, GU_OUT), BF16, is_transpose_dest=True)
    AR_BYTES = 161024
    arena = Arena(sb("arena", (128, AR_BYTES), mybir.dt.uint8), AR_BYTES)

    psA = psum("psA", (128, 1024), F32)
    psB = psum("psB", (128, 1024), F32)
    psC = psum("psC", (128, 1024), F32)
    psD = psum("psD", (128, 512), F32)
    psD2 = psum("psD2", (128, 512), F32)

    sLD = sem("sLD")
    sLD2 = sem("sLD2")
    sPE = sem("sPE")
    sDV = sem("sDV")
    sAC = sem("sAC")
    sGP = sem("sGP")
    if race_edges:
        sched.race_edges = True
        sched.engine_sems = {"vector": sDV, "scalar": sAC, "tensor": sPE}
    sST = sem("sST")
    sIN = sem("sIN")
    sWKV = sem("sWKV")
    sXFp = [sem("sXF0"), sem("sXF1")]
    sSP = sem("sSP")
    sLNB = sem("sLNB")
    sWQ2p = [sem("sWQ2a"), sem("sWQ2b")]
    sWO2p = [sem("sWO2a"), sem("sWO2b")]
    sXR = sem("sXR")
    sRS = sem("sRS")
    sPKp = [sem("sPKa"), sem("sPKb")]
    sSCp = [sem("sSCa"), sem("sSCb")]
    sWSp = [sem("sWSa"), sem("sWSb")]
    sTR = sem("sTR")
    sTS = sem("sTS")
    sWBp = [sem("sWBa"), sem("sWBb")]
    sWBSp = [sem("sWBSa"), sem("sWBSb")]
    sWGp = [sem("sWGa"), sem("sWGb")]
    sWDp = [sem("sWDa"), sem("sWDb")]
    sBD = sem("sBD")
    sW3p = [sem("sW3a"), sem("sW3b")]
    sFK = sem("sFK")
    sFKN = sem("sFKN")
    sV2 = sem("sV2")
    sFQ = sem("sFQ")
    sCS = sem("sCS")
    sCC = sem("sCC")
    sCCW = sem("sCCW")
    sKV = sem("sKV")
    sCC2 = sem("sCC2")
    sKVL = sem("sKVL")
    sP3L = sem("sP3L")
    sYS = sem("sYS")
    sYB = sem("sYB")

    xT_v = xT.rearrange("(c p) t -> p c t", p=128)
    lnsp_v = lnsp.rearrange("(c p) t -> p c t", p=128)
    res1_v = res1.rearrange("(c p) t -> p c t", p=128)
    res2_v = res2.rearrange("(c p) t -> p c t", p=128)
    res3_v = res3.rearrange("(c p) t -> p c t", p=128)
    outT_v = outT.rearrange("(c p) t -> p c t", p=128)
    Wqa_bf_v = Wqa_bf.rearrange("(c p) o -> p c o", p=128)
    Woa_bf_v = Woa_bf.rearrange("(c p) o -> p c o", p=128)

    barrier_sems = {}

    def barrier(tag):
        bar = barrier_sems.setdefault(tag, sem("bar_" + tag))
        for e in ENG:
            sched.op(e, lambda x: x.nop(), inc=bar, inc_n=1)
        target = bar.v
        for e in ENG:
            sched.op(e, None, waits=[(bar, target)])

    def init_consts():
        sched.op("gpsimd", lambda g: g.memset(ones1[:], 1.0), inc=sGP)
        sched.op("gpsimd", lambda g: g.memset(onescol[:], 1.0), inc=sGP)
        sched.op("gpsimd", lambda g: g.memset(epscol[:], ATTB_EPS), inc=sGP)
        sched.op("gpsimd", lambda g: g.memset(maskT[:], 1.0), inc=sGP)
        t0 = sched.op("gpsimd", lambda g: g.memset(ident[:], 0.0), inc=sGP)
        sched.op("gpsimd", lambda g: g.affine_select(
            out=maskT[:], in_=maskT[:], compare_op=OP.is_ge, fill=0.0,
            base=0, pattern=[[1, 128]], channel_multiplier=-1),
            waits=[(sGP, t0)], inc=sGP)
        sched.op("gpsimd", lambda g: g.affine_select(
            out=ident[:], in_=ident[:], compare_op=OP.not_equal, fill=1.0,
            base=0, pattern=[[-1, 128]], channel_multiplier=1),
            waits=[(sGP, t0)], inc=sGP)
        t0b = sched.op("gpsimd", lambda g: g.memset(ident_f32[:], 0.0), inc=sGP)
        sched.op("gpsimd", lambda g: g.affine_select(
            out=ident_f32[:], in_=ident_f32[:], compare_op=OP.not_equal, fill=1.0,
            base=0, pattern=[[-1, 128]], channel_multiplier=1),
            waits=[(sGP, t0b)], inc=sGP)
        sched.op("sync", lambda e: e.dma_start(out=bq_sb[:], in_=bq_a[:]), inc=sIN, inc_n=16)
        sched.op("sync", lambda e: e.dma_start(out=bo_sb[:], in_=bo_a[:]), inc=sIN, inc_n=16)
        sched.op("sync", lambda e: e.dma_start(out=bk_sb[:], in_=bk_a[:]), inc=sIN, inc_n=16)
        sched.op("sync", lambda e: e.dma_start(out=pfx_sb[:], in_=pfx8[:]), inc=sIN, inc_n=16)
        sched.op("gpsimd", lambda g: g.dma_start(
            out=wk_sb[:], in_=Wk_a.rearrange("(c p) d -> p c d", p=128)), inc=sWKV, inc_n=16)
        sched.op("gpsimd", lambda g: g.dma_start(
            out=wv_sb[:], in_=Wv_a.rearrange("(c p) d -> p c d", p=128)), inc=sWKV, inc_n=16)
        tok = sched.op("sync", lambda e: e.dma_start(out=bv_row[:], in_=bv_a[:]), inc=sIN, inc_n=16)
        mt = sched.op("tensor", lambda t: t.matmul(
            psD[:, 0:DA], ones1[:], bv_row[:], start=True, stop=True),
            waits=[(sIN, tok), (sGP, sGP.v)], inc=sPE)
        sched.op("vector", lambda v: v.tensor_copy(bvb[:], psD[:, 0:DA]),
                 waits=[(sPE, mt)], inc=sDV)

    arena.push()
    kT_sb = arena.take(BF16, S)
    v_sb = arena.take(BF16, S // 128, DA)
    wb_wbl = [arena.take(BF16, D) for _ in range(2)]
    wb_wstg = [arena.take(BF16, NCH, 128) for _ in range(2)]

    def p1_pass1(wb_emit=None):
        NSL = T // 512
        arena.push()
        xf = [arena.take(F32, 512) for i in range(2)]
        xb = arena.take(BF16, NCH, 512)
        xsq = arena.take(BF16, 512)
        mrow_b = rows[:, 2560:3072]
        isrow_b = rows[:, 3072:3584]
        mb = arena.take(F32, 512)
        ib = arena.take(F32, 512)
        lnf1 = arena.take(F32, 512)
        ln = arena.take(BF16, NCH, 512)

        st = {
            "xf_tok": [0, 0], "xf_free": [0, 0],
            "psD_free_dv": 0,
            "psD2_free_dv": 0,
            "xb_free_pe": 0,
            "xb_free_ac": 0,
            "ln_free_pe": 0,
            "ln_free_st": 0,
            "k_evac": 0,
            "v_evac": [0, 0],
            "rows_free_pe": 0,
        }

        for s in range(NSL):
            par = s % 2
            sl = slice(s * 512, (s + 1) * 512)
            cast_toks = []
            for c in range(NCH):
                cpar = c % 2
                st["xf_tok"][cpar] = sched.op(
                    "sync", lambda e, c=c, cpar=cpar, sl=sl: e.dma_start(
                        out=xf[cpar][:], in_=xT_v[:, c, sl]),
                    waits=[(sDV, st["xf_free"][cpar])], inc=sXFp[cpar], inc_n=16)
                w = [(sXFp[cpar], st["xf_tok"][cpar])]
                if c == 0:
                    w += [(sPE, st["xb_free_pe"]), (sAC, st["xb_free_ac"])]
                tok = sched.op("vector", lambda v, cpar=cpar, c=c: v.tensor_copy(xb[:, c], xf[cpar][:]),
                               waits=w, inc=sDV)
                cast_toks.append(tok)
                st["xf_free"][cpar] = sDV.v
            prev_sxx = 0
            sq_last = 0
            for c in range(NCH):
                w = [(sDV, cast_toks[c])]
                if c > 0:
                    w.append((sPE, prev_sxx))
                sq_last = sched.op("scalar", lambda a, c=c: a.activation(xsq[:], xb[:, c], AF.Square),
                                   waits=w, inc=sAC)
                w2 = [(sDV, cast_toks[c])]
                if c == 0:
                    w2.append((sDV, st["psD_free_dv"]))
                sched.op("tensor", lambda t, c=c: t.matmul(
                    psD[0:1, :], onescol[:], xb[:, c], start=(c == 0), stop=(c == NCH - 1)),
                    waits=w2)
                w3 = [(sAC, sq_last)]
                if c == 0:
                    w3.append((sDV, st["psD2_free_dv"]))
                prev_sxx = sched.op("tensor", lambda t, c=c: t.matmul(
                    psD2[0:1, :], onescol[:], xsq[:], start=(c == 0), stop=(c == NCH - 1)),
                    waits=w3, inc=sPE)
            st["xb_free_ac"] = sq_last
            stats_done = prev_sxx
            t1 = sched.op("vector", lambda v: v.tensor_scalar(
                rows[:, 0:512], psD[0:1, :], 1.0 / D, None, OP.mult),
                waits=[(sPE, stats_done), (sPE, st["rows_free_pe"])], inc=sDV)
            t2 = sched.op("vector", lambda v: v.tensor_scalar(
                rows[:, 1024:1536], psD2[0:1, :], 1.0 / D, float(LN_EPS),
                OP.mult, OP.add), inc=sDV)
            t3 = sched.op("scalar", lambda a: a.activation(
                rows[:, 512:1024], rows[:, 0:512], AF.Square), waits=[(sDV, t1)], inc=sAC)
            t4 = sched.op("vector", lambda v: v.tensor_sub(
                rows[:, 1536:2048], rows[:, 1024:1536], rows[:, 512:1024]),
                waits=[(sAC, t3)], inc=sDV)
            t5 = sched.op("scalar", lambda a: a.activation(
                rows[:, 2048:2560], rows[:, 1536:2048], AF.Ln),
                waits=[(sDV, t4)], inc=sAC)
            t6 = sched.op("scalar", lambda a: a.activation(
                rows2[:, 0:512], rows[:, 2048:2560], AF.Exp, scale=-0.5),
                waits=[(sAC, t5)], inc=sAC)
            t7 = sched.op("vector", lambda v: v.tensor_copy(mrow_b[:], rows[:, 0:512]), inc=sDV)
            t8 = sched.op("vector", lambda v: v.tensor_copy(isrow_b[:], rows2[:, 0:512]),
                          waits=[(sAC, t6)], inc=sDV)
            b1 = sched.op("tensor", lambda t: t.matmul(
                psD[:, :], ones1[:], mrow_b[:], start=True, stop=True),
                waits=[(sDV, t8)], inc=sPE)
            b2 = sched.op("tensor", lambda t: t.matmul(
                psD2[:, :], ones1[:], isrow_b[:], start=True, stop=True), inc=sPE)
            st["rows_free_pe"] = b2
            m1 = sched.op("vector", lambda v: v.tensor_copy(mb[:], psD[:, :]),
                          waits=[(sPE, b2)], inc=sDV)
            st["psD_free_dv"] = m1
            i1 = sched.op("vector", lambda v: v.tensor_copy(ib[:], psD2[:, :]), inc=sDV)
            st["psD2_free_dv"] = i1
            ln_toks = []
            for c in range(NCH):
                cpar = c % 2
                st["xf_tok"][cpar] = sched.op(
                    "sync", lambda e, c=c, cpar=cpar, sl=sl: e.dma_start(
                        out=xf[cpar][:], in_=xT_v[:, c, sl]),
                    waits=[(sDV, st["xf_free"][cpar])], inc=sXFp[cpar], inc_n=16)
                w = [(sDV, i1), (sXFp[cpar], st["xf_tok"][cpar])]
                if c == 0:
                    w += [(sPE, st["ln_free_pe"]), (sSP, st["ln_free_st"])]
                ct = sched.op("vector", lambda v, cpar=cpar: v.tensor_sub(
                    lnf1[:], xf[cpar][:], mb[:]), waits=w, inc=sDV)
                st["xf_free"][cpar] = ct
                tok = sched.op("vector", lambda v, c=c: v.tensor_mul(ln[:, c], lnf1[:], ib[:]),
                               inc=sDV)
                ln_toks.append(tok)
            if s < T // 512:
                st["ln_free_st"] = sched.op("sync", lambda e, sl=sl: e.dma_start(
                    out=lnsp_v[:, :, sl], in_=ln[:]),
                    waits=[(sDV, ln_toks[-1])], inc=sSP, inc_n=16)
            for c in range(NCH):
                w = [(sDV, ln_toks[c])]
                if c == 0:
                    w += [(sWKV, 32), (sAC, st["k_evac"])]
                sched.op("tensor", lambda t, c=c: t.matmul(
                    psA[:, 0:512], wk_sb[:, c], ln[:, c], start=(c == 0), stop=(c == NCH - 1)),
                    waits=w, inc=sPE if c == NCH - 1 else None)
            ktok = sPE.v
            st["k_evac"] = sched.op("scalar", lambda a, sl=sl: a.add(
                kT_sb[:, sl], psA[:, 0:512], bk_sb[:, 0:1]),
                waits=[(sPE, ktok)], inc=sAC)
            last_v_pe = 0
            for tt_ in range(4):
                vpar = tt_ % 2
                reg = psB[:, vpar * 512:vpar * 512 + 128]
                for c in range(NCH):
                    w = [(sDV, ln_toks[c])]
                    if c == 0:
                        w.append((sDV, st["v_evac"][vpar]))
                    sched.op("tensor", lambda t, tt_=tt_, c=c, reg=reg: t.matmul(
                        reg, ln[:, c, tt_ * 128:(tt_ + 1) * 128], wv_sb[:, c],
                        start=(c == 0), stop=(c == NCH - 1)),
                        waits=w, inc=sPE if c == NCH - 1 else None)
                vtok = sPE.v
                last_v_pe = vtok
                st["v_evac"][vpar] = sched.op("vector", lambda v, s=s, tt_=tt_, reg=reg: v.tensor_add(
                    v_sb[:, s * 4 + tt_], reg, bvb[:]),
                    waits=[(sPE, vtok)], inc=sDV)
            st["ln_free_pe"] = last_v_pe
            st["xb_free_pe"] = last_v_pe
            if wb_emit is not None:
                wb_emit(32)
        if wb_emit is not None:
            wb_emit(None)
        sched.op("sync", lambda e: e.dma_start(
            out=ckvI[:, 0:2048], in_=kT_sb[:, 0:2048]),
            waits=[(sAC, st["k_evac"])], inc=sKV, inc_n=16)
        sched.op("sync", lambda e: e.dma_start(
            out=ckvI[:, 2048:4096],
            in_=v_sb.rearrange("p a b -> p (a b)")[:, 0:2048]),
            waits=[(sDV, max(st["v_evac"]))], inc=sKV, inc_n=16)
        for m in range(2):
            sched.op("sync", lambda e, m=m: e.dma_start(
                out=kT_sb[:, m * 2048:(m + 1) * 2048], in_=ckvO[m, :, 0:2048]),
                waits=[(sCC2, 1)], inc=sKVL, inc_n=16)
            sched.op("sync", lambda e, m=m: e.dma_start(
                out=v_sb.rearrange("p a b -> p (a b)")[:, m * 2048:(m + 1) * 2048],
                in_=ckvO[m, :, 2048:4096]),
                inc=sKVL, inc_n=16)
        arena.pop()

    def p1_pass2():
        NBLK = T // 1024
        NKV = S // 128
        arena.push()
        sched.op("sync", None, waits=[(sSP, sSP.v)])
        lnb = arena.take(BF16, NCH, 1024)
        qh = arena.take(BF16, 1024)
        expb = [arena.take(BF16, 1024) for i in range(2)]
        acc = arena.take(F32, 1024)
        acc_bf = arena.take(BF16, 1024)
        out_t = arena.take(BF16, NCH, 1024)
        wq_t = [arena.take(BF16, NCH, 128) for i in range(2)]
        recb = arena.take(F32, 512)
        wob_t = [arena.take(BF16, NCH, 128) for i in range(2)]
        wo_res = arena.take(F32, 1024)
        wo_res2 = arena.take(F32, 1024)
        xres = arena.take(F32, 1024)
        den_sb = arena.row(F32, 1024)
        rec_row = arena.row(F32, 1024)

        cp0, cp1 = wb_cp_tok[0]
        sched.op("scalar", None, waits=[(sWBSp[0], cp0), (sWBSp[1], cp1)])
        st = {
            "wq_tok": [0, 0], "wq_free": [0, 0],
            "wo_tok": [0, 0], "wo_free": [0, 0],
            "lnb_free": 0,
            "q_evac": 0,
            "qh_free": 0,
            "out_done": 0,
            "exp_last": 0,
            "wo_evac": 0,
            "wo_evac_b": [0, 0], "res_store_b": [0, 0],
            "wo_mm": 0,
            "res_store": 0,
            "xres_free": 0,
            "den_free": 0,
            "den_mm": 0,
        }

        for blk in range(NBLK):
            bsl = slice(blk * 1024, (blk + 1) * 1024)
            lnb_tok = sched.op("sync", lambda e, bsl=bsl: e.dma_start(
                out=lnb[:], in_=lnsp_v[:, :, bsl]),
                waits=[(sPE, st["lnb_free"])], inc=sLNB, inc_n=16)
            for h in range(HA):
                wpar = h % 2
                st["wq_tok"][wpar] = sched.op("scalar", lambda a, h=h, wpar=wpar: a.dma_start(
                    out=wq_t[wpar][:], in_=Wqa_bf_v[:, :, h * 128:(h + 1) * 128]),
                    waits=[(sPE, st["wq_free"][wpar])], inc=sWQ2p[wpar], inc_n=16)
                for c in range(NCH):
                    w = []
                    if c == 0:
                        w = [(sWQ2p[wpar], st["wq_tok"][wpar]), (sLNB, lnb_tok),
                             (sDV, st["out_done"]), (sAC, st["wo_evac"])]
                    for half in range(2):
                        sched.op("tensor", lambda t, c=c, half=half, wpar=wpar: t.matmul(
                            psC[:, half * 512:(half + 1) * 512], wq_t[wpar][:, c],
                            lnb[:, c, half * 512:(half + 1) * 512],
                            start=(c == 0), stop=(c == NCH - 1)),
                            waits=w if half == 0 else (),
                            inc=sPE if (c == NCH - 1 and half == 1) else None)
                qtok = sPE.v
                st["wq_free"][wpar] = qtok
                qev = sched.op("scalar", lambda a, h=h: a.add(
                    qh[:], psC[:, :], bq_sb[:, h:h + 1]),
                    waits=[(sPE, qtok), (sPE, st["qh_free"])], inc=sAC)
                exp_tok = {}
                attn_mm = {}
                acc_tok = {}

                def emit_attnv(kv):
                    par = kv % 2
                    w1 = [(sAC, exp_tok[kv])]
                    if kv == 0:
                        w1 += [(sAC, qev), (sDV, st["out_done"]), (sAC, st["wo_evac"])]
                    for half in range(2):
                        sched.op("tensor", lambda t, kv=kv, half=half, par=par: t.matmul(
                            psC[:, half * 512:(half + 1) * 512], v_sb[:, kv],
                            expb[par][:, half * 512:(half + 1) * 512],
                            start=(kv == 0), stop=(kv == NKV - 1)),
                            waits=w1 if half == 0 else (),
                            inc=sPE if half == 1 else None)
                    attn_mm[kv] = sPE.v
                    wacc = [(sAC, exp_tok[kv])]
                    if kv == 0:
                        acc_tok[kv] = sched.op("vector", lambda v, par=par: v.tensor_copy(
                            acc[:], expb[par][:]), waits=wacc, inc=sDV)
                    else:
                        acc_tok[kv] = sched.op("vector", lambda v, par=par: v.tensor_add(
                            acc[:], acc[:], expb[par][:]), waits=wacc, inc=sDV)

                for kv in range(NKV):
                    par = kv % 2
                    pscore = psA if par == 0 else psB
                    w0 = [(sAC, qev)]
                    if kv >= 2:
                        w0.append((sAC, exp_tok[kv - 2]))
                    elif kv == 0:
                        w0.append((sAC, st["exp_last"]))
                        w0.append((sKVL, 64))
                    for half in range(2):
                        sched.op("tensor", lambda t, kv=kv, half=half, pscore=pscore: t.matmul(
                            pscore[:, half * 512:(half + 1) * 512],
                            kT_sb[:, kv * 128:(kv + 1) * 128],
                            qh[:, half * 512:(half + 1) * 512], start=True, stop=True),
                            waits=w0 if half == 0 else (),
                            inc=sPE if half == 1 else None)
                    sctok = sPE.v
                    if kv > 0:
                        emit_attnv(kv - 1)
                    wexp = [(sPE, sctok)]
                    if kv >= 2:
                        wexp.append((sPE, attn_mm[kv - 2]))
                        wexp.append((sDV, acc_tok[kv - 2]))
                    exp_tok[kv] = sched.op("scalar", lambda a, par=par, pscore=pscore: a.activation(
                        expb[par][:], pscore[:, :], AF.Exp, scale=RSQRT_DA),
                        waits=wexp, inc=sAC)
                emit_attnv(NKV - 1)
                av_tok = sPE.v
                st["qh_free"] = av_tok
                st["exp_last"] = exp_tok[NKV - 1]
                cb = sched.op("vector", lambda v: v.tensor_copy(acc_bf[:], acc[:]),
                              waits=[(sDV, acc_tok[NKV - 1]), (sPE, st["den_mm"])],
                              inc=sDV)
                sched.op("tensor", lambda t: t.matmul(
                    psD[0:1, :], onescol[:], acc_bf[:, 0:512], start=True, stop=True),
                    waits=[(sDV, cb)])
                den_mm = sched.op("tensor", lambda t: t.matmul(
                    psD2[0:1, :], onescol[:], acc_bf[:, 512:1024], start=True, stop=True),
                    inc=sPE)
                st["den_mm"] = den_mm
                d1 = sched.op("vector", lambda v: v.tensor_copy(den_sb[:, 0:512], psD[0:1, :]),
                              waits=[(sPE, den_mm), (sPE, st["den_free"])], inc=sDV)
                sched.op("vector", lambda v: v.tensor_copy(den_sb[:, 512:1024], psD2[0:1, :]),
                         inc=sDV)
                d3 = sched.op("vector", lambda v: v.reciprocal(rec_row[:], den_sb[:]), inc=sDV)
                out_last = 0
                for half in range(2):
                    pden = psD if half == 0 else psD2
                    bt = sched.op("tensor", lambda t, half=half, pden=pden: t.matmul(
                        pden[:, :], ones1[:], rec_row[:, half * 512:(half + 1) * 512],
                        start=True, stop=True),
                        waits=[(sDV, d3)], inc=sPE)
                    rv = sched.op("vector", lambda v, pden=pden: v.tensor_copy(recb[:], pden[:, :]),
                                  waits=[(sPE, bt)], inc=sDV)
                    out_last = sched.op("vector", lambda v, h=h, half=half: v.tensor_mul(
                        out_t[:, h, half * 512:(half + 1) * 512],
                        psC[:, half * 512:(half + 1) * 512], recb[:]),
                        waits=[(sPE, st["wo_mm"])] if (h == 0 and half == 0) else (),
                        inc=sDV)
                    st["den_free"] = bt
                st["out_done"] = out_last
            for ot in range(NCH):
                wpar = ot % 2
                st["wo_tok"][wpar] = sched.op("scalar", lambda a, ot=ot, wpar=wpar: a.dma_start(
                    out=wob_t[wpar][:], in_=Woa_bf_v[:, :, ot * 128:(ot + 1) * 128]),
                    waits=[(sPE, st["wo_free"][wpar])], inc=sWO2p[wpar], inc_n=16)
                xr_tok = sched.op("sync", lambda e, ot=ot, bsl=bsl: e.dma_start(
                    out=xres[:], in_=xT_v[:, ot, bsl]),
                    waits=[(sDV, st["xres_free"])], inc=sXR, inc_n=16)
                opar = ot % 2
                ppo = psC if opar == 0 else psA
                wrr = wo_res if opar == 0 else wo_res2
                for c in range(NCH):
                    w = []
                    if c == 0:
                        w = [(sWO2p[wpar], st["wo_tok"][wpar]), (sDV, st["out_done"]),
                             (sAC, st["wo_evac_b"][opar])]
                        if ot == 0:
                            w.append((sAC, st["q_evac"]))
                    for half in range(2):
                        sched.op("tensor", lambda t, c=c, half=half, wpar=wpar, ppo=ppo: t.matmul(
                            ppo[:, half * 512:(half + 1) * 512], wob_t[wpar][:, c],
                            out_t[:, c, half * 512:(half + 1) * 512],
                            start=(c == 0), stop=(c == NCH - 1)),
                            waits=w if half == 0 else (),
                            inc=sPE if (c == NCH - 1 and half == 1) else None)
                ottok = sPE.v
                st["wo_free"][wpar] = ottok
                st["wo_mm"] = ottok
                ev = sched.op("scalar", lambda a, ot=ot, ppo=ppo, wrr=wrr: a.add(
                    wrr[:], ppo[:, :], bo_sb[:, ot:ot + 1]),
                    waits=[(sPE, ottok), (sRS, st["res_store_b"][opar])], inc=sAC)
                st["wo_evac_b"][opar] = ev
                st["wo_evac"] = ev
                ad = sched.op("vector", lambda v, wrr=wrr: v.tensor_add(wrr[:], wrr[:], xres[:]),
                              waits=[(sAC, ev), (sXR, xr_tok)], inc=sDV)
                st["xres_free"] = ad
                st["res_store_b"][opar] = sched.op("sync", lambda e, ot=ot, bsl=bsl, wrr=wrr: e.dma_start(
                    out=res1_v[:, ot, bsl], in_=wrr[:]),
                    waits=[(sDV, ad)], inc=sRS, inc_n=16)
            st["lnb_free"] = sPE.v
        arena.pop()

    cwI_gu = cwI[:, 0:GU_SH].rearrange("p (j m) -> p j m", j=11)
    cwI_d = cwI[:, GU_SH:GU_SH + D_SH].rearrange("p (j m) -> p j m", j=2)
    cwo_gu = cwO[:, :, 0:GU_SH].rearrange("r p (j m) -> r p j m", j=11)
    cwo_d = cwO[:, :, GU_SH:GU_SH + D_SH].rearrange("r p (j m) -> r p j m", j=2)

    def p0_shard():
        arena.push()
        WMAX = I_FFN // 8
        pk = [arena.take(I32, WMAX) for _ in range(2)]
        a_all = trd[:, 0:11264].bitcast(I32).rearrange("p (a b) -> p a b", a=8)
        s_all = arena.take(I32, 8, WMAX)
        af = arena.take(BF16, 8, WMAX)
        sf = arena.take(BF16, 8, WMAX)
        e1 = arena.take(BF16, 8, WMAX)
        e2 = arena.take(BF16, 8, WMAX)
        mag = arena.take(BF16, 8, WMAX)
        wt0 = arena.take(BF16, 8 * WMAX)
        wt = [wt0, wt0]
        sc_h = [arena.take(F16, 128) for _ in range(2)]
        sc_f = [arena.take(F32, 128) for _ in range(2)]
        scl = [arena.take(F32, 64) for _ in range(2)]
        stg0 = arena.take(BF16, ICH, 128)
        stg = [stg0, stg0]

        st = {"pk_tok": [0, 0], "pk_free": [0, 0], "wt_free": 0,
              "sc_free": [0, 0], "trt_pe": [0, 0], "psD2_free": 0,
              "stg_prev": (sWSp[0], 0)}

        def one_tile(par, W, G, pk_src, sc_src, dst):
            st["pk_tok"][par] = sched.op("sync", lambda e, par=par, W=W, pk_src=pk_src: e.dma_start(
                out=pk[par][:, 0:W], in_=pk_src),
                waits=[(sDV, st["pk_free"][par])], inc=sPKp[par], inc_n=16)
            sct = sched.op("sync", lambda e, G=G, par=par, sc_src=sc_src: e.dma_start(
                out=sc_h[par][0:G, :], in_=sc_src),
                waits=[(sDV, st["sc_free"][par])], inc=sSCp[par], inc_n=16)
            cvt = sched.op("vector", lambda v, G=G, par=par: v.tensor_copy(
                sc_f[par][0:G, :], sc_h[par][0:G, :]),
                waits=[(sSCp[par], sct), (sPE, st["trt_pe"][par])], inc=sDV)
            st["sc_free"][par] = cvt
            trt = sched.op("tensor", lambda t, G=G, par=par: t.transpose(
                psD2[:, 0:G], sc_f[par][0:G, :], ident_f32[0:G, 0:G]),
                waits=[(sDV, cvt), (sDV, st["psD2_free"])], inc=sPE)
            st["trt_pe"][par] = trt
            sev = sched.op("vector", lambda v, par=par, G=G: v.tensor_copy(
                scl[par][:, 0:G], psD2[:, 0:G]),
                waits=[(sPE, trt)], inc=sDV)
            st["psD2_free"] = sev
            w0 = [(sPKp[par], st["pk_tok"][par])]
            for j in range(8):
                sched.op("vector", lambda v, j=j, par=par, W=W: v.tensor_scalar(
                    a_all[:, j, 0:W], pk[par][:, 0:W], 4 * j, 7,
                    OP.logical_shift_right, OP.bitwise_and),
                    waits=w0 if j == 0 else ())
                sched.op("vector", lambda v, j=j, par=par, W=W: v.tensor_scalar(
                    s_all[:, j, 0:W], pk[par][:, 0:W], 4 * j + 3, 1,
                    OP.logical_shift_right, OP.bitwise_and))
            tok = sched.op("vector", lambda v: v.engine_nop(), inc=sDV)
            st["pk_free"][par] = tok
            n = 8 * W
            aw = a_all[:, :, 0:W]
            sw_ = s_all[:, :, 0:W]
            afw = af[:, :, 0:W]
            sfw = sf[:, :, 0:W]
            e1w = e1[:, :, 0:W]
            e2w = e2[:, :, 0:W]
            magw = mag[:, :, 0:W]
            c1 = sched.op("vector", lambda v: v.tensor_copy(afw, aw), waits=[(sDV, tok)], inc=sDV)
            c2 = sched.op("vector", lambda v: v.tensor_copy(sfw, sw_), inc=sDV)
            c3 = sched.op("vector", lambda v: v.tensor_scalar(
                e1w, afw, 1.0, -2.0, OP.mult, OP.add), waits=[(sDV, c1)], inc=sDV)
            c4 = sched.op("vector", lambda v: v.tensor_scalar(
                e2w, afw, 2.0, -8.0, OP.mult, OP.add), inc=sDV)
            c5 = sched.op("vector", lambda v: v.tensor_tensor(magw, e1w, e2w, OP.max),
                          waits=[(sDV, c4)], inc=sDV)
            c6 = sched.op("vector", lambda v: v.scalar_tensor_tensor(
                e1w, afw, 0.5, magw, OP.mult, OP.max), waits=[(sDV, c5)], inc=sDV)
            c7 = sched.op("vector", lambda v: v.tensor_scalar(
                e2w, sfw, -2.0, 1.0, OP.mult, OP.add), waits=[(sDV, c2)], inc=sDV)
            wt_v = wt[par][:, 0:n].rearrange("p (w j) -> p j w", j=8)
            gtok = 0
            for g in range(G):
                w = []
                if g == 0:
                    w = [(sPE, st["wt_free"]), (sDV, c7), (sDV, sev)]
                gtok = sched.op("vector", lambda v, g=g, par=par, wt_v=wt_v: v.scalar_tensor_tensor(
                    wt_v[:, :, 16 * g:16 * g + 16], e1[:, :, 16 * g:16 * g + 16],
                    scl[par][:, g:g + 1], e2[:, :, 16 * g:16 * g + 16],
                    OP.mult, OP.mult), waits=w, inc=sDV)
            wtf = wt[par][:, 0:n]
            ev = {}
            for ck in range(G):
                reg = psC[:, (ck % 2) * 512:(ck % 2) * 512 + 128]
                w = []
                if ck == 0:
                    w = [(sDV, gtok)]
                if ck >= 2:
                    w.append((sDV, ev[ck - 2]))
                else:
                    w.append((sDV, st["psC_ev"][ck % 2]))
                trp = sched.op("tensor", lambda t, ck=ck, reg=reg, wtf=wtf: t.matmul(
                    reg, wtf[:, ck * 128:(ck + 1) * 128], ident[:],
                    start=True, stop=True), waits=w, inc=sPE)
                wev = [(sPE, trp)]
                if ck == 0:
                    wev.append(st["stg_prev"])
                ev[ck] = sched.op("vector", lambda v, ck=ck, par=par, reg=reg: v.tensor_copy(
                    stg[par][:, ck], reg), waits=wev, inc=sDV)
            st["psC_ev"][(G - 1) % 2] = ev[G - 1]
            st["psC_ev"][G % 2] = ev[G - 2]
            st["wt_free"] = sPE.v
            stok = sched.op("sync", lambda e, par=par, G=G, dst=dst: e.dma_start(
                out=dst, in_=stg[par].rearrange("p a b -> p (a b)")[:, 0:G * 128]),
                waits=[(sDV, ev[G - 1])], inc=sWSp[par], inc_n=16)
            st["stg_prev"] = (sWSp[par], stok)

        st["psC_ev"] = [0, 0]
        for j in range(11):
            one_tile(j % 2, 256, 16,
                     gup_sh[j * 128:(j + 1) * 128, :],
                     gus_sh[:, j * 128:(j + 1) * 128],
                     cwI_gu[:, j])
        for j in range(2):
            one_tile(j % 2, I_FFN // 8, ICH,
                     dp_sh[j * 128:(j + 1) * 128, :],
                     dsc_sh[:, j * 128:(j + 1) * 128],
                     cwI_d[:, j])
        sched.op("sync", None, waits=[(sDV, sDV.v), (sPE, sPE.v)])
        arena.pop()

    def make_p0_wb(wbl, wstg):
        st = {"ld": [0, 0], "ld_free": [(sPE, 0), (sPE, 0)],
              "stg_store": [0, 0], "psC_ev": [0, 0]}
        mats = [Wq_b, Wk_b, Wv_b, Wo_b]
        t_blocks = [("t", m, hg) for m in range(4) for hg in range(16)]
        cp = [(Wq_a, Wqa_bf), (Wo_a, Woa_bf)]
        c_blocks = [("c", m, rt) for m in range(2) for rt in range(16)]
        blocks = []
        for i in range(32):
            blocks.append(c_blocks[i])
            blocks.append(t_blocks[i])
        blocks += t_blocks[32:]
        cp_left = [len(c_blocks)]
        cursor = [0]

        def emit_copy(m, rt, par):
            src, dst = cp[m]
            ld = sched.op("gpsimd", lambda g, par=par, src=src, rt=rt: g.dma_start(
                out=wbl[par][:], in_=src[rt * 128:(rt + 1) * 128, :]),
                waits=[st["ld_free"][par]], inc=sWBp[par], inc_n=16)
            tok = sched.op("sync", lambda e, par=par, dst=dst, rt=rt: e.dma_start(
                out=dst[rt * 128:(rt + 1) * 128, :], in_=wbl[par][:]),
                waits=[(sWBp[par], ld)], inc=sWBSp[par], inc_n=16)
            st["ld_free"][par] = (sWBSp[par], tok)

        def emit_one(m, hg, par):
            st["ld"][par] = sched.op("gpsimd", lambda g, m=m, hg=hg, par=par: g.dma_start(
                out=wbl[par][:], in_=mats[m][hg * 128:(hg + 1) * 128, :]),
                waits=[st["ld_free"][par]], inc=sWBp[par], inc_n=16)
            ev = {}
            for ck in range(NCH):
                reg = psC[:, (ck % 2) * 512:(ck % 2) * 512 + 128]
                w = []
                if ck == 0:
                    w.append((sWBp[par], st["ld"][par]))
                if ck >= 2:
                    w.append((sDV, ev[ck - 2]))
                else:
                    w.append((sDV, st["psC_ev"][ck % 2]))
                trp = sched.op("tensor", lambda t, ck=ck, par=par, reg=reg: t.matmul(
                    reg, wbl[par][:, ck * 128:(ck + 1) * 128], ident[:],
                    start=True, stop=True), waits=w, inc=sPE)
                wev = [(sPE, trp)]
                if ck == 0:
                    wev.append((sWBSp[par], st["stg_store"][par]))
                ev[ck] = sched.op("vector", lambda v, ck=ck, par=par, reg=reg: v.tensor_copy(
                    wstg[par][:, ck], reg), waits=wev, inc=sDV)
            st["psC_ev"][(NCH - 1) % 2] = ev[NCH - 1]
            st["psC_ev"][NCH % 2] = ev[NCH - 2]
            st["ld_free"][par] = (sPE, sPE.v)
            if m < 3:
                dst = WbT[m, hg].rearrange("ck p q -> p ck q")
            else:
                dst = WoTb[:, hg].rearrange("ck p q -> p ck q")
            st["stg_store"][par] = sched.op("sync", lambda e, par=par, dst=dst: e.dma_start(
                out=dst, in_=wstg[par][:]),
                waits=[(sDV, ev[NCH - 1])], inc=sWBSp[par], inc_n=16)

        def emit(n=None):
            hi = len(blocks) if n is None else min(cursor[0] + n, len(blocks))
            while cursor[0] < hi:
                kind, m, i = blocks[cursor[0]]
                if kind == "t":
                    emit_one(m, i, cursor[0] % 2)
                else:
                    emit_copy(m, i, cursor[0] % 2)
                    cp_left[0] -= 1
                    if cp_left[0] == 0:
                        wb_cp_tok[0] = (sWBSp[0].v, sWBSp[1].v)
                cursor[0] += 1
        return emit

    def ffn_phase(src_v, dst_v, TOK, dump=False):
        W = 512
        NSL = TOK // W
        arena.push()
        xf = [arena.take(F32, W) for _ in range(2)]
        xbt = [arena.take(BF16, W) for _ in range(2)]
        ln2 = [arena.take(BF16, NCH, W) for _ in range(2)]
        hbuf = arena.take(BF16, ICH, W)
        wgu = [arena.take(BF16, NCH, 128) for _ in range(2)]
        upt = arena.take(BF16, W)
        ybuf = arena.take(F32, NCH, W)
        ysqt = arena.take(BF16, W)
        invrb = arena.take(F32, W)
        tmpf = [arena.take(F32, W) for _ in range(2)]
        outf = [arena.take(F32, W) for _ in range(2)]
        rt = [arena.take(F32, W) for _ in range(2)]
        xsqt = arena.take(BF16, W)
        mbv = arena.take(F32, W)
        ibv = arena.take(F32, W)
        lnf = arena.take(F32, W)
        rrms = arena.row(F32, 2048)
        mrow_b = rows[:, 2560:2560 + W]
        isrow_b = rows[:, 3072:3072 + W]
        wd = [trd[:, 0:I_FFN].rearrange("p (a b) -> p a b", b=128),
              trd[:, I_FFN:2 * I_FFN].rearrange("p (a b) -> p a b", b=128)]
        wdf = [trd[:, 0:I_FFN], trd[:, I_FFN:2 * I_FFN]]

        sched.op("sync", None, waits=[(sRS, sRS.v), (sCCW, 1),
                                      (sWBp[0], sWBp[0].v), (sWBp[1], sWBp[1].v),
                                      (sWBSp[0], sWBSp[0].v), (sWBSp[1], sWBSp[1].v)])
        st = {"xf_tok": [0, 0], "xf_free": [0, 0],
              "psD_free": 0, "psD2_free": 0, "rows_free_pe": 0,
              "ln_free": [0, 0], "ln_last": [0, 0],
              "silu_ac": [0, 0], "up_dv": [0, 0], "h_free": 0,
              "wgu_tok": [0, 0], "wgu_free": [0, 0],
              "wd_tok": [0, 0], "wd_free": [0, 0],
              "y_evac": [0, 0], "psC_free": 0, "rms_rows_pe": 0,
              "out_store": [0, 0], "rt_free": [0, 0], "up_cp": 0,
              "wgu_hoisted": False}

        def emit_ln(s):
            par = s % 2
            sl = slice(s * W, (s + 1) * W)
            prev_sxx = 0
            pe_toks = {}
            ac_toks = {}
            for c in range(NCH):
                cpar = c % 2
                st["xf_tok"][cpar] = sched.op("gpsimd", lambda g, c=c, cpar=cpar, sl=sl: g.dma_start(
                    out=xf[cpar][:], in_=src_v[:, c, sl]),
                    waits=[(sDV, st["xf_free"][cpar])], inc=sXFp[cpar], inc_n=16)
                wct = [(sXFp[cpar], st["xf_tok"][cpar])]
                if c >= 2:
                    wct += [(sPE, pe_toks[c - 2]), (sAC, ac_toks[c - 2])]
                ct = sched.op("vector", lambda v, cpar=cpar: v.tensor_copy(xbt[cpar][:], xf[cpar][:]),
                              waits=wct, inc=sDV)
                st["xf_free"][cpar] = ct
                w2 = [(sDV, ct)]
                if c > 0:
                    w2.append((sPE, prev_sxx))
                sq_last = sched.op("scalar", lambda a, cpar=cpar: a.activation(
                    xsqt[:], xbt[cpar][:], AF.Square), waits=w2, inc=sAC)
                ac_toks[c] = sq_last
                w3 = [(sDV, ct)]
                if c == 0:
                    w3.append((sDV, st["psD_free"]))
                sched.op("tensor", lambda t, c=c, cpar=cpar: t.matmul(
                    psD[0:1, 0:W], onescol[:], xbt[cpar][:],
                    start=(c == 0), stop=(c == NCH - 1)), waits=w3)
                w4 = [(sAC, sq_last)]
                if c == 0:
                    w4.append((sDV, st["psD2_free"]))
                prev_sxx = sched.op("tensor", lambda t, c=c: t.matmul(
                    psD2[0:1, 0:W], onescol[:], xsqt[:],
                    start=(c == 0), stop=(c == NCH - 1)), waits=w4, inc=sPE)
                pe_toks[c] = prev_sxx
            t1 = sched.op("vector", lambda v: v.tensor_scalar(
                rows[:, 0:W], psD[0:1, 0:W], 1.0 / D, None, OP.mult),
                waits=[(sPE, prev_sxx), (sPE, st["rows_free_pe"])], inc=sDV)
            t2 = sched.op("vector", lambda v: v.tensor_scalar(
                rows[:, 1024:1024 + W], psD2[0:1, 0:W], 1.0 / D, float(LN_EPS),
                OP.mult, OP.add), inc=sDV)
            t3 = sched.op("scalar", lambda a: a.activation(
                rows[:, 512:512 + W], rows[:, 0:W], AF.Square), waits=[(sDV, t1)], inc=sAC)
            t4 = sched.op("vector", lambda v: v.tensor_sub(
                rows[:, 1536:1536 + W], rows[:, 1024:1024 + W], rows[:, 512:512 + W]),
                waits=[(sAC, t3)], inc=sDV)
            t5 = sched.op("scalar", lambda a: a.activation(
                rows[:, 2048:2048 + W], rows[:, 1536:1536 + W], AF.Ln),
                waits=[(sDV, t4)], inc=sAC)
            t6 = sched.op("scalar", lambda a: a.activation(
                rows2[:, 0:W], rows[:, 2048:2048 + W], AF.Exp, scale=-0.5),
                waits=[(sAC, t5)], inc=sAC)
            t7 = sched.op("vector", lambda v: v.tensor_copy(mrow_b[:], rows[:, 0:W]), inc=sDV)
            t8 = sched.op("vector", lambda v: v.tensor_copy(isrow_b[:], rows2[:, 0:W]),
                          waits=[(sAC, t6)], inc=sDV)
            b1 = sched.op("tensor", lambda t: t.matmul(
                psD[:, 0:W], ones1[:], mrow_b[:], start=True, stop=True),
                waits=[(sDV, t8)], inc=sPE)
            b2 = sched.op("tensor", lambda t: t.matmul(
                psD2[:, 0:W], ones1[:], isrow_b[:], start=True, stop=True), inc=sPE)
            st["rows_free_pe"] = b2
            m1 = sched.op("vector", lambda v: v.tensor_copy(mbv[:], psD[:, 0:W]),
                          waits=[(sPE, b2)], inc=sDV)
            st["psD_free"] = m1
            i1 = sched.op("vector", lambda v: v.tensor_copy(ibv[:], psD2[:, 0:W]), inc=sDV)
            st["psD2_free"] = i1
            ln_last = 0
            for c in range(NCH):
                cpar = c % 2
                st["xf_tok"][cpar] = sched.op("gpsimd", lambda g, c=c, cpar=cpar, sl=sl: g.dma_start(
                    out=xf[cpar][:], in_=src_v[:, c, sl]),
                    waits=[(sDV, st["xf_free"][cpar])], inc=sXFp[cpar], inc_n=16)
                w = [(sXFp[cpar], st["xf_tok"][cpar]), (sDV, i1)]
                if c == 0:
                    w.append((sPE, st["ln_free"][par]))
                ct = sched.op("vector", lambda v, cpar=cpar: v.tensor_sub(
                    lnf[:], xf[cpar][:], mbv[:]), waits=w, inc=sDV)
                st["xf_free"][cpar] = ct
                ln_last = sched.op("vector", lambda v, c=c, par=par: v.tensor_mul(
                    ln2[par][:, c], lnf[:], ibv[:]), inc=sDV)
            st["ln_last"][par] = ln_last

        ost = {"s": None, "ot": 0}

        def emit_out_iter():
            if ost["s"] is None or ost["ot"] >= DOT:
                return
            ot = ost["ot"]
            par = ot % 2
            osl = slice(ost["s"] * W, (ost["s"] + 1) * W)
            ost["ot"] += 1
            rtok = sched.op("gpsimd", lambda g, ot=ot, osl=osl, par=par: g.dma_start(
                out=rt[par][:], in_=src_v[:, ot, osl]),
                waits=[(sDV, st["rt_free"][par])], inc=sXR, inc_n=16)
            sched.op("vector", lambda v, ot=ot, par=par: v.tensor_mul(
                tmpf[par][:], ybuf[:, ot], invrb[:]))
            atok = sched.op("vector", lambda v, par=par: v.tensor_add(
                outf[par][:], tmpf[par][:], rt[par][:]),
                waits=[(sXR, rtok), (sRS, st["out_store"][par])], inc=sDV)
            st["rt_free"][par] = atok
            st["out_store"][par] = sched.op("gpsimd", lambda g, ot=ot, osl=osl, par=par: g.dma_start(
                out=dst_v[:, ot, osl], in_=outf[par][:]),
                waits=[(sDV, atok)], inc=sRS, inc_n=16)

        emit_ln(0)
        for s in range(NSL):
            par = s % 2
            sl = slice(s * W, (s + 1) * W)
            ln_last = st["ln_last"][par]
            for ot0 in range(2):
                r_, jj = divmod(ot0, 2)
                st["wd_tok"][ot0] = sched.op("sync", lambda e, r_=r_, jj=jj, ot0=ot0: e.dma_start(
                    out=wdf[ot0], in_=cwo_d[r_, :, jj]),
                    waits=[(sPE, st["wd_free"][ot0]), (sCCW, 1)], inc=sWDp[ot0], inc_n=16)
            for i in range(88):
                wpar = i % 2
                if i >= 2 or not st["wgu_hoisted"]:
                    r_, jj = divmod(i, 11)
                    st["wgu_tok"][wpar] = sched.op("sync", lambda e, r_=r_, jj=jj, wpar=wpar: e.dma_start(
                        out=wgu[wpar].rearrange("p a b -> p (a b)"), in_=cwo_gu[r_, :, jj]),
                        waits=[(sPE, st["wgu_free"][wpar])], inc=sWGp[wpar], inc_n=16)
                ppar = i % 2
                pp = psA if ppar == 0 else psB
                for c in range(NCH):
                    w = []
                    if c == 0:
                        w = [(sWGp[wpar], st["wgu_tok"][wpar]), (sDV, ln_last),
                             (sAC, st["silu_ac"][ppar]), (sDV, st["up_dv"][ppar]),
                             (sDV, st["y_evac"][ppar])]
                    sched.op("tensor", lambda t, c=c, wpar=wpar, pp=pp, par=par: t.matmul(
                        pp[:, 0:W], wgu[wpar][:, c],
                        ln2[par][:, c], start=(c == 0), stop=(c == NCH - 1)),
                        waits=w, inc=sPE if c == NCH - 1 else None)
                mmtok = sPE.v
                st["wgu_free"][wpar] = mmtok
                if i < 44:
                    ev = sched.op("scalar", lambda a, i=i, pp=pp: a.activation(
                        hbuf[:, i], pp[:, 0:W], AF.Silu),
                        waits=[(sPE, mmtok), (sPE, st["h_free"])], inc=sAC)
                    st["silu_ac"][ppar] = ev
                else:
                    iu = i - 44
                    ev1 = sched.op("vector", lambda v, pp=pp, iu=iu: v.tensor_tensor(
                        upt[:], pp[:, 0:W], hbuf[:, iu], OP.mult),
                        waits=[(sPE, mmtok), (sAC, st["up_cp"])], inc=sDV)
                    st["up_cp"] = sched.op("scalar", lambda a, iu=iu: a.copy(
                        hbuf[:, iu], upt[:]), waits=[(sDV, ev1)], inc=sAC)
                    st["up_dv"][ppar] = ev1
                if i % 5 == 4:
                    emit_out_iter()
            for _ in range(DOT):
                emit_out_iter()
            ost["s"] = s
            ost["ot"] = 0
            h_done = st["up_cp"]
            st["ln_free"][par] = sPE.v
            if s + 1 < NSL:
                emit_ln(s + 1)
                for i0 in range(2):
                    r_, jj = divmod(i0, 11)
                    st["wgu_tok"][i0] = sched.op("sync", lambda e, r_=r_, jj=jj, i0=i0: e.dma_start(
                        out=wgu[i0].rearrange("p a b -> p (a b)"), in_=cwo_gu[r_, :, jj]),
                        waits=[(sPE, st["wgu_free"][i0])], inc=sWGp[i0], inc_n=16)
                st["wgu_hoisted"] = True
            sq_mm = 0
            for ot in range(DOT):
                wpar = ot % 2
                if ot >= 2:
                    r_, jj = divmod(ot, 2)
                    st["wd_tok"][wpar] = sched.op("sync", lambda e, r_=r_, jj=jj, wpar=wpar: e.dma_start(
                        out=wdf[wpar], in_=cwo_d[r_, :, jj]),
                        waits=[(sPE, st["wd_free"][wpar])], inc=sWDp[wpar], inc_n=16)
                pp = psA if ot % 2 == 0 else psB
                for c in range(ICH):
                    w = []
                    if c == 0:
                        w = [(sWDp[wpar], st["wd_tok"][wpar]), (sAC, h_done),
                             (sDV, st["y_evac"][ot % 2])]
                    sched.op("tensor", lambda t, c=c, wpar=wpar, pp=pp: t.matmul(
                        pp[:, 0:W], wd[wpar][:, c], hbuf[:, c],
                        start=(c == 0), stop=(c == ICH - 1)),
                        waits=w, inc=sPE if c == ICH - 1 else None)
                dtok = sPE.v
                st["wd_free"][wpar] = dtok
                yv = sched.op("vector", lambda v, ot=ot, pp=pp: v.tensor_copy(
                    ybuf[:, ot], pp[:, 0:W]), waits=[(sPE, dtok)], inc=sDV)
                st["y_evac"][ot % 2] = yv
                sqt = sched.op("scalar", lambda a, ot=ot: a.activation(
                    ysqt[:], ybuf[:, ot], AF.Square),
                    waits=[(sDV, yv), (sPE, sq_mm)], inc=sAC)
                w = [(sAC, sqt)]
                if ot == 0:
                    w.append((sDV, st["psC_free"]))
                sq_mm = sched.op("tensor", lambda t, ot=ot: t.matmul(
                    psC[0:1, 0:W], onescol[:], ysqt[:],
                    start=(ot == 0), stop=(ot == DOT - 1)), waits=w, inc=sPE)
            r1 = sched.op("vector", lambda v: v.tensor_scalar(
                rrms[:, 0:W], psC[0:1, 0:W], 1.0 / D, float(RMS_EPS), OP.mult, OP.add),
                waits=[(sPE, sq_mm), (sPE, st["rms_rows_pe"])], inc=sDV)
            r2 = sched.op("scalar", lambda a: a.activation(
                rrms[:, 512:512 + W], rrms[:, 0:W], AF.Ln), waits=[(sDV, r1)], inc=sAC)
            r3 = sched.op("scalar", lambda a: a.activation(
                rrms[:, 1024:1024 + W], rrms[:, 512:512 + W], AF.Exp, scale=-0.5),
                waits=[(sAC, r2)], inc=sAC)
            r4 = sched.op("vector", lambda v: v.tensor_copy(
                rrms[:, 1536:1536 + W], rrms[:, 1024:1024 + W]),
                waits=[(sAC, r3)], inc=sDV)
            b3 = sched.op("tensor", lambda t: t.matmul(
                psC[:, 0:W], ones1[:], rrms[0:1, 1536:1536 + W], start=True, stop=True),
                waits=[(sDV, r4)], inc=sPE)
            st["rms_rows_pe"] = b3
            iv = sched.op("vector", lambda v: v.tensor_copy(invrb[:], psC[:, 0:W]),
                          waits=[(sPE, b3)], inc=sDV)
            st["psC_free"] = iv
            st["h_free"] = sPE.v
        for ot in range(DOT):
            emit_out_iter()
        arena.pop()

    def p3_attnb():
        arena.push()
        rsk_all = arena.take(F32, 16, 32, 2)
        bdq_all = arena.take(BF16, 16, 128)
        bdk_all = arena.take(BF16, 16, 128)
        t0 = sched.op("gpsimd", lambda g: g.memset(bdq_all[:], 0.0), inc=sGP)
        sched.op("gpsimd", lambda g: g.memset(bdk_all[:], 0.0), inc=sGP)
        gp0 = sGP.v
        for hg in range(16):
            for hh in range(2):
                sched.op("gpsimd", lambda g, hg=hg, hh=hh: g.dma_start(
                    out=bdq_all[hh * 64:(hh + 1) * 64, hg, hh * 64:(hh + 1) * 64],
                    in_=Wfq[2 * hg + hh]),
                    waits=[(sGP, gp0)] if (hg == 0 and hh == 0) else (), inc=sBD, inc_n=16)
                sched.op("gpsimd", lambda g, hg=hg, hh=hh: g.dma_start(
                    out=bdk_all[hh * 64:(hh + 1) * 64, hg, hh * 64:(hh + 1) * 64],
                    in_=Wfk[2 * hg + hh]),
                    inc=sBD, inc_n=16)
        bd_tok = sBD.v

        lnst = {"psD_free": 0, "psD2_free": 0, "rows_free_pe": 0}

        def emit_ln512(src_v2, sl, ln_out, lbufs, ln_free_pe):
            xf, xbt, xsqt, lnf, mbv, ibv = lbufs
            prev_sxx = 0
            pe_toks = {}
            ac_toks = {}
            xf_tok = [lnst.get("xf_tok0", 0), lnst.get("xf_tok1", 0)]
            xf_free = [lnst.get("xf_free0", 0), lnst.get("xf_free1", 0)]
            for c in range(NCH):
                cpar = c % 2
                xf_tok[cpar] = sched.op("sync", lambda e, c=c, cpar=cpar, sl=sl: e.dma_start(
                    out=xf[cpar][:], in_=src_v2[:, c, sl]),
                    waits=[(sDV, xf_free[cpar])], inc=sXFp[cpar], inc_n=16)
                wct = [(sXFp[cpar], xf_tok[cpar])]
                if c >= 2:
                    wct += [(sPE, pe_toks[c - 2]), (sAC, ac_toks[c - 2])]
                ct = sched.op("vector", lambda v, cpar=cpar: v.tensor_copy(xbt[cpar][:], xf[cpar][:]),
                              waits=wct, inc=sDV)
                xf_free[cpar] = ct
                w2 = [(sDV, ct)]
                if c > 0:
                    w2.append((sPE, prev_sxx))
                sq = sched.op("scalar", lambda a, cpar=cpar: a.activation(
                    xsqt[:], xbt[cpar][:], AF.Square), waits=w2, inc=sAC)
                ac_toks[c] = sq
                w3 = [(sDV, ct)]
                if c == 0:
                    w3.append((sDV, lnst["psD_free"]))
                sched.op("tensor", lambda t, c=c, cpar=cpar: t.matmul(
                    psD[0:1, :], onescol[:], xbt[cpar][:],
                    start=(c == 0), stop=(c == NCH - 1)), waits=w3)
                w4 = [(sAC, sq)]
                if c == 0:
                    w4.append((sDV, lnst["psD2_free"]))
                prev_sxx = sched.op("tensor", lambda t, c=c: t.matmul(
                    psD2[0:1, :], onescol[:], xsqt[:],
                    start=(c == 0), stop=(c == NCH - 1)), waits=w4, inc=sPE)
                pe_toks[c] = prev_sxx
            t1 = sched.op("vector", lambda v: v.tensor_scalar(
                rows[:, 0:512], psD[0:1, :], 1.0 / D, None, OP.mult),
                waits=[(sPE, prev_sxx), (sPE, lnst["rows_free_pe"])], inc=sDV)
            sched.op("vector", lambda v: v.tensor_scalar(
                rows[:, 1024:1536], psD2[0:1, :], 1.0 / D, float(LN_EPS),
                OP.mult, OP.add), inc=sDV)
            t3 = sched.op("scalar", lambda a: a.activation(
                rows[:, 512:1024], rows[:, 0:512], AF.Square), waits=[(sDV, t1)], inc=sAC)
            t4 = sched.op("vector", lambda v: v.tensor_sub(
                rows[:, 1536:2048], rows[:, 1024:1536], rows[:, 512:1024]),
                waits=[(sAC, t3)], inc=sDV)
            t5 = sched.op("scalar", lambda a: a.activation(
                rows[:, 2048:2560], rows[:, 1536:2048], AF.Ln), waits=[(sDV, t4)], inc=sAC)
            t6 = sched.op("scalar", lambda a: a.activation(
                rows2[:, 0:512], rows[:, 2048:2560], AF.Exp, scale=-0.5),
                waits=[(sAC, t5)], inc=sAC)
            sched.op("vector", lambda v: v.tensor_copy(rows[:, 2560:3072], rows[:, 0:512]), inc=sDV)
            t8 = sched.op("vector", lambda v: v.tensor_copy(rows[:, 3072:3584], rows2[:, 0:512]),
                          waits=[(sAC, t6)], inc=sDV)
            b1 = sched.op("tensor", lambda t: t.matmul(
                psD[:, :], ones1[:], rows[0:1, 2560:3072], start=True, stop=True),
                waits=[(sDV, t8)], inc=sPE)
            b2 = sched.op("tensor", lambda t: t.matmul(
                psD2[:, :], ones1[:], rows[0:1, 3072:3584], start=True, stop=True), inc=sPE)
            lnst["rows_free_pe"] = b2
            m1 = sched.op("vector", lambda v: v.tensor_copy(mbv[:], psD[:, :]),
                          waits=[(sPE, b2)], inc=sDV)
            lnst["psD_free"] = m1
            i1 = sched.op("vector", lambda v: v.tensor_copy(ibv[:], psD2[:, :]), inc=sDV)
            lnst["psD2_free"] = i1
            ln_last = 0
            for c in range(NCH):
                cpar = c % 2
                xf_tok[cpar] = sched.op("sync", lambda e, c=c, cpar=cpar, sl=sl: e.dma_start(
                    out=xf[cpar][:], in_=src_v2[:, c, sl]),
                    waits=[(sDV, xf_free[cpar])], inc=sXFp[cpar], inc_n=16)
                w = [(sXFp[cpar], xf_tok[cpar]), (sDV, i1)]
                if c == 0:
                    w.append((sPE, ln_free_pe))
                ct = sched.op("vector", lambda v, cpar=cpar: v.tensor_sub(
                    lnf[:], xf[cpar][:], mbv[:]), waits=w, inc=sDV)
                xf_free[cpar] = ct
                ln_last = sched.op("vector", lambda v, c=c: v.tensor_mul(
                    ln_out[:, c], lnf[:], ibv[:]), inc=sDV)
            lnst["xf_tok0"], lnst["xf_tok1"] = xf_tok
            lnst["xf_free0"], lnst["xf_free1"] = xf_free
            return ln_last

        arena.push()
        lbufs = ([arena.take(F32, 512) for _ in range(2)],
                 [arena.take(BF16, 512) for _ in range(2)],
                 arena.take(BF16, 512), arena.take(F32, 512),
                 arena.take(F32, 512), arena.take(F32, 512))
        lbufs = (lbufs[0], lbufs[1], lbufs[2], lbufs[3], lbufs[4], lbufs[5])
        ln3 = arena.take(BF16, NCH, 1024)
        wql = [arena.take(BF16, NCH, 128) for _ in range(2)]
        wkl = [arena.take(BF16, NCH, 128) for _ in range(2)]
        wvl = [arena.take(BF16, NCH, 128) for _ in range(2)]
        kt = arena.take(BF16, 1024)
        qt = arena.take(BF16, 1024)
        vtf = arena.take(BF16, 1024)
        fk_sl = arena.take(BF16, 1024)
        fq_sl = arena.take(BF16, 1024)
        ek_tok = arena.take(BF16, 8, 128)
        fkn_sl = arena.take(BF16, 8, 128)
        v2_sl = arena.take(BF16, 8, 130)
        sks = arena.take(F32, 8, 2)
        accS = arena.take(F32, 16, 65)

        st = {"w_tok": [0, 0], "w_free": [0, 0], "ln_free": 0,
              "kt_free": 0, "qt_free": 0, "vtf_free": 0,
              "fk_store": 0, "fq_store": 0, "fkn_store": 0, "v2_store": 0,
              "psB_ac": 0, "psB_dv": 0, "ek_free": 0, "fkn_gp": 0,
              "psA_ev": 0, "zk_tev": 0, "state_pe": 0,
              "tr_ev": [0, 0], "wt3": [0, 0]}

        sched.op("gpsimd", lambda g: g.memset(v2_sl[:, :, 64:65], 1.0), inc=sGP)
        sched.op("gpsimd", lambda g: g.memset(v2_sl[:, :, 129:130], 1.0), inc=sGP)
        ones_gp = sGP.v

        sched.op("sync", None, waits=[(sRS, sRS.v),
                                      (sWBSp[0], sWBSp[0].v), (sWBSp[1], sWBSp[1].v)])
        NSL3 = T // 1024
        for s in range(NSL3):
            sl_lo = slice(s * 1024, s * 1024 + 512)
            sl_hi = slice(s * 1024 + 512, (s + 1) * 1024)
            emit_ln512(res2_v, sl_lo,
                       ln3.rearrange("p c (h w) -> p (c h) w", h=2)[:, 0::2] if False else ln3[:, :, 0:512],
                       lbufs, st["ln_free"])
            lnlast = emit_ln512(res2_v, sl_hi, ln3[:, :, 512:1024], lbufs, st["ln_free"])
            def wb_load(hg):
                par = hg % 2
                sched.op("sync", lambda e, hg=hg, par=par: e.dma_start(
                    out=wql[par][:], in_=WbT[0, hg].rearrange("ck p q -> p ck q")),
                    waits=[(sPE, st["w_free"][par])], inc=sW3p[par], inc_n=16)
                sched.op("sync", lambda e, hg=hg, par=par: e.dma_start(
                    out=wkl[par][:], in_=WbT[1, hg].rearrange("ck p q -> p ck q")),
                    inc=sW3p[par], inc_n=16)
                st["wt3"][par] = sched.op("sync", lambda e, hg=hg, par=par: e.dma_start(
                    out=wvl[par][:], in_=WbT[2, hg].rearrange("ck p q -> p ck q")),
                    inc=sW3p[par], inc_n=16)

            wb_load(0)
            for hg in range(16):
                par = hg % 2
                if hg + 1 < 16:
                    wb_load(hg + 1)
                wt3 = st["wt3"][par]
                for c in range(NCH):
                    w = []
                    if c == 0:
                        w = [(sW3p[par], wt3), (sDV, lnlast), (sDV, st["psA_ev"])]
                    for half in range(2):
                        sched.op("tensor", lambda t, c=c, par=par, half=half: t.matmul(
                            psA[:, half * 512:(half + 1) * 512], wkl[par][:, c],
                            ln3[:, c, half * 512:(half + 1) * 512],
                            start=(c == 0), stop=(c == NCH - 1)),
                            waits=w if half == 0 else (),
                            inc=sPE if (c == NCH - 1 and half == 1) else None)
                kmm = sPE.v
                ktv = sched.op("vector", lambda v: v.tensor_copy(kt[:], psA[:, :]),
                               waits=[(sPE, kmm), (sPE, st["kt_free"])], inc=sDV)
                st["psA_ev"] = ktv
                for half in range(2):
                    sched.op("tensor", lambda t, hg=hg, half=half: t.matmul(
                        psB[:, half * 512:(half + 1) * 512], bdk_all[:, hg],
                        kt[:, half * 512:(half + 1) * 512], start=True, stop=True),
                        waits=[(sDV, ktv), (sBD, bd_tok), (sAC, st["psB_ac"]),
                               (sDV, st["psB_dv"])] if half == 0 else (),
                        inc=sPE if half == 1 else None)
                fkmm = sPE.v
                fke = sched.op("scalar", lambda a: a.activation(fk_sl[:], psB[:, :], AF.Exp),
                               waits=[(sPE, fkmm), (sFK, st["fk_store"])], inc=sAC)
                st["psB_ac"] = fke
                st["fk_store"] = sched.op("sync", lambda e, hg=hg, s=s: e.dma_start(
                    out=fksp[hg][:, s * 1024:(s + 1) * 1024], in_=fk_sl[:]),
                    waits=[(sAC, fke)], inc=sFK, inc_n=16)
                for tt_ in range(8):
                    sched.op("tensor", lambda t, hg=hg, tt_=tt_: t.matmul(
                        psC[:, tt_ * 128:(tt_ + 1) * 128], kt[:, tt_ * 128:(tt_ + 1) * 128],
                        bdk_all[:, hg], start=True, stop=True),
                        waits=[(sDV, ktv), (sAC, st["ek_free"]), (sDV, st["zk_tev"])] if tt_ == 0 else (),
                        inc=sPE if tt_ == 7 else None)
                zt = sPE.v
                st["kt_free"] = zt
                eke = sched.op("scalar", lambda a: a.activation(
                    ek_tok.rearrange("p a b -> p (a b)"), psC[:, :], AF.Exp),
                    waits=[(sPE, zt), (sGP, st["fkn_gp"])], inc=sAC)
                st["ek_free"] = eke
                rdt = sched.op("vector", lambda v: v.reduce_sum(
                    sks[:], ek_tok.rearrange("p a (h f) -> p a h f", h=2),
                    axis=mybir.AxisListType.X),
                    waits=[(sAC, eke)], inc=sDV)
                rct = sched.op("vector", lambda v, hg=hg, s=s: v.reciprocal(
                    rsk_all[:, hg, s * 8:(s + 1) * 8], sks[:]),
                    waits=[(sDV, rdt)], inc=sDV)
                fknt = 0
                for tt_ in range(8):
                    for hh in range(2):
                        w = []
                        if tt_ == 0 and hh == 0:
                            w = [(sDV, rct), (sFKN, st["fkn_store"]),
                                 (sPE, st["state_pe"])]
                        fknt = sched.op("gpsimd", lambda g, tt_=tt_, hh=hh, hg=hg, s=s: g.tensor_scalar(
                            fkn_sl[:, tt_, hh * 64:(hh + 1) * 64],
                            ek_tok[:, tt_, hh * 64:(hh + 1) * 64],
                            rsk_all[:, hg, s * 8 + tt_, hh:hh + 1], None, OP.mult),
                            waits=w, inc=sGP)
                st["fkn_gp"] = fknt
                st["fkn_store"] = sched.op("sync", lambda e, hg=hg, s=s: e.dma_start(
                    out=fknsp[hg, s * 1024:(s + 1) * 1024].rearrange("(tt p) f -> p tt f", p=128),
                    in_=fkn_sl[:]),
                    waits=[(sGP, fknt)], inc=sFKN, inc_n=16)
                for c in range(NCH):
                    w = []
                    if c == 0:
                        w = [(sAC, fke), (sDV, lnlast)]
                    for half in range(2):
                        sched.op("tensor", lambda t, c=c, par=par, half=half: t.matmul(
                            psB[:, half * 512:(half + 1) * 512], wvl[par][:, c],
                            ln3[:, c, half * 512:(half + 1) * 512],
                            start=(c == 0), stop=(c == NCH - 1)),
                            waits=w if half == 0 else (),
                            inc=sPE if (c == NCH - 1 and half == 1) else None)
                vmm = sPE.v
                vev = sched.op("vector", lambda v: v.tensor_copy(vtf[:], psB[:, :]),
                               waits=[(sPE, vmm), (sPE, st["vtf_free"])], inc=sDV)
                st["psB_dv"] = vev
                for c in range(NCH):
                    w = []
                    if c == 0:
                        w = [(sW3p[par], wt3), (sDV, lnlast), (sDV, st["psA_ev"])]
                    for half in range(2):
                        sched.op("tensor", lambda t, c=c, par=par, half=half: t.matmul(
                            psA[:, half * 512:(half + 1) * 512], wql[par][:, c],
                            ln3[:, c, half * 512:(half + 1) * 512],
                            start=(c == 0), stop=(c == NCH - 1)),
                            waits=w if half == 0 else (),
                            inc=sPE if (c == NCH - 1 and half == 1) else None)
                qmm = sPE.v
                st["w_free"][par] = qmm
                qtv = sched.op("vector", lambda v: v.tensor_copy(qt[:], psA[:, :]),
                               waits=[(sPE, qmm), (sPE, st["qt_free"])], inc=sDV)
                st["psA_ev"] = qtv
                for half in range(2):
                    sched.op("tensor", lambda t, hg=hg, half=half: t.matmul(
                        psB[:, half * 512:(half + 1) * 512], bdq_all[:, hg],
                        qt[:, half * 512:(half + 1) * 512], start=True, stop=True),
                        waits=[(sDV, qtv), (sDV, st["psB_dv"]), (sAC, st["psB_ac"])] if half == 0 else (),
                        inc=sPE if half == 1 else None)
                fqmm = sPE.v
                st["qt_free"] = fqmm
                fqe = sched.op("scalar", lambda a: a.activation(fq_sl[:], psB[:, :], AF.Exp),
                               waits=[(sPE, fqmm), (sFQ, st["fq_store"])], inc=sAC)
                st["psB_ac"] = fqe
                st["fq_store"] = sched.op("sync", lambda e, hg=hg, s=s: e.dma_start(
                    out=fqsp[hg][:, s * 1024:(s + 1) * 1024], in_=fq_sl[:]),
                    waits=[(sAC, fqe)], inc=sFQ, inc_n=16)
                tregs = [psC[:, 0:128], psC[:, 512:640]]
                tev = {}
                v2t = 0
                for tt_ in range(8):
                    reg = tregs[tt_ % 2]
                    wtr = [(sDV, vev)] if tt_ == 0 else []
                    if tt_ < 2:
                        wtr += [(sDV, st["tr_ev"][tt_ % 2]), (sAC, eke)]
                    else:
                        wtr.append((sDV, tev[tt_ - 2]))
                    trp = sched.op("tensor", lambda t, tt_=tt_, reg=reg: t.matmul(
                        reg, vtf[:, tt_ * 128:(tt_ + 1) * 128], ident[:],
                        start=True, stop=True), waits=wtr, inc=sPE)
                    wev = [(sPE, trp)]
                    if tt_ == 0:
                        wev += [(sV2, st["v2_store"]), (sPE, st["state_pe"]),
                                (sGP, ones_gp)]
                    sched.op("vector", lambda v, tt_=tt_, reg=reg: v.tensor_copy(
                        v2_sl[:, tt_, 0:64], reg[:, 0:64]),
                        waits=wev, inc=sDV)
                    v2t = sched.op("vector", lambda v, tt_=tt_, reg=reg: v.tensor_copy(
                        v2_sl[:, tt_, 65:129], reg[:, 64:128]),
                        inc=sDV)
                    tev[tt_] = v2t
                st["tr_ev"][0] = tev[6]
                st["tr_ev"][1] = tev[7]
                st["zk_tev"] = tev[7]
                st["vtf_free"] = sPE.v
                st["v2_store"] = sched.op("sync", lambda e, hg=hg, s=s: e.dma_start(
                    out=v2sp[hg, s * 1024:(s + 1) * 1024].rearrange("(tt p) d -> p tt d", p=128),
                    in_=v2_sl[:]),
                    waits=[(sDV, v2t)], inc=sV2, inc_n=16)
                for hh in range(2):
                    for tt_ in range(8):
                        first = (tt_ == 0 and hh == 0)
                        last = (tt_ == 7 and hh == 1)
                        w = []
                        if first:
                            w = [(sGP, fknt), (sDV, v2t), (sDV, lnst["psD_free"])]
                        sched.op("tensor", lambda t, tt_=tt_, hh=hh: t.matmul(
                            psD[hh * 64:(hh + 1) * 64, 0:65],
                            fkn_sl[:, tt_, hh * 64:(hh + 1) * 64],
                            v2_sl[:, tt_, hh * 65:(hh + 1) * 65],
                            start=(tt_ == 0), stop=(tt_ == 7)),
                            waits=w, inc=sPE if last else None)
                st["state_pe"] = sPE.v
                if s == 0:
                    sev = sched.op("vector", lambda v, hg=hg: v.tensor_copy(
                        accS[:, hg], psD[:, 0:65]),
                        waits=[(sPE, st["state_pe"])], inc=sDV)
                else:
                    sev = sched.op("vector", lambda v, hg=hg: v.scalar_tensor_tensor(
                        accS[:, hg], psD[:, 0:65], 0.0, accS[:, hg], OP.add, OP.add),
                        waits=[(sPE, st["state_pe"])], inc=sDV)
                lnst["psD_free"] = sev
            st["ln_free"] = sPE.v
            lnst["psD2_free"] = max(lnst["psD2_free"], sDV.v)
        cst = sched.op("sync", lambda e: e.dma_start(
            out=cstI.rearrange("p (g c) -> p g c", g=16), in_=accS[:]),
            waits=[(sDV, lnst["psD_free"])], inc=sCS, inc_n=16)
        sched.op("gpsimd", lambda g: g.collective_compute(
            "AllGather", OP.bypass,
            ins=[cstI[:]], outs=[cstO[:]],
            replica_groups=[[0, 1, 2, 3, 4, 5, 6, 7]]),
            waits=[(sCS, cst)], inc=sCC)
        arena.pop()
        barrier("p3a")

        arena.push()
        pfx_all = arena.take(F32, 8, 16, 65)
        pfxb = [arena.take(F32, 16, 65) for _ in range(2)]
        ctx = []
        for s_i in range(2):
            ctx.append({
                "fqt": arena.take(BF16, T), "fkt": arena.take(BF16, T),
                "fkn": arena.take(BF16, 16, 128), "v2": arena.take(BF16, 16, 130),
                "Sf": [arena.take(F32, 65) for _ in range(2)],
                "S_bf": arena.take(BF16, 65), "a_sb": arena.take(BF16, 2, 128),
                "rec_sb": arena.take(F32, 2), "den_sb": arena.take(F32, 2),
                "yhgT": arena.take(BF16, 16, 128),
                "regA": [psA[:, s_i * 512:s_i * 512 + 128],
                         psA[:, s_i * 512 + 128:s_i * 512 + 256]],
                "regY": [psB[:, s_i * 512:s_i * 512 + 65],
                         psC[:, s_i * 512:s_i * 512 + 65]],
                "regU": (psD if s_i == 0 else psD2),
                "ld": 0, "yst": 0, "sf_wr": 0,
            })

        sched.op("sync", None, waits=[(sFK, sFK.v), (sFKN, sFKN.v), (sV2, sV2.v),
                                      (sFQ, sFQ.v), (sCS, sCS.v)])
        pfl = sched.op("sync", lambda e: e.dma_start(
            out=pfx_all[:], in_=cstO.rearrange("s p c -> p s c").rearrange(
                "p s (g c) -> p s g c", g=16)),
            waits=[(sCC, 1)], inc=sP3L, inc_n=16)
        pft = sched.op("vector", lambda v: v.tensor_scalar(
            pfxb[0][:], pfx_all[:, 0], pfx_sb[:, 0:1], None, OP.mult),
            waits=[(sP3L, pfl), (sIN, sIN.v)], inc=sDV)
        for s_ in range(1, 8):
            pft = sched.op("vector", lambda v, s_=s_: v.scalar_tensor_tensor(
                pfxb[s_ % 2][:], pfx_all[:, s_], pfx_sb[:, s_:s_ + 1],
                pfxb[(s_ + 1) % 2][:], OP.mult, OP.add), inc=sDV)
        pfx_final = pfxb[1]

        def emit_loads(cx, hg):
            sched.op("sync", lambda e, hg=hg, cx=cx: e.dma_start(
                out=cx["fqt"][:], in_=fqsp[hg]), waits=[(sDV, cx["ld"])],
                inc=sP3L, inc_n=16)
            sched.op("sync", lambda e, hg=hg, cx=cx: e.dma_start(
                out=cx["fkt"][:], in_=fksp[hg]), inc=sP3L, inc_n=16)
            for qtr in range(2):
                sched.op("sync", lambda e, hg=hg, qtr=qtr, cx=cx: e.dma_start(
                    out=cx["fkn"][:, qtr * 8:(qtr + 1) * 8],
                    in_=fknsp[hg, qtr * 1024:(qtr + 1) * 1024].rearrange("(tt p) f -> p tt f", p=128)),
                    inc=sP3L, inc_n=16)
                sched.op("sync", lambda e, hg=hg, qtr=qtr, cx=cx: e.dma_start(
                    out=cx["v2"][:, qtr * 8:(qtr + 1) * 8],
                    in_=v2sp[hg, qtr * 1024:(qtr + 1) * 1024].rearrange("(tt p) d -> p tt d", p=128)),
                    inc=sP3L, inc_n=16)
            cx["ldt"] = sP3L.v

        def emit_init(cx, hg):
            cSf = sched.op("vector", lambda v, hg=hg, cx=cx: v.tensor_copy(
                cx["Sf"][0][:], pfx_final[:, hg]),
                waits=[(sDV, pft), (sDV, cx["sf_wr"])], inc=sDV)
            cx["sbf"] = sched.op("vector", lambda v, cx=cx: v.tensor_copy(
                cx["S_bf"][:], cx["Sf"][0][:]),
                waits=[(sDV, cSf), (sP3L, cx["ldt"])], inc=sDV)
            cx["aev"] = [0, 0]
            cx["yfh"] = [0, 0]
            cx["ygrp"] = 0
            cx["add2"] = 0

        def emit_chunk_pair(cxs, hgs, ch):
            for cx in cxs:
                if ch > 0:
                    cx["sbf"] = sched.op("vector", lambda v, c=ch % 2, cx=cx: v.tensor_copy(
                        cx["S_bf"][:], cx["Sf"][c][:]),
                        waits=[(sDV, cx["add2"]), (sPE, cx["ygrp"])], inc=sDV)
            for hh in range(2):
                for cx in cxs:
                    amw = [(sDV, max(cx["aev"]))]
                    if ch == 0 and hh == 0:
                        amw.append((sDV, cx["sbf"]))
                    cx["amm"] = sched.op("tensor", lambda t, hh=hh, ch=ch, cx=cx: t.matmul(
                        cx["regA"][hh], cx["fkt"][hh * 64:(hh + 1) * 64, ch * 128:(ch + 1) * 128],
                        cx["fqt"][hh * 64:(hh + 1) * 64, ch * 128:(ch + 1) * 128],
                        start=True, stop=True), waits=amw, inc=sPE)
                for s_i, cx in enumerate(cxs):
                    cx["aev"][hh] = sched.op("vector", lambda v, hh=hh, hg=hgs[s_i], ch=ch, cx=cx: v.scalar_tensor_tensor(
                        cx["a_sb"][:, hh], cx["regA"][hh], rsk_all[:, hg, ch, hh:hh + 1],
                        maskT[:], OP.mult, OP.mult),
                        waits=[(sPE, cx["amm"])], inc=sDV)
            for hh in range(2):
                for cx in cxs:
                    sched.op("tensor", lambda t, hh=hh, ch=ch, cx=cx: t.matmul(
                        cx["regY"][hh], cx["a_sb"][:, hh],
                        cx["v2"][:, ch, hh * 65:(hh + 1) * 65],
                        start=True, stop=False),
                        waits=[(sDV, cx["aev"][hh]), (sDV, cx["yfh"][hh])])
                    sched.op("tensor", lambda t, hh=hh, ch=ch, cx=cx: t.matmul(
                        cx["regY"][hh], cx["fqt"][hh * 64:(hh + 1) * 64, ch * 128:(ch + 1) * 128],
                        cx["S_bf"][hh * 64:(hh + 1) * 64, 0:65],
                        start=False, stop=False), waits=[(sDV, cx["sbf"])])
                    cx["ygrp"] = sched.op("tensor", lambda t, hh=hh, ch=ch, cx=cx: t.matmul(
                        cx["regY"][hh][:, 64:65],
                        cx["fqt"][hh * 64:(hh + 1) * 64, ch * 128:(ch + 1) * 128],
                        epscol[hh * 64:(hh + 1) * 64, 0:1],
                        start=False, stop=True), inc=sPE)
            for hh in range(2):
                for cx in cxs:
                    ltk = sched.op("scalar", lambda a, hh=hh, cx=cx: a.activation(
                        cx["den_sb"][:, hh:hh + 1], cx["regY"][hh][:, 64:65], AF.Ln),
                        waits=[(sPE, cx["ygrp"]), (sDV, cx["yfh"][hh])], inc=sAC)
                    rtk = sched.op("scalar", lambda a, hh=hh, cx=cx: a.activation(
                        cx["rec_sb"][:, hh:hh + 1], cx["den_sb"][:, hh:hh + 1],
                        AF.Exp, scale=-1.0), inc=sAC)
                    wyf = [(sAC, rtk)]
                    if ch == 0 and hh == 0:
                        wyf.append((sYS, cx["yst"]))
                    cx["yfh"][hh] = sched.op("vector", lambda v, hh=hh, ch=ch, cx=cx: v.tensor_scalar(
                        cx["yhgT"][:, ch, hh * 64:(hh + 1) * 64], cx["regY"][hh][:, 0:64],
                        cx["rec_sb"][:, hh:hh + 1], None, OP.mult),
                        waits=wyf, inc=sDV)
            for cx in cxs:
                supd = 0
                for hh in range(2):
                    w = [(sDV, cx["sbf"])] if hh == 0 else ()
                    supd = sched.op("tensor", lambda t, hh=hh, ch=ch, cx=cx: t.matmul(
                        cx["regU"][hh * 64:(hh + 1) * 64, 0:65],
                        cx["fkn"][:, ch, hh * 64:(hh + 1) * 64],
                        cx["v2"][:, ch, hh * 65:(hh + 1) * 65],
                        start=True, stop=True), waits=w,
                        inc=sPE if hh == 1 else None)
                cx["supd"] = supd
            for cx in cxs:
                c, n = ch % 2, (ch + 1) % 2
                cx["add2"] = sched.op("vector", lambda v, c=c, n=n, cx=cx: v.scalar_tensor_tensor(
                    cx["Sf"][n][:, 0:65], cx["regU"][:, 0:65], 0.0, cx["Sf"][c][:, 0:65],
                    OP.add, OP.add), waits=[(sPE, cx["supd"])], inc=sDV)

        for pair in range(8):
            hgs = (2 * pair, 2 * pair + 1)
            for s_i in range(2):
                emit_loads(ctx[s_i], hgs[s_i])
            for s_i in range(2):
                emit_init(ctx[s_i], hgs[s_i])
            for ch in range(16):
                emit_chunk_pair(ctx, hgs, ch)
            for s_i in range(2):
                cx = ctx[s_i]
                cx["yst"] = sched.op("sync", lambda e, hg=hgs[s_i], cx=cx: e.dma_start(
                    out=Yall2[hg].rearrange("(c p) h -> p c h", p=128), in_=cx["yhgT"][:]),
                    waits=[(sDV, cx["yfh"][1])], inc=sYS, inc_n=16)
                cx["ld"] = sDV.v
                cx["sf_wr"] = cx["add2"]
        arena.pop()
        barrier("p3b")

        arena.push()
        yblk = [arena.take(BF16, NCH, 1024) for _ in range(2)]
        yld = [arena.take(BF16, 8, 128) for _ in range(2)]
        wol = [arena.take(BF16, NCH, 128) for _ in range(2)]
        rt3 = arena.take(F32, 1024)
        outf3 = arena.take(F32, 1024)
        st3 = {"w_tok": [0, 0], "w_free": [0, 0], "ev": 0, "store": 0, "rtf": 0,
               "yld_free": [(sPE, 0), (sPE, 0)], "tev": [0, 0],
               "yblk_free": [0, 0], "tr_done": [0, 0]}
        sched.op("sync", None, waits=[(sYS, sYS.v), (sRS, sRS.v)])

        def emit_ytr_c(blk, c):
            bpar = blk % 2
            if True:
                par = c % 2
                ld = sched.op("sync", lambda e, c=c, blk=blk, par=par: e.dma_start(
                    out=yld[par][:],
                    in_=Yall2[c, blk * 1024:(blk + 1) * 1024].rearrange(
                        "(tt p) h -> p tt h", p=128)),
                    waits=[st3["yld_free"][par]], inc=sYB, inc_n=16)
                for tt in range(8):
                    reg = (psD if tt % 2 == 0 else psD2)[:, 128:256]
                    w = [(sDV, st3["tev"][tt % 2])]
                    if tt == 0:
                        w.append((sYB, ld))
                        w.append((sPE, st3["yblk_free"][bpar]))
                    trp = sched.op("tensor", lambda t, tt=tt, par=par, reg=reg: t.matmul(
                        reg, yld[par][:, tt], ident[:], start=True, stop=True),
                        waits=w, inc=sPE)
                    st3["tev"][tt % 2] = sched.op("vector",
                        lambda v, tt=tt, c=c, reg=reg, bpar=bpar: v.tensor_copy(
                            yblk[bpar][:, c, tt * 128:(tt + 1) * 128], reg),
                        waits=[(sPE, trp)], inc=sDV)
                st3["yld_free"][par] = (sPE, sPE.v)
            st3["tr_done"][bpar] = st3["tev"][(8 - 1) % 2]

        for c in range(NCH):
            emit_ytr_c(0, c)
        for blk in range(T // 1024):
            bpar = blk % 2
            bsl = slice(blk * 1024, (blk + 1) * 1024)
            for ot in range(NCH):
                if blk + 1 < T // 1024:
                    emit_ytr_c(blk + 1, ot)
                par = ot % 2
                st3["w_tok"][par] = sched.op("sync", lambda e, ot=ot, par=par: e.dma_start(
                    out=wol[par][:], in_=WoTb[:, ot].rearrange("ck p q -> p ck q")),
                    waits=[(sPE, st3["w_free"][par])], inc=sW3p[par], inc_n=16)
                rtt = sched.op("sync", lambda e, ot=ot, bsl=bsl: e.dma_start(
                    out=rt3[:], in_=res2_v[:, ot, bsl]),
                    waits=[(sDV, st3["rtf"])], inc=sXR, inc_n=16)
                pp = psA if ot % 2 == 0 else psB
                for c in range(NCH):
                    w = []
                    if c == 0:
                        w = [(sW3p[par], st3["w_tok"][par]),
                             (sDV, st3["tr_done"][bpar]), (sDV, st3["ev"])]
                    for half in range(2):
                        sched.op("tensor", lambda t, c=c, par=par, half=half, pp=pp, bpar=bpar: t.matmul(
                            pp[:, half * 512:(half + 1) * 512], wol[par][:, c],
                            yblk[bpar][:, c, half * 512:(half + 1) * 512],
                            start=(c == 0), stop=(c == NCH - 1)),
                            waits=w if half == 0 else (),
                            inc=sPE if (c == NCH - 1 and half == 1) else None)
                omm = sPE.v
                st3["w_free"][par] = omm
                ev = sched.op("vector", lambda v, pp=pp: v.tensor_add(outf3[:], pp[:, :], rt3[:]),
                              waits=[(sPE, omm), (sXR, rtt), (sRS, st3["store"])], inc=sDV)
                st3["ev"] = ev
                st3["rtf"] = ev
                st3["store"] = sched.op("sync", lambda e, ot=ot, bsl=bsl: e.dma_start(
                    out=res3_v[:, ot, bsl], in_=outf3[:]),
                    waits=[(sDV, ev)], inc=sRS, inc_n=16)
            st3["yblk_free"][bpar] = sPE.v
        arena.pop()
        arena.pop()

    def copy_out(src_v, ncols):
        arena.push()
        cp = arena.take(F32, 1024)
        sched.op("sync", None, waits=[(sRS, sRS.v), (sYS, sYS.v)])
        prev = 0
        for c in range(NCH):
            for blk in range(ncols // 1024):
                bsl = slice(blk * 1024, (blk + 1) * 1024)
                tok = sched.op("sync", lambda e, c=c, bsl=bsl: e.dma_start(
                    out=cp[:], in_=src_v[:, c, bsl]),
                    waits=[(sRS, prev)], inc=sXFp[0], inc_n=16)
                prev = sched.op("sync", lambda e, c=c, bsl=bsl: e.dma_start(
                    out=outT_v[:, c, bsl], in_=cp[:]),
                    waits=[(sXFp[0], tok)], inc=sRS, inc_n=16)
        sched.op("sync", None, waits=[(sRS, sRS.v)])
        arena.pop()

    init_consts()
    barrier("init")
    if stop_after != "res1":
        p0_shard()
    wb_emit = make_p0_wb(wb_wbl, wb_wstg)
    p1_pass1(wb_emit=wb_emit)
    sched.op("gpsimd", lambda g: g.collective_compute(
        "AllGather", OP.bypass, ins=[ckvI[:]], outs=[ckvO[:]],
        replica_groups=[[0, 1], [2, 3], [4, 5], [6, 7]]),
        waits=[(sKV, 32)], inc=sCC2)
    sched.op("gpsimd", lambda g: g.collective_compute(
        "AllGather", OP.bypass, ins=[cwI[:]], outs=[cwO[:]],
        replica_groups=[[0, 1, 2, 3, 4, 5, 6, 7]]),
        waits=[(sWSp[0], sWSp[0].v), (sWSp[1], sWSp[1].v)], inc=sCCW)
    barrier("p1a")
    p1_pass2()
    barrier("p1b")

    arena.pop()

    if stop_after == "res1":
        copy_out(res1_v, T)
        barrier("end")
    elif stop_after == "res2":
        ffn_phase(res1_v, res2_v, T)
        barrier("p2")
        copy_out(res2_v, T)
        barrier("end")
    elif stop_after == "ffndbg":
        ffn_phase(res1_v, res2_v, T, dump=True)
        barrier("end")
    elif stop_after == "y2dump":
        ffn_phase(res1_v, res2_v, T)
        barrier("p2")
        p3_attnb()
        barrier("p3")
        sched.op("sync", None, waits=[(sYS, sYS.v), (sRS, sRS.v)])
        ob = outT.bitcast(BF16)
        for g in range(16):
            sched.op("sync", lambda e, g=g: e.dma_start(
                out=ob[:, g * 128:(g + 1) * 128], in_=Yall2[g]),
                inc=sRS, inc_n=16)
        sched.op("sync", None, waits=[(sRS, sRS.v)])
        barrier("end")
    elif stop_after == "res3":
        ffn_phase(res1_v, res2_v, T)
        barrier("p2")
        p3_attnb()
        barrier("p3")
        copy_out(res3_v, T)
        barrier("end")
    elif stop_after in ("full", "full_dbg"):
        ffn_phase(res1_v, res2_v, T)
        barrier("p2")
        p3_attnb()
        barrier("p3")
        if stop_after == "full_dbg":
            dbg_v = dbgT.rearrange("(c p) t -> p c t", p=128)
            arena.push()
            cpd = arena.take(F32, 1024)
            prevd = 0
            sched.op("sync", None, waits=[(sRS, sRS.v)])
            for c in range(NCH):
                for blk in range(T // 1024):
                    bsl = slice(blk * 1024, (blk + 1) * 1024)
                    tokd = sched.op("sync", lambda e, c=c, bsl=bsl: e.dma_start(
                        out=cpd[:], in_=res3_v[:, c, bsl]),
                        waits=[(sRS, prevd)], inc=sXFp[0], inc_n=16)
                    prevd = sched.op("sync", lambda e, c=c, bsl=bsl: e.dma_start(
                        out=dbg_v[:, c, bsl], in_=cpd[:]),
                        waits=[(sXFp[0], tokd)], inc=sRS, inc_n=16)
            arena.pop()
            barrier("dbgc")
        ffn_phase(res3_v, outT_v, T)
        barrier("p4")
        sched.op("sync", None, waits=[(sRS, sRS.v)])
        barrier("end")
    else:
        raise NotImplementedError(stop_after)

    with stack, nc.Block() as block:
        @block.sync
        def _(e):
            sched.run("sync", e)

        @block.gpsimd
        def _(e):
            sched.run("gpsimd", e)

        @block.vector
        def _(e):
            sched.run("vector", e)

        @block.scalar
        def _(e):
            sched.run("scalar", e)

        @block.tensor
        def _(e):
            sched.run("tensor", e)

    return nc


def _shard_inputs(inputs):
    x = np.asarray(inputs["L0_attn_a_x"], np.float32)
    common = {
        "Wq_a": np.ascontiguousarray(np.asarray(inputs["L0_attn_a_Wq"], np.float32)),
        "bq_a": np.ascontiguousarray(np.asarray(inputs["L0_attn_a_bq"], np.float32).reshape(NCH, 128).T),
        "Wk_a": np.ascontiguousarray(np.asarray(inputs["L0_attn_a_Wk"], np.float32)),
        "bk_a": np.asarray(inputs["L0_attn_a_bk"], np.float32).reshape(DA, 1),
        "Wv_a": np.ascontiguousarray(np.asarray(inputs["L0_attn_a_Wv"], np.float32)),
        "bv_a": np.asarray(inputs["L0_attn_a_bv"], np.float32).reshape(1, DA),
        "Wo_a": np.ascontiguousarray(np.asarray(inputs["L0_attn_a_Wo"], np.float32)),
        "bo_a": np.ascontiguousarray(np.asarray(inputs["L0_attn_a_bo"], np.float32).reshape(NCH, 128).T),
        "Wq_b": np.ascontiguousarray(np.asarray(inputs["L0_attn_b_Wq"], np.float32)),
        "Wk_b": np.ascontiguousarray(np.asarray(inputs["L0_attn_b_Wk"], np.float32)),
        "Wv_b": np.ascontiguousarray(np.asarray(inputs["L0_attn_b_Wv"], np.float32)),
        "Wo_b": np.ascontiguousarray(np.asarray(inputs["L0_attn_b_Wo"], np.float32)),
        "Wfq": np.ascontiguousarray(np.asarray(inputs["L0_attn_b_Wfq"], np.float32)),
        "Wfk": np.ascontiguousarray(np.asarray(inputs["L0_attn_b_Wfk"], np.float32)),
    }
    gup = np.asarray(inputs["L0_ffn_gate_up_packed"], np.int32)
    gus = np.asarray(inputs["L0_ffn_gate_up_scales"], np.float16)
    dp = np.asarray(inputs["L0_ffn_down_packed"], np.int32)
    dsc = np.asarray(inputs["L0_ffn_down_scales"], np.float16)
    per_core = []
    for r in range(8):
        b, h = r // 2, r % 2
        own = x[b, h * T:(h + 1) * T]
        par = x[b, (1 - h) * T:(2 - h) * T]
        m = dict(common)
        m["xT"] = np.ascontiguousarray(np.concatenate([own, par], 0).T)
        onehot = np.zeros((128, 8), np.float32)
        if h == 1:
            onehot[:, r - 1] = 1.0
        m["pfx8"] = onehot
        m["gup_sh"] = np.ascontiguousarray(gup[r * 1408:(r + 1) * 1408])
        m["gus_sh"] = np.ascontiguousarray(gus[:, r * 1408:(r + 1) * 1408])
        m["dp_sh"] = np.ascontiguousarray(dp[r * 256:(r + 1) * 256])
        m["dsc_sh"] = np.ascontiguousarray(dsc[:, r * 256:(r + 1) * 256])
        per_core.append(m)
    return per_core


def run_cores(inputs, stop_after="full"):
    nc = build(stop_after=stop_after)
    in_maps = _shard_inputs(inputs)
    return run_bass_kernel_spmd(nc, in_maps, core_ids=list(range(8)))


def kernel(**inputs):
    res = run_cores(inputs, stop_after="full")
    out = np.empty((4, S, D), np.float32)
    for r in range(8):
        b, h = r // 2, r % 2
        out[b, h * T:(h + 1) * T] = np.asarray(res.results[r]["outT"]).T
    return out


# version 36 (final)
# speedup vs baseline: 1.0246x; 1.0246x over previous
import contextlib

import numpy as np
import ml_dtypes

import concourse.bass as bass
import concourse.mybir as mybir
from concourse.bass_utils import run_bass_kernel_spmd

F32 = mybir.dt.float32
BF16 = mybir.dt.bfloat16
F16 = mybir.dt.float16
I32 = mybir.dt.int32
AF = mybir.ActivationFunctionType
OP = mybir.AluOpType

D = 2048
S = 4096
T = 2048
NCH = 16
HA, DA = 16, 128
HB, DB, FB = 32, 64, 64
I_FFN = 5632
GU_OUT = 2 * I_FFN
GOT = GU_OUT // 128
DOT = D // 128
ICH = I_FFN // 128
RSQRT_DA = float(1.0 / np.sqrt(128.0))
LN_EPS = 1e-5
RMS_EPS = 1e-6
ATTB_EPS = 1e-12

ENG = ("sync", "gpsimd", "vector", "scalar", "tensor")


class Sem:
    def __init__(self, handle, name):
        self.h = handle
        self.name = name
        self.v = 0


class Arena:

    def __init__(self, ap, nbytes):
        self.ap = ap
        self.size = nbytes
        self.off = 0
        self.marks = []

    def push(self):
        self.marks.append(self.off)

    def pop(self):
        self.off = self.marks.pop()

    def _carve(self, prow, dtype, shape):
        n = int(np.prod(shape)) * mybir.dt.size(dtype)
        self.off = (self.off + 63) & ~63
        assert self.off + n <= self.size, f"arena overflow: {self.off}+{n} > {self.size}"
        v = self.ap[0:prow, self.off:self.off + n].bitcast(dtype)
        self.off += n
        if len(shape) > 1:
            names = " ".join(f"d{i}" for i in range(len(shape)))
            v = v.rearrange(f"p ({names}) -> p {names}",
                            **{f"d{i}": s for i, s in enumerate(shape)})
        return v

    def take(self, dtype, *shape):
        return self._carve(128, dtype, shape)

    def row(self, dtype, n):
        return self._carve(1, dtype, (n,))


class Sched:
    def __init__(self):
        self.steps = []
        self.race_edges = False
        self.engine_sems = {}

    def op(self, eng, fn, waits=(), inc=None, inc_n=1):
        incs = []
        ws = list(waits)
        auto = None
        if self.race_edges and fn is not None and eng in self.engine_sems:
            es = self.engine_sems[eng]
            if es.v > 0:
                ws.append((es, es.v))
            if inc is not es:
                auto = es
        if inc is not None:
            inc.v += inc_n
            incs.append((inc, inc_n))
        tok = inc.v if inc is not None else None
        if auto is not None:
            auto.v += 1
            incs.append((auto, 1))
        ws2 = tuple((s, v) for (s, v) in ws if v > 0)
        self.steps.append((eng, fn, ws2, tuple(incs)))
        return tok

    def run(self, eng_name, eng):
        for e, fn, waits, incs in self.steps:
            if e != eng_name:
                continue
            for sem, val in waits:
                eng.wait_ge(sem.h, val)
            instr = fn(eng) if fn is not None else None
            if incs:
                instr.then_inc(incs[0][0].h, incs[0][1])
                for sem, n in incs[1:]:
                    eng.nop().then_inc(sem.h, n)


def build(stop_after="full", race_edges=False):
    nc = bass.Bass("TRN2", target_bir_lowering=False, debug=False)

    def din(name, shape, dt):
        return nc.dram_tensor(name, list(shape), dt, kind="ExternalInput").ap()

    xT = din("xT", (D, S), F32)
    Wq_a = din("Wq_a", (D, D), F32)
    bq_a = din("bq_a", (128, NCH), F32)
    Wk_a = din("Wk_a", (D, DA), F32)
    bk_a = din("bk_a", (DA, 1), F32)
    Wv_a = din("Wv_a", (D, DA), F32)
    bv_a = din("bv_a", (1, DA), F32)
    Wo_a = din("Wo_a", (D, D), F32)
    bo_a = din("bo_a", (128, NCH), F32)
    Wq_b = din("Wq_b", (D, D), F32)
    Wk_b = din("Wk_b", (D, D), F32)
    Wv_b = din("Wv_b", (D, D), F32)
    Wo_b = din("Wo_b", (D, D), F32)
    Wfq = din("Wfq", (HB, DB, FB), F32)
    Wfk = din("Wfk", (HB, DB, FB), F32)
    gup_sh = din("gup_sh", (11 * 128, D // 8), I32)
    gus_sh = din("gus_sh", (D // 128, 11 * 128), F16)
    dp_sh = din("dp_sh", (2 * 128, I_FFN // 8), I32)
    dsc_sh = din("dsc_sh", (ICH, 2 * 128), F16)
    pfx8 = din("pfx8", (128, 8), F32)

    outT = nc.dram_tensor("outT", [D, T], F32, kind="ExternalOutput").ap()
    dbgT = None
    if stop_after == "full_dbg":
        dbgT = nc.dram_tensor("dbgT", [D, T], F32, kind="ExternalOutput").ap()

    def dint(name, shape, dt):
        return nc.dram_tensor(name, list(shape), dt, kind="Internal").ap()

    res1 = dint("res1", (D, S), F32)
    res2 = dint("res2", (D, S), F32)
    res3 = dint("res3", (D, T), F32)
    lnsp = dint("lnsp", (D, S), BF16)
    WbT = dint("WbT", (3, 16, NCH, 128, 128), BF16)
    WoTb = dint("WoTb", (NCH, DOT, 128, 128), BF16)
    Wqa_bf = dint("Wqa_bf", (D, D), BF16)
    Woa_bf = dint("Woa_bf", (D, D), BF16)
    GU_SH = 11 * NCH * 128
    D_SH = 2 * ICH * 128
    cwI = dint("cwI", (128, GU_SH + D_SH), BF16)
    cwO = dint("cwO", (8, 128, GU_SH + D_SH), BF16)
    Yall2 = dint("Yall2", (16, T, 128), BF16)
    fqsp = dint("fqsp", (16, 128, T), BF16)
    fksp = dint("fksp", (16, 128, T), BF16)
    fknsp = dint("fknsp", (16, T, 128), BF16)
    v2sp = dint("v2sp", (16, T, 130), BF16)
    cstI = dint("cstI", (128, 16 * 65), F32)
    cstO = dint("cstO", (8, 128, 16 * 65), F32)
    ckvI = dint("ckvI", (128, 4096), BF16)
    ckvO = dint("ckvO", (2, 128, 4096), BF16)

    sched = Sched()
    stack = contextlib.ExitStack()
    ec = stack.enter_context
    wb_cp_tok = [(0, 0)]

    def sb(name, shape, dt, **kw):
        return ec(nc.sbuf_tensor(name, list(shape), dt, **kw))

    def psum(name, shape, dt):
        return ec(nc.psum_tensor(name, list(shape), dt))

    def sem(name):
        return Sem(ec(nc.semaphore(name)), name)

    ones1 = sb("ones1", (1, 128), F32)
    onescol = sb("onescol", (128, 1), BF16)
    epscol = sb("epscol", (128, 1), BF16)
    maskT = sb("maskT", (128, 128), BF16)
    ident = sb("ident", (128, 128), BF16)
    bvb = sb("bvb", (128, DA), BF16)
    bq_sb = sb("bq_sb", (128, NCH), F32)
    bo_sb = sb("bo_sb", (128, NCH), F32)
    bk_sb = sb("bk_sb", (128, 1), F32)
    wk_sb = sb("wk_sb", (128, NCH, DA), BF16)
    wv_sb = sb("wv_sb", (128, NCH, DA), BF16)
    pfx_sb = sb("pfx_sb", (128, 8), F32)
    rows = sb("rows", (1, 3584), F32)
    rows2 = sb("rows2", (1, 1024), F32)
    bv_row = sb("bv_row", (1, DA), F32)
    ident_f32 = sb("ident_f32", (128, 128), F32)
    trd = sb("trd", (128, GU_OUT), BF16, is_transpose_dest=True)
    AR_BYTES = 161024
    arena = Arena(sb("arena", (128, AR_BYTES), mybir.dt.uint8), AR_BYTES)

    psA = psum("psA", (128, 1024), F32)
    psB = psum("psB", (128, 1024), F32)
    psC = psum("psC", (128, 1024), F32)
    psD = psum("psD", (128, 512), F32)
    psD2 = psum("psD2", (128, 512), F32)

    sLD = sem("sLD")
    sLD2 = sem("sLD2")
    sPE = sem("sPE")
    sDV = sem("sDV")
    sAC = sem("sAC")
    sGP = sem("sGP")
    if race_edges:
        sched.race_edges = True
        sched.engine_sems = {"vector": sDV, "scalar": sAC, "tensor": sPE}
    sST = sem("sST")
    sIN = sem("sIN")
    sWKV = sem("sWKV")
    sXFp = [sem("sXF0"), sem("sXF1")]
    sSP = sem("sSP")
    sLNB = sem("sLNB")
    sWQ2p = [sem("sWQ2a"), sem("sWQ2b")]
    sWO2p = [sem("sWO2a"), sem("sWO2b")]
    sXR = sem("sXR")
    sRS = sem("sRS")
    sPKp = [sem("sPKa"), sem("sPKb")]
    sSCp = [sem("sSCa"), sem("sSCb")]
    sWSp = [sem("sWSa"), sem("sWSb")]
    sTR = sem("sTR")
    sTS = sem("sTS")
    sWBp = [sem("sWBa"), sem("sWBb")]
    sWBSp = [sem("sWBSa"), sem("sWBSb")]
    sWGp = [sem("sWGa"), sem("sWGb")]
    sWDp = [sem("sWDa"), sem("sWDb")]
    sBD = sem("sBD")
    sW3p = [sem("sW3a"), sem("sW3b")]
    sFK = sem("sFK")
    sFKN = sem("sFKN")
    sV2 = sem("sV2")
    sFQ = sem("sFQ")
    sCS = sem("sCS")
    sCC = sem("sCC")
    sCCW = sem("sCCW")
    sKV = sem("sKV")
    sCC2 = sem("sCC2")
    sKVL = sem("sKVL")
    sP3L = sem("sP3L")
    sYS = sem("sYS")
    sYB = sem("sYB")

    xT_v = xT.rearrange("(c p) t -> p c t", p=128)
    lnsp_v = lnsp.rearrange("(c p) t -> p c t", p=128)
    res1_v = res1.rearrange("(c p) t -> p c t", p=128)
    res2_v = res2.rearrange("(c p) t -> p c t", p=128)
    res3_v = res3.rearrange("(c p) t -> p c t", p=128)
    outT_v = outT.rearrange("(c p) t -> p c t", p=128)
    Wqa_bf_v = Wqa_bf.rearrange("(c p) o -> p c o", p=128)
    Woa_bf_v = Woa_bf.rearrange("(c p) o -> p c o", p=128)

    barrier_sems = {}

    def barrier(tag):
        bar = barrier_sems.setdefault(tag, sem("bar_" + tag))
        for e in ENG:
            sched.op(e, lambda x: x.nop(), inc=bar, inc_n=1)
        target = bar.v
        for e in ENG:
            sched.op(e, None, waits=[(bar, target)])

    def init_consts():
        sched.op("gpsimd", lambda g: g.memset(ones1[:], 1.0), inc=sGP)
        sched.op("gpsimd", lambda g: g.memset(onescol[:], 1.0), inc=sGP)
        sched.op("gpsimd", lambda g: g.memset(epscol[:], ATTB_EPS), inc=sGP)
        sched.op("gpsimd", lambda g: g.memset(maskT[:], 1.0), inc=sGP)
        t0 = sched.op("gpsimd", lambda g: g.memset(ident[:], 0.0), inc=sGP)
        sched.op("gpsimd", lambda g: g.affine_select(
            out=maskT[:], in_=maskT[:], compare_op=OP.is_ge, fill=0.0,
            base=0, pattern=[[1, 128]], channel_multiplier=-1),
            waits=[(sGP, t0)], inc=sGP)
        sched.op("gpsimd", lambda g: g.affine_select(
            out=ident[:], in_=ident[:], compare_op=OP.not_equal, fill=1.0,
            base=0, pattern=[[-1, 128]], channel_multiplier=1),
            waits=[(sGP, t0)], inc=sGP)
        t0b = sched.op("gpsimd", lambda g: g.memset(ident_f32[:], 0.0), inc=sGP)
        sched.op("gpsimd", lambda g: g.affine_select(
            out=ident_f32[:], in_=ident_f32[:], compare_op=OP.not_equal, fill=1.0,
            base=0, pattern=[[-1, 128]], channel_multiplier=1),
            waits=[(sGP, t0b)], inc=sGP)
        sched.op("sync", lambda e: e.dma_start(out=bq_sb[:], in_=bq_a[:]), inc=sIN, inc_n=16)
        sched.op("sync", lambda e: e.dma_start(out=bo_sb[:], in_=bo_a[:]), inc=sIN, inc_n=16)
        sched.op("sync", lambda e: e.dma_start(out=bk_sb[:], in_=bk_a[:]), inc=sIN, inc_n=16)
        sched.op("sync", lambda e: e.dma_start(out=pfx_sb[:], in_=pfx8[:]), inc=sIN, inc_n=16)
        sched.op("gpsimd", lambda g: g.dma_start(
            out=wk_sb[:], in_=Wk_a.rearrange("(c p) d -> p c d", p=128)), inc=sWKV, inc_n=16)
        sched.op("gpsimd", lambda g: g.dma_start(
            out=wv_sb[:], in_=Wv_a.rearrange("(c p) d -> p c d", p=128)), inc=sWKV, inc_n=16)
        tok = sched.op("sync", lambda e: e.dma_start(out=bv_row[:], in_=bv_a[:]), inc=sIN, inc_n=16)
        mt = sched.op("tensor", lambda t: t.matmul(
            psD[:, 0:DA], ones1[:], bv_row[:], start=True, stop=True),
            waits=[(sIN, tok), (sGP, sGP.v)], inc=sPE)
        sched.op("vector", lambda v: v.tensor_copy(bvb[:], psD[:, 0:DA]),
                 waits=[(sPE, mt)], inc=sDV)

    arena.push()
    kT_sb = arena.take(BF16, S)
    v_sb = arena.take(BF16, S // 128, DA)
    wb_wbl = [arena.take(BF16, D) for _ in range(2)]
    wb_wstg = [arena.take(BF16, NCH, 128) for _ in range(2)]

    def p1_pass1(wb_emit=None):
        NSL = T // 512
        arena.push()
        xf = [arena.take(F32, 512) for i in range(2)]
        xb = arena.take(BF16, NCH, 512)
        xsq = arena.take(BF16, 512)
        mrow_b = rows[:, 2560:3072]
        isrow_b = rows[:, 3072:3584]
        mb = arena.take(F32, 512)
        ib = arena.take(F32, 512)
        lnf1 = arena.take(F32, 512)
        ln = arena.take(BF16, NCH, 512)

        st = {
            "xf_tok": [0, 0], "xf_free": [0, 0],
            "psD_free_dv": 0,
            "psD2_free_dv": 0,
            "xb_free_pe": 0,
            "xb_free_ac": 0,
            "ln_free_pe": 0,
            "ln_free_st": 0,
            "k_evac": 0,
            "v_evac": [0, 0],
            "rows_free_pe": 0,
        }

        for s in range(NSL):
            par = s % 2
            sl = slice(s * 512, (s + 1) * 512)
            cast_toks = []
            for c in range(NCH):
                cpar = c % 2
                st["xf_tok"][cpar] = sched.op(
                    "sync", lambda e, c=c, cpar=cpar, sl=sl: e.dma_start(
                        out=xf[cpar][:], in_=xT_v[:, c, sl]),
                    waits=[(sDV, st["xf_free"][cpar])], inc=sXFp[cpar], inc_n=16)
                w = [(sXFp[cpar], st["xf_tok"][cpar])]
                if c == 0:
                    w += [(sPE, st["xb_free_pe"]), (sAC, st["xb_free_ac"])]
                tok = sched.op("vector", lambda v, cpar=cpar, c=c: v.tensor_copy(xb[:, c], xf[cpar][:]),
                               waits=w, inc=sDV)
                cast_toks.append(tok)
                st["xf_free"][cpar] = sDV.v
            prev_sxx = 0
            sq_last = 0
            for c in range(NCH):
                w = [(sDV, cast_toks[c])]
                if c > 0:
                    w.append((sPE, prev_sxx))
                sq_last = sched.op("scalar", lambda a, c=c: a.activation(xsq[:], xb[:, c], AF.Square),
                                   waits=w, inc=sAC)
                w2 = [(sDV, cast_toks[c])]
                if c == 0:
                    w2.append((sDV, st["psD_free_dv"]))
                sched.op("tensor", lambda t, c=c: t.matmul(
                    psD[0:1, :], onescol[:], xb[:, c], start=(c == 0), stop=(c == NCH - 1)),
                    waits=w2)
                w3 = [(sAC, sq_last)]
                if c == 0:
                    w3.append((sDV, st["psD2_free_dv"]))
                prev_sxx = sched.op("tensor", lambda t, c=c: t.matmul(
                    psD2[0:1, :], onescol[:], xsq[:], start=(c == 0), stop=(c == NCH - 1)),
                    waits=w3, inc=sPE)
            st["xb_free_ac"] = sq_last
            stats_done = prev_sxx
            t1 = sched.op("vector", lambda v: v.tensor_scalar(
                rows[:, 0:512], psD[0:1, :], 1.0 / D, None, OP.mult),
                waits=[(sPE, stats_done), (sPE, st["rows_free_pe"])], inc=sDV)
            t2 = sched.op("vector", lambda v: v.tensor_scalar(
                rows[:, 1024:1536], psD2[0:1, :], 1.0 / D, float(LN_EPS),
                OP.mult, OP.add), inc=sDV)
            t3 = sched.op("scalar", lambda a: a.activation(
                rows[:, 512:1024], rows[:, 0:512], AF.Square), waits=[(sDV, t1)], inc=sAC)
            t4 = sched.op("vector", lambda v: v.tensor_sub(
                rows[:, 1536:2048], rows[:, 1024:1536], rows[:, 512:1024]),
                waits=[(sAC, t3)], inc=sDV)
            t5 = sched.op("scalar", lambda a: a.activation(
                rows[:, 2048:2560], rows[:, 1536:2048], AF.Ln),
                waits=[(sDV, t4)], inc=sAC)
            t6 = sched.op("scalar", lambda a: a.activation(
                rows2[:, 0:512], rows[:, 2048:2560], AF.Exp, scale=-0.5),
                waits=[(sAC, t5)], inc=sAC)
            t7 = sched.op("vector", lambda v: v.tensor_copy(mrow_b[:], rows[:, 0:512]), inc=sDV)
            t8 = sched.op("vector", lambda v: v.tensor_copy(isrow_b[:], rows2[:, 0:512]),
                          waits=[(sAC, t6)], inc=sDV)
            b1 = sched.op("tensor", lambda t: t.matmul(
                psD[:, :], ones1[:], mrow_b[:], start=True, stop=True),
                waits=[(sDV, t8)], inc=sPE)
            b2 = sched.op("tensor", lambda t: t.matmul(
                psD2[:, :], ones1[:], isrow_b[:], start=True, stop=True), inc=sPE)
            st["rows_free_pe"] = b2
            m1 = sched.op("vector", lambda v: v.tensor_copy(mb[:], psD[:, :]),
                          waits=[(sPE, b2)], inc=sDV)
            st["psD_free_dv"] = m1
            i1 = sched.op("vector", lambda v: v.tensor_copy(ib[:], psD2[:, :]), inc=sDV)
            st["psD2_free_dv"] = i1
            ln_toks = []
            for c in range(NCH):
                cpar = c % 2
                st["xf_tok"][cpar] = sched.op(
                    "sync", lambda e, c=c, cpar=cpar, sl=sl: e.dma_start(
                        out=xf[cpar][:], in_=xT_v[:, c, sl]),
                    waits=[(sDV, st["xf_free"][cpar])], inc=sXFp[cpar], inc_n=16)
                w = [(sDV, i1), (sXFp[cpar], st["xf_tok"][cpar])]
                if c == 0:
                    w += [(sPE, st["ln_free_pe"]), (sSP, st["ln_free_st"])]
                ct = sched.op("vector", lambda v, cpar=cpar: v.tensor_sub(
                    lnf1[:], xf[cpar][:], mb[:]), waits=w, inc=sDV)
                st["xf_free"][cpar] = ct
                tok = sched.op("vector", lambda v, c=c: v.tensor_mul(ln[:, c], lnf1[:], ib[:]),
                               inc=sDV)
                ln_toks.append(tok)
            if s < T // 512:
                st["ln_free_st"] = sched.op("sync", lambda e, sl=sl: e.dma_start(
                    out=lnsp_v[:, :, sl], in_=ln[:]),
                    waits=[(sDV, ln_toks[-1])], inc=sSP, inc_n=16)
            for c in range(NCH):
                w = [(sDV, ln_toks[c])]
                if c == 0:
                    w += [(sWKV, 32), (sAC, st["k_evac"])]
                sched.op("tensor", lambda t, c=c: t.matmul(
                    psA[:, 0:512], wk_sb[:, c], ln[:, c], start=(c == 0), stop=(c == NCH - 1)),
                    waits=w, inc=sPE if c == NCH - 1 else None)
            ktok = sPE.v
            st["k_evac"] = sched.op("scalar", lambda a, sl=sl: a.add(
                kT_sb[:, sl], psA[:, 0:512], bk_sb[:, 0:1]),
                waits=[(sPE, ktok)], inc=sAC)
            last_v_pe = 0
            for tt_ in range(4):
                vpar = tt_ % 2
                reg = psB[:, vpar * 512:vpar * 512 + 128]
                for c in range(NCH):
                    w = [(sDV, ln_toks[c])]
                    if c == 0:
                        w.append((sDV, st["v_evac"][vpar]))
                    sched.op("tensor", lambda t, tt_=tt_, c=c, reg=reg: t.matmul(
                        reg, ln[:, c, tt_ * 128:(tt_ + 1) * 128], wv_sb[:, c],
                        start=(c == 0), stop=(c == NCH - 1)),
                        waits=w, inc=sPE if c == NCH - 1 else None)
                vtok = sPE.v
                last_v_pe = vtok
                st["v_evac"][vpar] = sched.op("vector", lambda v, s=s, tt_=tt_, reg=reg: v.tensor_add(
                    v_sb[:, s * 4 + tt_], reg, bvb[:]),
                    waits=[(sPE, vtok)], inc=sDV)
            st["ln_free_pe"] = last_v_pe
            st["xb_free_pe"] = last_v_pe
            if wb_emit is not None:
                wb_emit(32)
        if wb_emit is not None:
            wb_emit(None)
        sched.op("sync", lambda e: e.dma_start(
            out=ckvI[:, 0:2048], in_=kT_sb[:, 0:2048]),
            waits=[(sAC, st["k_evac"])], inc=sKV, inc_n=16)
        sched.op("sync", lambda e: e.dma_start(
            out=ckvI[:, 2048:4096],
            in_=v_sb.rearrange("p a b -> p (a b)")[:, 0:2048]),
            waits=[(sDV, max(st["v_evac"]))], inc=sKV, inc_n=16)
        for m in range(2):
            sched.op("sync", lambda e, m=m: e.dma_start(
                out=kT_sb[:, m * 2048:(m + 1) * 2048], in_=ckvO[m, :, 0:2048]),
                waits=[(sCC2, 1)], inc=sKVL, inc_n=16)
            sched.op("sync", lambda e, m=m: e.dma_start(
                out=v_sb.rearrange("p a b -> p (a b)")[:, m * 2048:(m + 1) * 2048],
                in_=ckvO[m, :, 2048:4096]),
                inc=sKVL, inc_n=16)
        arena.pop()

    def p1_pass2():
        NBLK = T // 1024
        NKV = S // 128
        arena.push()
        sched.op("sync", None, waits=[(sSP, sSP.v)])
        lnb = arena.take(BF16, NCH, 1024)
        qh = arena.take(BF16, 1024)
        expb = [arena.take(BF16, 1024) for i in range(2)]
        acc = arena.take(F32, 1024)
        acc_bf = arena.take(BF16, 1024)
        out_t = arena.take(BF16, NCH, 1024)
        wq_t = [arena.take(BF16, NCH, 128) for i in range(2)]
        recb = arena.take(F32, 512)
        wob_t = [arena.take(BF16, NCH, 128) for i in range(2)]
        wo_res = arena.take(F32, 1024)
        wo_res2 = arena.take(F32, 1024)
        xres = arena.take(F32, 1024)
        den_sb = arena.row(F32, 1024)
        rec_row = arena.row(F32, 1024)

        cp0, cp1 = wb_cp_tok[0]
        sched.op("scalar", None, waits=[(sWBSp[0], cp0), (sWBSp[1], cp1)])
        sched.op("sync", None, waits=[(sWBSp[0], cp0), (sWBSp[1], cp1)])

        def load_wq(h):
            wpar = h % 2
            st["wq_tok"][wpar] = sched.op("sync", lambda e, h=h, wpar=wpar: e.dma_start(
                out=wq_t[wpar][:], in_=Wqa_bf_v[:, :, h * 128:(h + 1) * 128]),
                waits=[(sPE, st["wq_free"][wpar])], inc=sWQ2p[wpar], inc_n=16)

        def load_wo(ot):
            wpar = ot % 2
            st["wo_tok"][wpar] = sched.op("scalar", lambda a, ot=ot, wpar=wpar: a.dma_start(
                out=wob_t[wpar][:], in_=Woa_bf_v[:, :, ot * 128:(ot + 1) * 128]),
                waits=[(sPE, st["wo_free"][wpar])], inc=sWO2p[wpar], inc_n=16)
        st = {
            "wq_tok": [0, 0], "wq_free": [0, 0],
            "wo_tok": [0, 0], "wo_free": [0, 0],
            "lnb_free": 0,
            "q_evac": 0,
            "qh_free": 0,
            "out_done": 0,
            "exp_last": 0,
            "wo_evac": 0,
            "wo_evac_b": [0, 0], "res_store_b": [0, 0],
            "wo_mm": 0,
            "res_store": 0,
            "xres_free": 0,
            "den_free": 0,
            "den_mm": 0,
        }

        for blk in range(NBLK):
            bsl = slice(blk * 1024, (blk + 1) * 1024)
            lnb_tok = sched.op("sync", lambda e, bsl=bsl: e.dma_start(
                out=lnb[:], in_=lnsp_v[:, :, bsl]),
                waits=[(sPE, st["lnb_free"])], inc=sLNB, inc_n=16)
            load_wq(0)
            load_wq(1)
            load_wo(0)
            load_wo(1)
            for h in range(HA):
                wpar = h % 2
                wq_tok = st["wq_tok"][wpar]
                for c in range(NCH):
                    w = []
                    if c == 0:
                        w = [(sWQ2p[wpar], wq_tok), (sLNB, lnb_tok),
                             (sDV, st["out_done"]), (sAC, st["wo_evac"])]
                    for half in range(2):
                        sched.op("tensor", lambda t, c=c, half=half, wpar=wpar: t.matmul(
                            psC[:, half * 512:(half + 1) * 512], wq_t[wpar][:, c],
                            lnb[:, c, half * 512:(half + 1) * 512],
                            start=(c == 0), stop=(c == NCH - 1)),
                            waits=w if half == 0 else (),
                            inc=sPE if (c == NCH - 1 and half == 1) else None)
                qtok = sPE.v
                st["wq_free"][wpar] = qtok
                if h + 2 < HA:
                    load_wq(h + 2)
                qev = sched.op("scalar", lambda a, h=h: a.add(
                    qh[:], psC[:, :], bq_sb[:, h:h + 1]),
                    waits=[(sPE, qtok), (sPE, st["qh_free"])], inc=sAC)
                exp_tok = {}
                attn_mm = {}
                acc_tok = {}

                def emit_attnv(kv):
                    par = kv % 2
                    w1 = [(sAC, exp_tok[kv])]
                    if kv == 0:
                        w1 += [(sAC, qev), (sDV, st["out_done"]), (sAC, st["wo_evac"])]
                    for half in range(2):
                        sched.op("tensor", lambda t, kv=kv, half=half, par=par: t.matmul(
                            psC[:, half * 512:(half + 1) * 512], v_sb[:, kv],
                            expb[par][:, half * 512:(half + 1) * 512],
                            start=(kv == 0), stop=(kv == NKV - 1)),
                            waits=w1 if half == 0 else (),
                            inc=sPE if half == 1 else None)
                    attn_mm[kv] = sPE.v
                    wacc = [(sAC, exp_tok[kv])]
                    if kv == 0:
                        acc_tok[kv] = sched.op("vector", lambda v, par=par: v.tensor_copy(
                            acc[:], expb[par][:]), waits=wacc, inc=sDV)
                    else:
                        acc_tok[kv] = sched.op("vector", lambda v, par=par: v.tensor_add(
                            acc[:], acc[:], expb[par][:]), waits=wacc, inc=sDV)

                for kv in range(NKV):
                    par = kv % 2
                    pscore = psA if par == 0 else psB
                    w0 = [(sAC, qev)]
                    if kv >= 2:
                        w0.append((sAC, exp_tok[kv - 2]))
                    elif kv == 0:
                        w0.append((sAC, st["exp_last"]))
                        w0.append((sKVL, 64))
                    for half in range(2):
                        sched.op("tensor", lambda t, kv=kv, half=half, pscore=pscore: t.matmul(
                            pscore[:, half * 512:(half + 1) * 512],
                            kT_sb[:, kv * 128:(kv + 1) * 128],
                            qh[:, half * 512:(half + 1) * 512], start=True, stop=True),
                            waits=w0 if half == 0 else (),
                            inc=sPE if half == 1 else None)
                    sctok = sPE.v
                    if kv > 0:
                        emit_attnv(kv - 1)
                    wexp = [(sPE, sctok)]
                    if kv >= 2:
                        wexp.append((sPE, attn_mm[kv - 2]))
                        wexp.append((sDV, acc_tok[kv - 2]))
                    exp_tok[kv] = sched.op("scalar", lambda a, par=par, pscore=pscore: a.activation(
                        expb[par][:], pscore[:, :], AF.Exp, scale=RSQRT_DA),
                        waits=wexp, inc=sAC)
                emit_attnv(NKV - 1)
                av_tok = sPE.v
                st["qh_free"] = av_tok
                st["exp_last"] = exp_tok[NKV - 1]
                cb = sched.op("vector", lambda v: v.tensor_copy(acc_bf[:], acc[:]),
                              waits=[(sDV, acc_tok[NKV - 1]), (sPE, st["den_mm"])],
                              inc=sDV)
                sched.op("tensor", lambda t: t.matmul(
                    psD[0:1, :], onescol[:], acc_bf[:, 0:512], start=True, stop=True),
                    waits=[(sDV, cb)])
                den_mm = sched.op("tensor", lambda t: t.matmul(
                    psD2[0:1, :], onescol[:], acc_bf[:, 512:1024], start=True, stop=True),
                    inc=sPE)
                st["den_mm"] = den_mm
                d1 = sched.op("vector", lambda v: v.tensor_copy(den_sb[:, 0:512], psD[0:1, :]),
                              waits=[(sPE, den_mm), (sPE, st["den_free"])], inc=sDV)
                sched.op("vector", lambda v: v.tensor_copy(den_sb[:, 512:1024], psD2[0:1, :]),
                         inc=sDV)
                d3 = sched.op("vector", lambda v: v.reciprocal(rec_row[:], den_sb[:]), inc=sDV)
                out_last = 0
                for half in range(2):
                    pden = psD if half == 0 else psD2
                    bt = sched.op("tensor", lambda t, half=half, pden=pden: t.matmul(
                        pden[:, :], ones1[:], rec_row[:, half * 512:(half + 1) * 512],
                        start=True, stop=True),
                        waits=[(sDV, d3)], inc=sPE)
                    rv = sched.op("vector", lambda v, pden=pden: v.tensor_copy(recb[:], pden[:, :]),
                                  waits=[(sPE, bt)], inc=sDV)
                    out_last = sched.op("vector", lambda v, h=h, half=half: v.tensor_mul(
                        out_t[:, h, half * 512:(half + 1) * 512],
                        psC[:, half * 512:(half + 1) * 512], recb[:]),
                        waits=[(sPE, st["wo_mm"])] if (h == 0 and half == 0) else (),
                        inc=sDV)
                    st["den_free"] = bt
                st["out_done"] = out_last
            for ot in range(NCH):
                wpar = ot % 2
                wo_tok = st["wo_tok"][wpar]
                xr_tok = sched.op("sync", lambda e, ot=ot, bsl=bsl: e.dma_start(
                    out=xres[:], in_=xT_v[:, ot, bsl]),
                    waits=[(sDV, st["xres_free"])], inc=sXR, inc_n=16)
                opar = ot % 2
                ppo = psC if opar == 0 else psA
                wrr = wo_res if opar == 0 else wo_res2
                for c in range(NCH):
                    w = []
                    if c == 0:
                        w = [(sWO2p[wpar], wo_tok), (sDV, st["out_done"]),
                             (sAC, st["wo_evac_b"][opar])]
                        if ot == 0:
                            w.append((sAC, st["q_evac"]))
                    for half in range(2):
                        sched.op("tensor", lambda t, c=c, half=half, wpar=wpar, ppo=ppo: t.matmul(
                            ppo[:, half * 512:(half + 1) * 512], wob_t[wpar][:, c],
                            out_t[:, c, half * 512:(half + 1) * 512],
                            start=(c == 0), stop=(c == NCH - 1)),
                            waits=w if half == 0 else (),
                            inc=sPE if (c == NCH - 1 and half == 1) else None)
                ottok = sPE.v
                st["wo_free"][wpar] = ottok
                st["wo_mm"] = ottok
                if ot + 2 < NCH:
                    load_wo(ot + 2)
                ev = sched.op("scalar", lambda a, ot=ot, ppo=ppo, wrr=wrr: a.add(
                    wrr[:], ppo[:, :], bo_sb[:, ot:ot + 1]),
                    waits=[(sPE, ottok), (sRS, st["res_store_b"][opar])], inc=sAC)
                st["wo_evac_b"][opar] = ev
                st["wo_evac"] = ev
                ad = sched.op("vector", lambda v, wrr=wrr: v.tensor_add(wrr[:], wrr[:], xres[:]),
                              waits=[(sAC, ev), (sXR, xr_tok)], inc=sDV)
                st["xres_free"] = ad
                st["res_store_b"][opar] = sched.op("sync", lambda e, ot=ot, bsl=bsl, wrr=wrr: e.dma_start(
                    out=res1_v[:, ot, bsl], in_=wrr[:]),
                    waits=[(sDV, ad)], inc=sRS, inc_n=16)
            st["lnb_free"] = sPE.v
        arena.pop()

    cwI_gu = cwI[:, 0:GU_SH].rearrange("p (j m) -> p j m", j=11)
    cwI_d = cwI[:, GU_SH:GU_SH + D_SH].rearrange("p (j m) -> p j m", j=2)
    cwo_gu = cwO[:, :, 0:GU_SH].rearrange("r p (j m) -> r p j m", j=11)
    cwo_d = cwO[:, :, GU_SH:GU_SH + D_SH].rearrange("r p (j m) -> r p j m", j=2)

    def p0_shard():
        arena.push()
        WMAX = I_FFN // 8
        pk = [arena.take(I32, WMAX) for _ in range(2)]
        a_all = trd[:, 0:11264].bitcast(I32).rearrange("p (a b) -> p a b", a=8)
        s_all = arena.take(I32, 8, WMAX)
        af = arena.take(BF16, 8, WMAX)
        sf = arena.take(BF16, 8, WMAX)
        e1 = arena.take(BF16, 8, WMAX)
        e2 = arena.take(BF16, 8, WMAX)
        mag = arena.take(BF16, 8, WMAX)
        wt0 = arena.take(BF16, 8 * WMAX)
        wt = [wt0, wt0]
        sc_h = [arena.take(F16, 128) for _ in range(2)]
        sc_f = [arena.take(F32, 128) for _ in range(2)]
        scl = [arena.take(F32, 64) for _ in range(2)]
        stg0 = arena.take(BF16, ICH, 128)
        stg = [stg0, stg0]

        st = {"pk_tok": [0, 0], "pk_free": [0, 0], "wt_free": 0,
              "sc_free": [0, 0], "trt_pe": [0, 0], "psD2_free": 0,
              "stg_prev": (sWSp[0], 0)}

        def one_tile(par, W, G, pk_src, sc_src, dst):
            st["pk_tok"][par] = sched.op("sync", lambda e, par=par, W=W, pk_src=pk_src: e.dma_start(
                out=pk[par][:, 0:W], in_=pk_src),
                waits=[(sDV, st["pk_free"][par])], inc=sPKp[par], inc_n=16)
            sct = sched.op("sync", lambda e, G=G, par=par, sc_src=sc_src: e.dma_start(
                out=sc_h[par][0:G, :], in_=sc_src),
                waits=[(sDV, st["sc_free"][par])], inc=sSCp[par], inc_n=16)
            cvt = sched.op("vector", lambda v, G=G, par=par: v.tensor_copy(
                sc_f[par][0:G, :], sc_h[par][0:G, :]),
                waits=[(sSCp[par], sct), (sPE, st["trt_pe"][par])], inc=sDV)
            st["sc_free"][par] = cvt
            trt = sched.op("tensor", lambda t, G=G, par=par: t.transpose(
                psD2[:, 0:G], sc_f[par][0:G, :], ident_f32[0:G, 0:G]),
                waits=[(sDV, cvt), (sDV, st["psD2_free"])], inc=sPE)
            st["trt_pe"][par] = trt
            sev = sched.op("vector", lambda v, par=par, G=G: v.tensor_copy(
                scl[par][:, 0:G], psD2[:, 0:G]),
                waits=[(sPE, trt)], inc=sDV)
            st["psD2_free"] = sev
            w0 = [(sPKp[par], st["pk_tok"][par])]
            for j in range(8):
                sched.op("vector", lambda v, j=j, par=par, W=W: v.tensor_scalar(
                    a_all[:, j, 0:W], pk[par][:, 0:W], 4 * j, 7,
                    OP.logical_shift_right, OP.bitwise_and),
                    waits=w0 if j == 0 else ())
                sched.op("vector", lambda v, j=j, par=par, W=W: v.tensor_scalar(
                    s_all[:, j, 0:W], pk[par][:, 0:W], 4 * j + 3, 1,
                    OP.logical_shift_right, OP.bitwise_and))
            tok = sched.op("vector", lambda v: v.engine_nop(), inc=sDV)
            st["pk_free"][par] = tok
            n = 8 * W
            aw = a_all[:, :, 0:W]
            sw_ = s_all[:, :, 0:W]
            afw = af[:, :, 0:W]
            sfw = sf[:, :, 0:W]
            e1w = e1[:, :, 0:W]
            e2w = e2[:, :, 0:W]
            magw = mag[:, :, 0:W]
            c1 = sched.op("vector", lambda v: v.tensor_copy(afw, aw), waits=[(sDV, tok)], inc=sDV)
            c2 = sched.op("vector", lambda v: v.tensor_copy(sfw, sw_), inc=sDV)
            c3 = sched.op("vector", lambda v: v.tensor_scalar(
                e1w, afw, 1.0, -2.0, OP.mult, OP.add), waits=[(sDV, c1)], inc=sDV)
            c4 = sched.op("vector", lambda v: v.tensor_scalar(
                e2w, afw, 2.0, -8.0, OP.mult, OP.add), inc=sDV)
            c5 = sched.op("vector", lambda v: v.tensor_tensor(magw, e1w, e2w, OP.max),
                          waits=[(sDV, c4)], inc=sDV)
            c6 = sched.op("vector", lambda v: v.scalar_tensor_tensor(
                e1w, afw, 0.5, magw, OP.mult, OP.max), waits=[(sDV, c5)], inc=sDV)
            c7 = sched.op("vector", lambda v: v.tensor_scalar(
                e2w, sfw, -2.0, 1.0, OP.mult, OP.add), waits=[(sDV, c2)], inc=sDV)
            wt_v = wt[par][:, 0:n].rearrange("p (w j) -> p j w", j=8)
            gtok = 0
            for g in range(G):
                w = []
                if g == 0:
                    w = [(sPE, st["wt_free"]), (sDV, c7), (sDV, sev)]
                gtok = sched.op("vector", lambda v, g=g, par=par, wt_v=wt_v: v.scalar_tensor_tensor(
                    wt_v[:, :, 16 * g:16 * g + 16], e1[:, :, 16 * g:16 * g + 16],
                    scl[par][:, g:g + 1], e2[:, :, 16 * g:16 * g + 16],
                    OP.mult, OP.mult), waits=w, inc=sDV)
            wtf = wt[par][:, 0:n]
            ev = {}
            for ck in range(G):
                reg = psC[:, (ck % 2) * 512:(ck % 2) * 512 + 128]
                w = []
                if ck == 0:
                    w = [(sDV, gtok)]
                if ck >= 2:
                    w.append((sDV, ev[ck - 2]))
                else:
                    w.append((sDV, st["psC_ev"][ck % 2]))
                trp = sched.op("tensor", lambda t, ck=ck, reg=reg, wtf=wtf: t.matmul(
                    reg, wtf[:, ck * 128:(ck + 1) * 128], ident[:],
                    start=True, stop=True), waits=w, inc=sPE)
                wev = [(sPE, trp)]
                if ck == 0:
                    wev.append(st["stg_prev"])
                ev[ck] = sched.op("vector", lambda v, ck=ck, par=par, reg=reg: v.tensor_copy(
                    stg[par][:, ck], reg), waits=wev, inc=sDV)
            st["psC_ev"][(G - 1) % 2] = ev[G - 1]
            st["psC_ev"][G % 2] = ev[G - 2]
            st["wt_free"] = sPE.v
            stok = sched.op("sync", lambda e, par=par, G=G, dst=dst: e.dma_start(
                out=dst, in_=stg[par].rearrange("p a b -> p (a b)")[:, 0:G * 128]),
                waits=[(sDV, ev[G - 1])], inc=sWSp[par], inc_n=16)
            st["stg_prev"] = (sWSp[par], stok)

        st["psC_ev"] = [0, 0]
        for j in range(11):
            one_tile(j % 2, 256, 16,
                     gup_sh[j * 128:(j + 1) * 128, :],
                     gus_sh[:, j * 128:(j + 1) * 128],
                     cwI_gu[:, j])
        for j in range(2):
            one_tile(j % 2, I_FFN // 8, ICH,
                     dp_sh[j * 128:(j + 1) * 128, :],
                     dsc_sh[:, j * 128:(j + 1) * 128],
                     cwI_d[:, j])
        sched.op("sync", None, waits=[(sDV, sDV.v), (sPE, sPE.v)])
        arena.pop()

    def make_p0_wb(wbl, wstg):
        st = {"ld": [0, 0], "ld_free": [(sPE, 0), (sPE, 0)],
              "stg_store": [0, 0], "psC_ev": [0, 0]}
        mats = [Wq_b, Wk_b, Wv_b, Wo_b]
        t_blocks = [("t", m, hg) for m in range(4) for hg in range(16)]
        cp = [(Wq_a, Wqa_bf), (Wo_a, Woa_bf)]
        c_blocks = [("c", m, rt) for m in range(2) for rt in range(16)]
        blocks = []
        for i in range(32):
            blocks.append(c_blocks[i])
            blocks.append(t_blocks[i])
        blocks += t_blocks[32:]
        cp_left = [len(c_blocks)]
        cursor = [0]

        def emit_copy(m, rt, par):
            src, dst = cp[m]
            ld = sched.op("gpsimd", lambda g, par=par, src=src, rt=rt: g.dma_start(
                out=wbl[par][:], in_=src[rt * 128:(rt + 1) * 128, :]),
                waits=[st["ld_free"][par]], inc=sWBp[par], inc_n=16)
            tok = sched.op("sync", lambda e, par=par, dst=dst, rt=rt: e.dma_start(
                out=dst[rt * 128:(rt + 1) * 128, :], in_=wbl[par][:]),
                waits=[(sWBp[par], ld)], inc=sWBSp[par], inc_n=16)
            st["ld_free"][par] = (sWBSp[par], tok)

        def emit_one(m, hg, par):
            st["ld"][par] = sched.op("gpsimd", lambda g, m=m, hg=hg, par=par: g.dma_start(
                out=wbl[par][:], in_=mats[m][hg * 128:(hg + 1) * 128, :]),
                waits=[st["ld_free"][par]], inc=sWBp[par], inc_n=16)
            ev = {}
            for ck in range(NCH):
                reg = psC[:, (ck % 2) * 512:(ck % 2) * 512 + 128]
                w = []
                if ck == 0:
                    w.append((sWBp[par], st["ld"][par]))
                if ck >= 2:
                    w.append((sDV, ev[ck - 2]))
                else:
                    w.append((sDV, st["psC_ev"][ck % 2]))
                trp = sched.op("tensor", lambda t, ck=ck, par=par, reg=reg: t.matmul(
                    reg, wbl[par][:, ck * 128:(ck + 1) * 128], ident[:],
                    start=True, stop=True), waits=w, inc=sPE)
                wev = [(sPE, trp)]
                if ck == 0:
                    wev.append((sWBSp[par], st["stg_store"][par]))
                ev[ck] = sched.op("vector", lambda v, ck=ck, par=par, reg=reg: v.tensor_copy(
                    wstg[par][:, ck], reg), waits=wev, inc=sDV)
            st["psC_ev"][(NCH - 1) % 2] = ev[NCH - 1]
            st["psC_ev"][NCH % 2] = ev[NCH - 2]
            st["ld_free"][par] = (sPE, sPE.v)
            if m < 3:
                dst = WbT[m, hg].rearrange("ck p q -> p ck q")
            else:
                dst = WoTb[:, hg].rearrange("ck p q -> p ck q")
            st["stg_store"][par] = sched.op("sync", lambda e, par=par, dst=dst: e.dma_start(
                out=dst, in_=wstg[par][:]),
                waits=[(sDV, ev[NCH - 1])], inc=sWBSp[par], inc_n=16)

        def emit(n=None):
            hi = len(blocks) if n is None else min(cursor[0] + n, len(blocks))
            while cursor[0] < hi:
                kind, m, i = blocks[cursor[0]]
                if kind == "t":
                    emit_one(m, i, cursor[0] % 2)
                else:
                    emit_copy(m, i, cursor[0] % 2)
                    cp_left[0] -= 1
                    if cp_left[0] == 0:
                        wb_cp_tok[0] = (sWBSp[0].v, sWBSp[1].v)
                cursor[0] += 1
        return emit

    def ffn_phase(src_v, dst_v, TOK, dump=False):
        W = 512
        NSL = TOK // W
        arena.push()
        xf = [arena.take(F32, W) for _ in range(2)]
        xbt = [arena.take(BF16, W) for _ in range(2)]
        ln2 = [arena.take(BF16, NCH, W) for _ in range(2)]
        hbuf = arena.take(BF16, ICH, W)
        wgu = [arena.take(BF16, NCH, 128) for _ in range(2)]
        upt = arena.take(BF16, W)
        ybuf = arena.take(F32, NCH, W)
        ysqt = arena.take(BF16, W)
        invrb = arena.take(F32, W)
        tmpf = [arena.take(F32, W) for _ in range(2)]
        outf = [arena.take(F32, W) for _ in range(2)]
        rt = [arena.take(F32, W) for _ in range(2)]
        xsqt = arena.take(BF16, W)
        mbv = arena.take(F32, W)
        ibv = arena.take(F32, W)
        lnf = arena.take(F32, W)
        rrms = arena.row(F32, 2048)
        mrow_b = rows[:, 2560:2560 + W]
        isrow_b = rows[:, 3072:3072 + W]
        wd = [trd[:, 0:I_FFN].rearrange("p (a b) -> p a b", b=128),
              trd[:, I_FFN:2 * I_FFN].rearrange("p (a b) -> p a b", b=128)]
        wdf = [trd[:, 0:I_FFN], trd[:, I_FFN:2 * I_FFN]]

        sched.op("sync", None, waits=[(sRS, sRS.v), (sCCW, 1),
                                      (sWBp[0], sWBp[0].v), (sWBp[1], sWBp[1].v),
                                      (sWBSp[0], sWBSp[0].v), (sWBSp[1], sWBSp[1].v)])
        st = {"xf_tok": [0, 0], "xf_free": [0, 0],
              "psD_free": 0, "psD2_free": 0, "rows_free_pe": 0,
              "ln_free": [0, 0], "ln_last": [0, 0],
              "silu_ac": [0, 0], "up_dv": [0, 0], "h_free": 0,
              "wgu_tok": [0, 0], "wgu_free": [0, 0],
              "wd_tok": [0, 0], "wd_free": [0, 0],
              "y_evac": [0, 0], "psC_free": 0, "rms_rows_pe": 0,
              "out_store": [0, 0], "rt_free": [0, 0], "up_cp": 0,
              "wgu_hoisted": False}

        def emit_ln(s):
            par = s % 2
            sl = slice(s * W, (s + 1) * W)
            prev_sxx = 0
            pe_toks = {}
            ac_toks = {}
            for c in range(NCH):
                cpar = c % 2
                st["xf_tok"][cpar] = sched.op("gpsimd", lambda g, c=c, cpar=cpar, sl=sl: g.dma_start(
                    out=xf[cpar][:], in_=src_v[:, c, sl]),
                    waits=[(sDV, st["xf_free"][cpar])], inc=sXFp[cpar], inc_n=16)
                wct = [(sXFp[cpar], st["xf_tok"][cpar])]
                if c >= 2:
                    wct += [(sPE, pe_toks[c - 2]), (sAC, ac_toks[c - 2])]
                ct = sched.op("vector", lambda v, cpar=cpar: v.tensor_copy(xbt[cpar][:], xf[cpar][:]),
                              waits=wct, inc=sDV)
                st["xf_free"][cpar] = ct
                w2 = [(sDV, ct)]
                if c > 0:
                    w2.append((sPE, prev_sxx))
                sq_last = sched.op("scalar", lambda a, cpar=cpar: a.activation(
                    xsqt[:], xbt[cpar][:], AF.Square), waits=w2, inc=sAC)
                ac_toks[c] = sq_last
                w3 = [(sDV, ct)]
                if c == 0:
                    w3.append((sDV, st["psD_free"]))
                sched.op("tensor", lambda t, c=c, cpar=cpar: t.matmul(
                    psD[0:1, 0:W], onescol[:], xbt[cpar][:],
                    start=(c == 0), stop=(c == NCH - 1)), waits=w3)
                w4 = [(sAC, sq_last)]
                if c == 0:
                    w4.append((sDV, st["psD2_free"]))
                prev_sxx = sched.op("tensor", lambda t, c=c: t.matmul(
                    psD2[0:1, 0:W], onescol[:], xsqt[:],
                    start=(c == 0), stop=(c == NCH - 1)), waits=w4, inc=sPE)
                pe_toks[c] = prev_sxx
            t1 = sched.op("vector", lambda v: v.tensor_scalar(
                rows[:, 0:W], psD[0:1, 0:W], 1.0 / D, None, OP.mult),
                waits=[(sPE, prev_sxx), (sPE, st["rows_free_pe"])], inc=sDV)
            t2 = sched.op("vector", lambda v: v.tensor_scalar(
                rows[:, 1024:1024 + W], psD2[0:1, 0:W], 1.0 / D, float(LN_EPS),
                OP.mult, OP.add), inc=sDV)
            t3 = sched.op("scalar", lambda a: a.activation(
                rows[:, 512:512 + W], rows[:, 0:W], AF.Square), waits=[(sDV, t1)], inc=sAC)
            t4 = sched.op("vector", lambda v: v.tensor_sub(
                rows[:, 1536:1536 + W], rows[:, 1024:1024 + W], rows[:, 512:512 + W]),
                waits=[(sAC, t3)], inc=sDV)
            t5 = sched.op("scalar", lambda a: a.activation(
                rows[:, 2048:2048 + W], rows[:, 1536:1536 + W], AF.Ln),
                waits=[(sDV, t4)], inc=sAC)
            t6 = sched.op("scalar", lambda a: a.activation(
                rows2[:, 0:W], rows[:, 2048:2048 + W], AF.Exp, scale=-0.5),
                waits=[(sAC, t5)], inc=sAC)
            t7 = sched.op("vector", lambda v: v.tensor_copy(mrow_b[:], rows[:, 0:W]), inc=sDV)
            t8 = sched.op("vector", lambda v: v.tensor_copy(isrow_b[:], rows2[:, 0:W]),
                          waits=[(sAC, t6)], inc=sDV)
            b1 = sched.op("tensor", lambda t: t.matmul(
                psD[:, 0:W], ones1[:], mrow_b[:], start=True, stop=True),
                waits=[(sDV, t8)], inc=sPE)
            b2 = sched.op("tensor", lambda t: t.matmul(
                psD2[:, 0:W], ones1[:], isrow_b[:], start=True, stop=True), inc=sPE)
            st["rows_free_pe"] = b2
            m1 = sched.op("vector", lambda v: v.tensor_copy(mbv[:], psD[:, 0:W]),
                          waits=[(sPE, b2)], inc=sDV)
            st["psD_free"] = m1
            i1 = sched.op("vector", lambda v: v.tensor_copy(ibv[:], psD2[:, 0:W]), inc=sDV)
            st["psD2_free"] = i1
            ln_last = 0
            for c in range(NCH):
                cpar = c % 2
                st["xf_tok"][cpar] = sched.op("gpsimd", lambda g, c=c, cpar=cpar, sl=sl: g.dma_start(
                    out=xf[cpar][:], in_=src_v[:, c, sl]),
                    waits=[(sDV, st["xf_free"][cpar])], inc=sXFp[cpar], inc_n=16)
                w = [(sXFp[cpar], st["xf_tok"][cpar]), (sDV, i1)]
                if c == 0:
                    w.append((sPE, st["ln_free"][par]))
                ct = sched.op("vector", lambda v, cpar=cpar: v.tensor_sub(
                    lnf[:], xf[cpar][:], mbv[:]), waits=w, inc=sDV)
                st["xf_free"][cpar] = ct
                ln_last = sched.op("vector", lambda v, c=c, par=par: v.tensor_mul(
                    ln2[par][:, c], lnf[:], ibv[:]), inc=sDV)
            st["ln_last"][par] = ln_last

        ost = {"s": None, "ot": 0}

        def emit_out_iter():
            if ost["s"] is None or ost["ot"] >= DOT:
                return
            ot = ost["ot"]
            par = ot % 2
            osl = slice(ost["s"] * W, (ost["s"] + 1) * W)
            ost["ot"] += 1
            rtok = sched.op("gpsimd", lambda g, ot=ot, osl=osl, par=par: g.dma_start(
                out=rt[par][:], in_=src_v[:, ot, osl]),
                waits=[(sDV, st["rt_free"][par])], inc=sXR, inc_n=16)
            sched.op("vector", lambda v, ot=ot, par=par: v.tensor_mul(
                tmpf[par][:], ybuf[:, ot], invrb[:]))
            atok = sched.op("vector", lambda v, par=par: v.tensor_add(
                outf[par][:], tmpf[par][:], rt[par][:]),
                waits=[(sXR, rtok), (sRS, st["out_store"][par])], inc=sDV)
            st["rt_free"][par] = atok
            st["out_store"][par] = sched.op("gpsimd", lambda g, ot=ot, osl=osl, par=par: g.dma_start(
                out=dst_v[:, ot, osl], in_=outf[par][:]),
                waits=[(sDV, atok)], inc=sRS, inc_n=16)

        emit_ln(0)
        for s in range(NSL):
            par = s % 2
            sl = slice(s * W, (s + 1) * W)
            ln_last = st["ln_last"][par]
            for ot0 in range(2):
                r_, jj = divmod(ot0, 2)
                st["wd_tok"][ot0] = sched.op("sync", lambda e, r_=r_, jj=jj, ot0=ot0: e.dma_start(
                    out=wdf[ot0], in_=cwo_d[r_, :, jj]),
                    waits=[(sPE, st["wd_free"][ot0]), (sCCW, 1)], inc=sWDp[ot0], inc_n=16)
            for i in range(88):
                wpar = i % 2
                if i >= 2 or not st["wgu_hoisted"]:
                    r_, jj = divmod(i, 11)
                    st["wgu_tok"][wpar] = sched.op("sync", lambda e, r_=r_, jj=jj, wpar=wpar: e.dma_start(
                        out=wgu[wpar].rearrange("p a b -> p (a b)"), in_=cwo_gu[r_, :, jj]),
                        waits=[(sPE, st["wgu_free"][wpar])], inc=sWGp[wpar], inc_n=16)
                ppar = i % 2
                pp = psA if ppar == 0 else psB
                for c in range(NCH):
                    w = []
                    if c == 0:
                        w = [(sWGp[wpar], st["wgu_tok"][wpar]), (sDV, ln_last),
                             (sAC, st["silu_ac"][ppar]), (sDV, st["up_dv"][ppar]),
                             (sDV, st["y_evac"][ppar])]
                    sched.op("tensor", lambda t, c=c, wpar=wpar, pp=pp, par=par: t.matmul(
                        pp[:, 0:W], wgu[wpar][:, c],
                        ln2[par][:, c], start=(c == 0), stop=(c == NCH - 1)),
                        waits=w, inc=sPE if c == NCH - 1 else None)
                mmtok = sPE.v
                st["wgu_free"][wpar] = mmtok
                if i < 44:
                    ev = sched.op("scalar", lambda a, i=i, pp=pp: a.activation(
                        hbuf[:, i], pp[:, 0:W], AF.Silu),
                        waits=[(sPE, mmtok), (sPE, st["h_free"])], inc=sAC)
                    st["silu_ac"][ppar] = ev
                else:
                    iu = i - 44
                    ev1 = sched.op("vector", lambda v, pp=pp, iu=iu: v.tensor_tensor(
                        upt[:], pp[:, 0:W], hbuf[:, iu], OP.mult),
                        waits=[(sPE, mmtok), (sAC, st["up_cp"])], inc=sDV)
                    st["up_cp"] = sched.op("scalar", lambda a, iu=iu: a.copy(
                        hbuf[:, iu], upt[:]), waits=[(sDV, ev1)], inc=sAC)
                    st["up_dv"][ppar] = ev1
                if i % 5 == 4:
                    emit_out_iter()
            for _ in range(DOT):
                emit_out_iter()
            ost["s"] = s
            ost["ot"] = 0
            h_done = st["up_cp"]
            st["ln_free"][par] = sPE.v
            if s + 1 < NSL:
                emit_ln(s + 1)
                for i0 in range(2):
                    r_, jj = divmod(i0, 11)
                    st["wgu_tok"][i0] = sched.op("sync", lambda e, r_=r_, jj=jj, i0=i0: e.dma_start(
                        out=wgu[i0].rearrange("p a b -> p (a b)"), in_=cwo_gu[r_, :, jj]),
                        waits=[(sPE, st["wgu_free"][i0])], inc=sWGp[i0], inc_n=16)
                st["wgu_hoisted"] = True
            sq_mm = 0
            for ot in range(DOT):
                wpar = ot % 2
                if ot >= 2:
                    r_, jj = divmod(ot, 2)
                    st["wd_tok"][wpar] = sched.op("sync", lambda e, r_=r_, jj=jj, wpar=wpar: e.dma_start(
                        out=wdf[wpar], in_=cwo_d[r_, :, jj]),
                        waits=[(sPE, st["wd_free"][wpar])], inc=sWDp[wpar], inc_n=16)
                pp = psA if ot % 2 == 0 else psB
                for c in range(ICH):
                    w = []
                    if c == 0:
                        w = [(sWDp[wpar], st["wd_tok"][wpar]), (sAC, h_done),
                             (sDV, st["y_evac"][ot % 2])]
                    sched.op("tensor", lambda t, c=c, wpar=wpar, pp=pp: t.matmul(
                        pp[:, 0:W], wd[wpar][:, c], hbuf[:, c],
                        start=(c == 0), stop=(c == ICH - 1)),
                        waits=w, inc=sPE if c == ICH - 1 else None)
                dtok = sPE.v
                st["wd_free"][wpar] = dtok
                yv = sched.op("vector", lambda v, ot=ot, pp=pp: v.tensor_copy(
                    ybuf[:, ot], pp[:, 0:W]), waits=[(sPE, dtok)], inc=sDV)
                st["y_evac"][ot % 2] = yv
                sqt = sched.op("scalar", lambda a, ot=ot: a.activation(
                    ysqt[:], ybuf[:, ot], AF.Square),
                    waits=[(sDV, yv), (sPE, sq_mm)], inc=sAC)
                w = [(sAC, sqt)]
                if ot == 0:
                    w.append((sDV, st["psC_free"]))
                sq_mm = sched.op("tensor", lambda t, ot=ot: t.matmul(
                    psC[0:1, 0:W], onescol[:], ysqt[:],
                    start=(ot == 0), stop=(ot == DOT - 1)), waits=w, inc=sPE)
            r1 = sched.op("vector", lambda v: v.tensor_scalar(
                rrms[:, 0:W], psC[0:1, 0:W], 1.0 / D, float(RMS_EPS), OP.mult, OP.add),
                waits=[(sPE, sq_mm), (sPE, st["rms_rows_pe"])], inc=sDV)
            r2 = sched.op("scalar", lambda a: a.activation(
                rrms[:, 512:512 + W], rrms[:, 0:W], AF.Ln), waits=[(sDV, r1)], inc=sAC)
            r3 = sched.op("scalar", lambda a: a.activation(
                rrms[:, 1024:1024 + W], rrms[:, 512:512 + W], AF.Exp, scale=-0.5),
                waits=[(sAC, r2)], inc=sAC)
            r4 = sched.op("vector", lambda v: v.tensor_copy(
                rrms[:, 1536:1536 + W], rrms[:, 1024:1024 + W]),
                waits=[(sAC, r3)], inc=sDV)
            b3 = sched.op("tensor", lambda t: t.matmul(
                psC[:, 0:W], ones1[:], rrms[0:1, 1536:1536 + W], start=True, stop=True),
                waits=[(sDV, r4)], inc=sPE)
            st["rms_rows_pe"] = b3
            iv = sched.op("vector", lambda v: v.tensor_copy(invrb[:], psC[:, 0:W]),
                          waits=[(sPE, b3)], inc=sDV)
            st["psC_free"] = iv
            st["h_free"] = sPE.v
        for ot in range(DOT):
            emit_out_iter()
        arena.pop()

    def p3_attnb():
        arena.push()
        rsk_all = arena.take(F32, 16, 32, 2)
        bdq_all = arena.take(BF16, 16, 128)
        bdk_all = arena.take(BF16, 16, 128)
        t0 = sched.op("gpsimd", lambda g: g.memset(bdq_all[:], 0.0), inc=sGP)
        sched.op("gpsimd", lambda g: g.memset(bdk_all[:], 0.0), inc=sGP)
        gp0 = sGP.v
        for hg in range(16):
            for hh in range(2):
                sched.op("gpsimd", lambda g, hg=hg, hh=hh: g.dma_start(
                    out=bdq_all[hh * 64:(hh + 1) * 64, hg, hh * 64:(hh + 1) * 64],
                    in_=Wfq[2 * hg + hh]),
                    waits=[(sGP, gp0)] if (hg == 0 and hh == 0) else (), inc=sBD, inc_n=16)
                sched.op("gpsimd", lambda g, hg=hg, hh=hh: g.dma_start(
                    out=bdk_all[hh * 64:(hh + 1) * 64, hg, hh * 64:(hh + 1) * 64],
                    in_=Wfk[2 * hg + hh]),
                    inc=sBD, inc_n=16)
        bd_tok = sBD.v

        lnst = {"psD_free": 0, "psD2_free": 0, "rows_free_pe": 0}

        def emit_ln512(src_v2, sl, ln_out, lbufs, ln_free_pe):
            xf, xbt, xsqt, lnf, mbv, ibv = lbufs
            prev_sxx = 0
            pe_toks = {}
            ac_toks = {}
            xf_tok = [lnst.get("xf_tok0", 0), lnst.get("xf_tok1", 0)]
            xf_free = [lnst.get("xf_free0", 0), lnst.get("xf_free1", 0)]
            for c in range(NCH):
                cpar = c % 2
                xf_tok[cpar] = sched.op("sync", lambda e, c=c, cpar=cpar, sl=sl: e.dma_start(
                    out=xf[cpar][:], in_=src_v2[:, c, sl]),
                    waits=[(sDV, xf_free[cpar])], inc=sXFp[cpar], inc_n=16)
                wct = [(sXFp[cpar], xf_tok[cpar])]
                if c >= 2:
                    wct += [(sPE, pe_toks[c - 2]), (sAC, ac_toks[c - 2])]
                ct = sched.op("vector", lambda v, cpar=cpar: v.tensor_copy(xbt[cpar][:], xf[cpar][:]),
                              waits=wct, inc=sDV)
                xf_free[cpar] = ct
                w2 = [(sDV, ct)]
                if c > 0:
                    w2.append((sPE, prev_sxx))
                sq = sched.op("scalar", lambda a, cpar=cpar: a.activation(
                    xsqt[:], xbt[cpar][:], AF.Square), waits=w2, inc=sAC)
                ac_toks[c] = sq
                w3 = [(sDV, ct)]
                if c == 0:
                    w3.append((sDV, lnst["psD_free"]))
                sched.op("tensor", lambda t, c=c, cpar=cpar: t.matmul(
                    psD[0:1, :], onescol[:], xbt[cpar][:],
                    start=(c == 0), stop=(c == NCH - 1)), waits=w3)
                w4 = [(sAC, sq)]
                if c == 0:
                    w4.append((sDV, lnst["psD2_free"]))
                prev_sxx = sched.op("tensor", lambda t, c=c: t.matmul(
                    psD2[0:1, :], onescol[:], xsqt[:],
                    start=(c == 0), stop=(c == NCH - 1)), waits=w4, inc=sPE)
                pe_toks[c] = prev_sxx
            t1 = sched.op("vector", lambda v: v.tensor_scalar(
                rows[:, 0:512], psD[0:1, :], 1.0 / D, None, OP.mult),
                waits=[(sPE, prev_sxx), (sPE, lnst["rows_free_pe"])], inc=sDV)
            sched.op("vector", lambda v: v.tensor_scalar(
                rows[:, 1024:1536], psD2[0:1, :], 1.0 / D, float(LN_EPS),
                OP.mult, OP.add), inc=sDV)
            t3 = sched.op("scalar", lambda a: a.activation(
                rows[:, 512:1024], rows[:, 0:512], AF.Square), waits=[(sDV, t1)], inc=sAC)
            t4 = sched.op("vector", lambda v: v.tensor_sub(
                rows[:, 1536:2048], rows[:, 1024:1536], rows[:, 512:1024]),
                waits=[(sAC, t3)], inc=sDV)
            t5 = sched.op("scalar", lambda a: a.activation(
                rows[:, 2048:2560], rows[:, 1536:2048], AF.Ln), waits=[(sDV, t4)], inc=sAC)
            t6 = sched.op("scalar", lambda a: a.activation(
                rows2[:, 0:512], rows[:, 2048:2560], AF.Exp, scale=-0.5),
                waits=[(sAC, t5)], inc=sAC)
            sched.op("vector", lambda v: v.tensor_copy(rows[:, 2560:3072], rows[:, 0:512]), inc=sDV)
            t8 = sched.op("vector", lambda v: v.tensor_copy(rows[:, 3072:3584], rows2[:, 0:512]),
                          waits=[(sAC, t6)], inc=sDV)
            b1 = sched.op("tensor", lambda t: t.matmul(
                psD[:, :], ones1[:], rows[0:1, 2560:3072], start=True, stop=True),
                waits=[(sDV, t8)], inc=sPE)
            b2 = sched.op("tensor", lambda t: t.matmul(
                psD2[:, :], ones1[:], rows[0:1, 3072:3584], start=True, stop=True), inc=sPE)
            lnst["rows_free_pe"] = b2
            m1 = sched.op("vector", lambda v: v.tensor_copy(mbv[:], psD[:, :]),
                          waits=[(sPE, b2)], inc=sDV)
            lnst["psD_free"] = m1
            i1 = sched.op("vector", lambda v: v.tensor_copy(ibv[:], psD2[:, :]), inc=sDV)
            lnst["psD2_free"] = i1
            ln_last = 0
            for c in range(NCH):
                cpar = c % 2
                xf_tok[cpar] = sched.op("sync", lambda e, c=c, cpar=cpar, sl=sl: e.dma_start(
                    out=xf[cpar][:], in_=src_v2[:, c, sl]),
                    waits=[(sDV, xf_free[cpar])], inc=sXFp[cpar], inc_n=16)
                w = [(sXFp[cpar], xf_tok[cpar]), (sDV, i1)]
                if c == 0:
                    w.append((sPE, ln_free_pe))
                ct = sched.op("vector", lambda v, cpar=cpar: v.tensor_sub(
                    lnf[:], xf[cpar][:], mbv[:]), waits=w, inc=sDV)
                xf_free[cpar] = ct
                ln_last = sched.op("vector", lambda v, c=c: v.tensor_mul(
                    ln_out[:, c], lnf[:], ibv[:]), inc=sDV)
            lnst["xf_tok0"], lnst["xf_tok1"] = xf_tok
            lnst["xf_free0"], lnst["xf_free1"] = xf_free
            return ln_last

        arena.push()
        lbufs = ([arena.take(F32, 512) for _ in range(2)],
                 [arena.take(BF16, 512) for _ in range(2)],
                 arena.take(BF16, 512), arena.take(F32, 512),
                 arena.take(F32, 512), arena.take(F32, 512))
        lbufs = (lbufs[0], lbufs[1], lbufs[2], lbufs[3], lbufs[4], lbufs[5])
        ln3 = arena.take(BF16, NCH, 1024)
        wql = [arena.take(BF16, NCH, 128) for _ in range(2)]
        wkl = [arena.take(BF16, NCH, 128) for _ in range(2)]
        wvl = [arena.take(BF16, NCH, 128) for _ in range(2)]
        kt = arena.take(BF16, 1024)
        qt = arena.take(BF16, 1024)
        vtf = arena.take(BF16, 1024)
        fk_sl = arena.take(BF16, 1024)
        fq_sl = arena.take(BF16, 1024)
        ek_tok = arena.take(BF16, 8, 128)
        fkn_sl = arena.take(BF16, 8, 128)
        v2_sl = arena.take(BF16, 8, 130)
        sks = arena.take(F32, 8, 2)
        accS = arena.take(F32, 16, 65)

        st = {"w_tok": [0, 0], "w_free": [0, 0], "ln_free": 0,
              "kt_free": 0, "qt_free": 0, "vtf_free": 0,
              "fk_store": 0, "fq_store": 0, "fkn_store": 0, "v2_store": 0,
              "psB_ac": 0, "psB_dv": 0, "ek_free": 0, "fkn_gp": 0,
              "psA_ev": 0, "zk_tev": 0, "state_pe": 0,
              "tr_ev": [0, 0], "wt3": [0, 0]}

        sched.op("gpsimd", lambda g: g.memset(v2_sl[:, :, 64:65], 1.0), inc=sGP)
        sched.op("gpsimd", lambda g: g.memset(v2_sl[:, :, 129:130], 1.0), inc=sGP)
        ones_gp = sGP.v

        sched.op("sync", None, waits=[(sRS, sRS.v),
                                      (sWBSp[0], sWBSp[0].v), (sWBSp[1], sWBSp[1].v)])
        NSL3 = T // 1024
        for s in range(NSL3):
            sl_lo = slice(s * 1024, s * 1024 + 512)
            sl_hi = slice(s * 1024 + 512, (s + 1) * 1024)
            emit_ln512(res2_v, sl_lo,
                       ln3.rearrange("p c (h w) -> p (c h) w", h=2)[:, 0::2] if False else ln3[:, :, 0:512],
                       lbufs, st["ln_free"])
            lnlast = emit_ln512(res2_v, sl_hi, ln3[:, :, 512:1024], lbufs, st["ln_free"])
            def wb_load(hg):
                par = hg % 2
                sched.op("sync", lambda e, hg=hg, par=par: e.dma_start(
                    out=wql[par][:], in_=WbT[0, hg].rearrange("ck p q -> p ck q")),
                    waits=[(sPE, st["w_free"][par])], inc=sW3p[par], inc_n=16)
                sched.op("sync", lambda e, hg=hg, par=par: e.dma_start(
                    out=wkl[par][:], in_=WbT[1, hg].rearrange("ck p q -> p ck q")),
                    inc=sW3p[par], inc_n=16)
                st["wt3"][par] = sched.op("sync", lambda e, hg=hg, par=par: e.dma_start(
                    out=wvl[par][:], in_=WbT[2, hg].rearrange("ck p q -> p ck q")),
                    inc=sW3p[par], inc_n=16)

            wb_load(0)
            for hg in range(16):
                par = hg % 2
                if hg + 1 < 16:
                    wb_load(hg + 1)
                wt3 = st["wt3"][par]
                for c in range(NCH):
                    w = []
                    if c == 0:
                        w = [(sW3p[par], wt3), (sDV, lnlast), (sDV, st["psA_ev"])]
                    for half in range(2):
                        sched.op("tensor", lambda t, c=c, par=par, half=half: t.matmul(
                            psA[:, half * 512:(half + 1) * 512], wkl[par][:, c],
                            ln3[:, c, half * 512:(half + 1) * 512],
                            start=(c == 0), stop=(c == NCH - 1)),
                            waits=w if half == 0 else (),
                            inc=sPE if (c == NCH - 1 and half == 1) else None)
                kmm = sPE.v
                ktv = sched.op("vector", lambda v: v.tensor_copy(kt[:], psA[:, :]),
                               waits=[(sPE, kmm), (sPE, st["kt_free"])], inc=sDV)
                st["psA_ev"] = ktv
                for half in range(2):
                    sched.op("tensor", lambda t, hg=hg, half=half: t.matmul(
                        psB[:, half * 512:(half + 1) * 512], bdk_all[:, hg],
                        kt[:, half * 512:(half + 1) * 512], start=True, stop=True),
                        waits=[(sDV, ktv), (sBD, bd_tok), (sAC, st["psB_ac"]),
                               (sDV, st["psB_dv"])] if half == 0 else (),
                        inc=sPE if half == 1 else None)
                fkmm = sPE.v
                fke = sched.op("scalar", lambda a: a.activation(fk_sl[:], psB[:, :], AF.Exp),
                               waits=[(sPE, fkmm), (sFK, st["fk_store"])], inc=sAC)
                st["psB_ac"] = fke
                st["fk_store"] = sched.op("sync", lambda e, hg=hg, s=s: e.dma_start(
                    out=fksp[hg][:, s * 1024:(s + 1) * 1024], in_=fk_sl[:]),
                    waits=[(sAC, fke)], inc=sFK, inc_n=16)
                for tt_ in range(8):
                    sched.op("tensor", lambda t, hg=hg, tt_=tt_: t.matmul(
                        psC[:, tt_ * 128:(tt_ + 1) * 128], kt[:, tt_ * 128:(tt_ + 1) * 128],
                        bdk_all[:, hg], start=True, stop=True),
                        waits=[(sDV, ktv), (sAC, st["ek_free"]), (sDV, st["zk_tev"])] if tt_ == 0 else (),
                        inc=sPE if tt_ == 7 else None)
                zt = sPE.v
                st["kt_free"] = zt
                eke = sched.op("scalar", lambda a: a.activation(
                    ek_tok.rearrange("p a b -> p (a b)"), psC[:, :], AF.Exp),
                    waits=[(sPE, zt), (sGP, st["fkn_gp"])], inc=sAC)
                st["ek_free"] = eke
                rdt = sched.op("vector", lambda v: v.reduce_sum(
                    sks[:], ek_tok.rearrange("p a (h f) -> p a h f", h=2),
                    axis=mybir.AxisListType.X),
                    waits=[(sAC, eke)], inc=sDV)
                rct = sched.op("vector", lambda v, hg=hg, s=s: v.reciprocal(
                    rsk_all[:, hg, s * 8:(s + 1) * 8], sks[:]),
                    waits=[(sDV, rdt)], inc=sDV)
                fknt = 0
                for tt_ in range(8):
                    for hh in range(2):
                        w = []
                        if tt_ == 0 and hh == 0:
                            w = [(sDV, rct), (sFKN, st["fkn_store"]),
                                 (sPE, st["state_pe"])]
                        fknt = sched.op("gpsimd", lambda g, tt_=tt_, hh=hh, hg=hg, s=s: g.tensor_scalar(
                            fkn_sl[:, tt_, hh * 64:(hh + 1) * 64],
                            ek_tok[:, tt_, hh * 64:(hh + 1) * 64],
                            rsk_all[:, hg, s * 8 + tt_, hh:hh + 1], None, OP.mult),
                            waits=w, inc=sGP)
                st["fkn_gp"] = fknt
                st["fkn_store"] = sched.op("sync", lambda e, hg=hg, s=s: e.dma_start(
                    out=fknsp[hg, s * 1024:(s + 1) * 1024].rearrange("(tt p) f -> p tt f", p=128),
                    in_=fkn_sl[:]),
                    waits=[(sGP, fknt)], inc=sFKN, inc_n=16)
                for c in range(NCH):
                    w = []
                    if c == 0:
                        w = [(sAC, fke), (sDV, lnlast)]
                    for half in range(2):
                        sched.op("tensor", lambda t, c=c, par=par, half=half: t.matmul(
                            psB[:, half * 512:(half + 1) * 512], wvl[par][:, c],
                            ln3[:, c, half * 512:(half + 1) * 512],
                            start=(c == 0), stop=(c == NCH - 1)),
                            waits=w if half == 0 else (),
                            inc=sPE if (c == NCH - 1 and half == 1) else None)
                vmm = sPE.v
                vev = sched.op("vector", lambda v: v.tensor_copy(vtf[:], psB[:, :]),
                               waits=[(sPE, vmm), (sPE, st["vtf_free"])], inc=sDV)
                st["psB_dv"] = vev
                for c in range(NCH):
                    w = []
                    if c == 0:
                        w = [(sW3p[par], wt3), (sDV, lnlast), (sDV, st["psA_ev"])]
                    for half in range(2):
                        sched.op("tensor", lambda t, c=c, par=par, half=half: t.matmul(
                            psA[:, half * 512:(half + 1) * 512], wql[par][:, c],
                            ln3[:, c, half * 512:(half + 1) * 512],
                            start=(c == 0), stop=(c == NCH - 1)),
                            waits=w if half == 0 else (),
                            inc=sPE if (c == NCH - 1 and half == 1) else None)
                qmm = sPE.v
                st["w_free"][par] = qmm
                qtv = sched.op("vector", lambda v: v.tensor_copy(qt[:], psA[:, :]),
                               waits=[(sPE, qmm), (sPE, st["qt_free"])], inc=sDV)
                st["psA_ev"] = qtv
                for half in range(2):
                    sched.op("tensor", lambda t, hg=hg, half=half: t.matmul(
                        psB[:, half * 512:(half + 1) * 512], bdq_all[:, hg],
                        qt[:, half * 512:(half + 1) * 512], start=True, stop=True),
                        waits=[(sDV, qtv), (sDV, st["psB_dv"]), (sAC, st["psB_ac"])] if half == 0 else (),
                        inc=sPE if half == 1 else None)
                fqmm = sPE.v
                st["qt_free"] = fqmm
                fqe = sched.op("scalar", lambda a: a.activation(fq_sl[:], psB[:, :], AF.Exp),
                               waits=[(sPE, fqmm), (sFQ, st["fq_store"])], inc=sAC)
                st["psB_ac"] = fqe
                st["fq_store"] = sched.op("sync", lambda e, hg=hg, s=s: e.dma_start(
                    out=fqsp[hg][:, s * 1024:(s + 1) * 1024], in_=fq_sl[:]),
                    waits=[(sAC, fqe)], inc=sFQ, inc_n=16)
                tregs = [psC[:, 0:128], psC[:, 512:640]]
                tev = {}
                v2t = 0
                for tt_ in range(8):
                    reg = tregs[tt_ % 2]
                    wtr = [(sDV, vev)] if tt_ == 0 else []
                    if tt_ < 2:
                        wtr += [(sDV, st["tr_ev"][tt_ % 2]), (sAC, eke)]
                    else:
                        wtr.append((sDV, tev[tt_ - 2]))
                    trp = sched.op("tensor", lambda t, tt_=tt_, reg=reg: t.matmul(
                        reg, vtf[:, tt_ * 128:(tt_ + 1) * 128], ident[:],
                        start=True, stop=True), waits=wtr, inc=sPE)
                    wev = [(sPE, trp)]
                    if tt_ == 0:
                        wev += [(sV2, st["v2_store"]), (sPE, st["state_pe"]),
                                (sGP, ones_gp)]
                    sched.op("vector", lambda v, tt_=tt_, reg=reg: v.tensor_copy(
                        v2_sl[:, tt_, 0:64], reg[:, 0:64]),
                        waits=wev, inc=sDV)
                    v2t = sched.op("vector", lambda v, tt_=tt_, reg=reg: v.tensor_copy(
                        v2_sl[:, tt_, 65:129], reg[:, 64:128]),
                        inc=sDV)
                    tev[tt_] = v2t
                st["tr_ev"][0] = tev[6]
                st["tr_ev"][1] = tev[7]
                st["zk_tev"] = tev[7]
                st["vtf_free"] = sPE.v
                st["v2_store"] = sched.op("sync", lambda e, hg=hg, s=s: e.dma_start(
                    out=v2sp[hg, s * 1024:(s + 1) * 1024].rearrange("(tt p) d -> p tt d", p=128),
                    in_=v2_sl[:]),
                    waits=[(sDV, v2t)], inc=sV2, inc_n=16)
                for hh in range(2):
                    for tt_ in range(8):
                        first = (tt_ == 0 and hh == 0)
                        last = (tt_ == 7 and hh == 1)
                        w = []
                        if first:
                            w = [(sGP, fknt), (sDV, v2t), (sDV, lnst["psD_free"])]
                        sched.op("tensor", lambda t, tt_=tt_, hh=hh: t.matmul(
                            psD[hh * 64:(hh + 1) * 64, 0:65],
                            fkn_sl[:, tt_, hh * 64:(hh + 1) * 64],
                            v2_sl[:, tt_, hh * 65:(hh + 1) * 65],
                            start=(tt_ == 0), stop=(tt_ == 7)),
                            waits=w, inc=sPE if last else None)
                st["state_pe"] = sPE.v
                if s == 0:
                    sev = sched.op("vector", lambda v, hg=hg: v.tensor_copy(
                        accS[:, hg], psD[:, 0:65]),
                        waits=[(sPE, st["state_pe"])], inc=sDV)
                else:
                    sev = sched.op("vector", lambda v, hg=hg: v.scalar_tensor_tensor(
                        accS[:, hg], psD[:, 0:65], 0.0, accS[:, hg], OP.add, OP.add),
                        waits=[(sPE, st["state_pe"])], inc=sDV)
                lnst["psD_free"] = sev
            st["ln_free"] = sPE.v
            lnst["psD2_free"] = max(lnst["psD2_free"], sDV.v)
        cst = sched.op("sync", lambda e: e.dma_start(
            out=cstI.rearrange("p (g c) -> p g c", g=16), in_=accS[:]),
            waits=[(sDV, lnst["psD_free"])], inc=sCS, inc_n=16)
        sched.op("gpsimd", lambda g: g.collective_compute(
            "AllGather", OP.bypass,
            ins=[cstI[:]], outs=[cstO[:]],
            replica_groups=[[0, 1, 2, 3, 4, 5, 6, 7]]),
            waits=[(sCS, cst)], inc=sCC)
        arena.pop()
        barrier("p3a")

        arena.push()
        pfx_all = arena.take(F32, 8, 16, 65)
        pfxb = [arena.take(F32, 16, 65) for _ in range(2)]
        ctx = []
        for s_i in range(2):
            ctx.append({
                "fqt": arena.take(BF16, T), "fkt": arena.take(BF16, T),
                "fkn": arena.take(BF16, 16, 128), "v2": arena.take(BF16, 16, 130),
                "Sf": [arena.take(F32, 65) for _ in range(2)],
                "S_bf": arena.take(BF16, 65), "a_sb": arena.take(BF16, 2, 128),
                "rec_sb": arena.take(F32, 2), "den_sb": arena.take(F32, 2),
                "yhgT": arena.take(BF16, 16, 128),
                "regA": [psA[:, s_i * 512:s_i * 512 + 128],
                         psA[:, s_i * 512 + 128:s_i * 512 + 256]],
                "regY": [psB[:, s_i * 512:s_i * 512 + 65],
                         psC[:, s_i * 512:s_i * 512 + 65]],
                "regU": (psD if s_i == 0 else psD2),
                "ld": 0, "yst": 0, "sf_wr": 0,
            })

        sched.op("sync", None, waits=[(sFK, sFK.v), (sFKN, sFKN.v), (sV2, sV2.v),
                                      (sFQ, sFQ.v), (sCS, sCS.v)])
        pfl = sched.op("sync", lambda e: e.dma_start(
            out=pfx_all[:], in_=cstO.rearrange("s p c -> p s c").rearrange(
                "p s (g c) -> p s g c", g=16)),
            waits=[(sCC, 1)], inc=sP3L, inc_n=16)
        pft = sched.op("vector", lambda v: v.tensor_scalar(
            pfxb[0][:], pfx_all[:, 0], pfx_sb[:, 0:1], None, OP.mult),
            waits=[(sP3L, pfl), (sIN, sIN.v)], inc=sDV)
        for s_ in range(1, 8):
            pft = sched.op("vector", lambda v, s_=s_: v.scalar_tensor_tensor(
                pfxb[s_ % 2][:], pfx_all[:, s_], pfx_sb[:, s_:s_ + 1],
                pfxb[(s_ + 1) % 2][:], OP.mult, OP.add), inc=sDV)
        pfx_final = pfxb[1]

        def emit_loads(cx, hg):
            sched.op("sync", lambda e, hg=hg, cx=cx: e.dma_start(
                out=cx["fqt"][:], in_=fqsp[hg]), waits=[(sDV, cx["ld"])],
                inc=sP3L, inc_n=16)
            sched.op("sync", lambda e, hg=hg, cx=cx: e.dma_start(
                out=cx["fkt"][:], in_=fksp[hg]), inc=sP3L, inc_n=16)
            for qtr in range(2):
                sched.op("sync", lambda e, hg=hg, qtr=qtr, cx=cx: e.dma_start(
                    out=cx["fkn"][:, qtr * 8:(qtr + 1) * 8],
                    in_=fknsp[hg, qtr * 1024:(qtr + 1) * 1024].rearrange("(tt p) f -> p tt f", p=128)),
                    inc=sP3L, inc_n=16)
                sched.op("sync", lambda e, hg=hg, qtr=qtr, cx=cx: e.dma_start(
                    out=cx["v2"][:, qtr * 8:(qtr + 1) * 8],
                    in_=v2sp[hg, qtr * 1024:(qtr + 1) * 1024].rearrange("(tt p) d -> p tt d", p=128)),
                    inc=sP3L, inc_n=16)
            cx["ldt"] = sP3L.v

        def emit_init(cx, hg):
            cSf = sched.op("vector", lambda v, hg=hg, cx=cx: v.tensor_copy(
                cx["Sf"][0][:], pfx_final[:, hg]),
                waits=[(sDV, pft), (sDV, cx["sf_wr"])], inc=sDV)
            cx["sbf"] = sched.op("vector", lambda v, cx=cx: v.tensor_copy(
                cx["S_bf"][:], cx["Sf"][0][:]),
                waits=[(sDV, cSf), (sP3L, cx["ldt"])], inc=sDV)
            cx["aev"] = [0, 0]
            cx["yfh"] = [0, 0]
            cx["ygrp"] = 0
            cx["add2"] = 0

        def emit_chunk_pair(cxs, hgs, ch):
            for cx in cxs:
                if ch > 0:
                    cx["sbf"] = sched.op("vector", lambda v, c=ch % 2, cx=cx: v.tensor_copy(
                        cx["S_bf"][:], cx["Sf"][c][:]),
                        waits=[(sDV, cx["add2"]), (sPE, cx["ygrp"])], inc=sDV)
            for hh in range(2):
                for cx in cxs:
                    amw = [(sDV, max(cx["aev"]))]
                    if ch == 0 and hh == 0:
                        amw.append((sDV, cx["sbf"]))
                    cx["amm"] = sched.op("tensor", lambda t, hh=hh, ch=ch, cx=cx: t.matmul(
                        cx["regA"][hh], cx["fkt"][hh * 64:(hh + 1) * 64, ch * 128:(ch + 1) * 128],
                        cx["fqt"][hh * 64:(hh + 1) * 64, ch * 128:(ch + 1) * 128],
                        start=True, stop=True), waits=amw, inc=sPE)
                for s_i, cx in enumerate(cxs):
                    cx["aev"][hh] = sched.op("vector", lambda v, hh=hh, hg=hgs[s_i], ch=ch, cx=cx: v.scalar_tensor_tensor(
                        cx["a_sb"][:, hh], cx["regA"][hh], rsk_all[:, hg, ch, hh:hh + 1],
                        maskT[:], OP.mult, OP.mult),
                        waits=[(sPE, cx["amm"])], inc=sDV)
            for hh in range(2):
                for cx in cxs:
                    sched.op("tensor", lambda t, hh=hh, ch=ch, cx=cx: t.matmul(
                        cx["regY"][hh], cx["a_sb"][:, hh],
                        cx["v2"][:, ch, hh * 65:(hh + 1) * 65],
                        start=True, stop=False),
                        waits=[(sDV, cx["aev"][hh]), (sDV, cx["yfh"][hh])])
                    sched.op("tensor", lambda t, hh=hh, ch=ch, cx=cx: t.matmul(
                        cx["regY"][hh], cx["fqt"][hh * 64:(hh + 1) * 64, ch * 128:(ch + 1) * 128],
                        cx["S_bf"][hh * 64:(hh + 1) * 64, 0:65],
                        start=False, stop=False), waits=[(sDV, cx["sbf"])])
                    cx["ygrp"] = sched.op("tensor", lambda t, hh=hh, ch=ch, cx=cx: t.matmul(
                        cx["regY"][hh][:, 64:65],
                        cx["fqt"][hh * 64:(hh + 1) * 64, ch * 128:(ch + 1) * 128],
                        epscol[hh * 64:(hh + 1) * 64, 0:1],
                        start=False, stop=True), inc=sPE)
            for hh in range(2):
                for cx in cxs:
                    ltk = sched.op("scalar", lambda a, hh=hh, cx=cx: a.activation(
                        cx["den_sb"][:, hh:hh + 1], cx["regY"][hh][:, 64:65], AF.Ln),
                        waits=[(sPE, cx["ygrp"]), (sDV, cx["yfh"][hh])], inc=sAC)
                    rtk = sched.op("scalar", lambda a, hh=hh, cx=cx: a.activation(
                        cx["rec_sb"][:, hh:hh + 1], cx["den_sb"][:, hh:hh + 1],
                        AF.Exp, scale=-1.0), inc=sAC)
                    wyf = [(sAC, rtk)]
                    if ch == 0 and hh == 0:
                        wyf.append((sYS, cx["yst"]))
                    cx["yfh"][hh] = sched.op("vector", lambda v, hh=hh, ch=ch, cx=cx: v.tensor_scalar(
                        cx["yhgT"][:, ch, hh * 64:(hh + 1) * 64], cx["regY"][hh][:, 0:64],
                        cx["rec_sb"][:, hh:hh + 1], None, OP.mult),
                        waits=wyf, inc=sDV)
            for cx in cxs:
                supd = 0
                for hh in range(2):
                    w = [(sDV, cx["sbf"])] if hh == 0 else ()
                    supd = sched.op("tensor", lambda t, hh=hh, ch=ch, cx=cx: t.matmul(
                        cx["regU"][hh * 64:(hh + 1) * 64, 0:65],
                        cx["fkn"][:, ch, hh * 64:(hh + 1) * 64],
                        cx["v2"][:, ch, hh * 65:(hh + 1) * 65],
                        start=True, stop=True), waits=w,
                        inc=sPE if hh == 1 else None)
                cx["supd"] = supd
            for cx in cxs:
                c, n = ch % 2, (ch + 1) % 2
                cx["add2"] = sched.op("vector", lambda v, c=c, n=n, cx=cx: v.scalar_tensor_tensor(
                    cx["Sf"][n][:, 0:65], cx["regU"][:, 0:65], 0.0, cx["Sf"][c][:, 0:65],
                    OP.add, OP.add), waits=[(sPE, cx["supd"])], inc=sDV)

        for pair in range(8):
            hgs = (2 * pair, 2 * pair + 1)
            for s_i in range(2):
                emit_loads(ctx[s_i], hgs[s_i])
            for s_i in range(2):
                emit_init(ctx[s_i], hgs[s_i])
            for ch in range(16):
                emit_chunk_pair(ctx, hgs, ch)
            for s_i in range(2):
                cx = ctx[s_i]
                cx["yst"] = sched.op("sync", lambda e, hg=hgs[s_i], cx=cx: e.dma_start(
                    out=Yall2[hg].rearrange("(c p) h -> p c h", p=128), in_=cx["yhgT"][:]),
                    waits=[(sDV, cx["yfh"][1])], inc=sYS, inc_n=16)
                cx["ld"] = sDV.v
                cx["sf_wr"] = cx["add2"]
        arena.pop()
        barrier("p3b")

        arena.push()
        yblk = [arena.take(BF16, NCH, 1024) for _ in range(2)]
        yld = [arena.take(BF16, 8, 128) for _ in range(2)]
        wol = [arena.take(BF16, NCH, 128) for _ in range(2)]
        rt3 = arena.take(F32, 1024)
        outf3 = arena.take(F32, 1024)
        st3 = {"w_tok": [0, 0], "w_free": [0, 0], "ev": 0, "store": 0, "rtf": 0,
               "yld_free": [(sPE, 0), (sPE, 0)], "tev": [0, 0],
               "yblk_free": [0, 0], "tr_done": [0, 0]}
        sched.op("sync", None, waits=[(sYS, sYS.v), (sRS, sRS.v)])

        def emit_ytr_c(blk, c):
            bpar = blk % 2
            if True:
                par = c % 2
                ld = sched.op("sync", lambda e, c=c, blk=blk, par=par: e.dma_start(
                    out=yld[par][:],
                    in_=Yall2[c, blk * 1024:(blk + 1) * 1024].rearrange(
                        "(tt p) h -> p tt h", p=128)),
                    waits=[st3["yld_free"][par]], inc=sYB, inc_n=16)
                for tt in range(8):
                    reg = (psD if tt % 2 == 0 else psD2)[:, 128:256]
                    w = [(sDV, st3["tev"][tt % 2])]
                    if tt == 0:
                        w.append((sYB, ld))
                        w.append((sPE, st3["yblk_free"][bpar]))
                    trp = sched.op("tensor", lambda t, tt=tt, par=par, reg=reg: t.matmul(
                        reg, yld[par][:, tt], ident[:], start=True, stop=True),
                        waits=w, inc=sPE)
                    st3["tev"][tt % 2] = sched.op("vector",
                        lambda v, tt=tt, c=c, reg=reg, bpar=bpar: v.tensor_copy(
                            yblk[bpar][:, c, tt * 128:(tt + 1) * 128], reg),
                        waits=[(sPE, trp)], inc=sDV)
                st3["yld_free"][par] = (sPE, sPE.v)
            st3["tr_done"][bpar] = st3["tev"][(8 - 1) % 2]

        for c in range(NCH):
            emit_ytr_c(0, c)
        for blk in range(T // 1024):
            bpar = blk % 2
            bsl = slice(blk * 1024, (blk + 1) * 1024)
            for ot in range(NCH):
                if blk + 1 < T // 1024:
                    emit_ytr_c(blk + 1, ot)
                par = ot % 2
                st3["w_tok"][par] = sched.op("sync", lambda e, ot=ot, par=par: e.dma_start(
                    out=wol[par][:], in_=WoTb[:, ot].rearrange("ck p q -> p ck q")),
                    waits=[(sPE, st3["w_free"][par])], inc=sW3p[par], inc_n=16)
                rtt = sched.op("sync", lambda e, ot=ot, bsl=bsl: e.dma_start(
                    out=rt3[:], in_=res2_v[:, ot, bsl]),
                    waits=[(sDV, st3["rtf"])], inc=sXR, inc_n=16)
                pp = psA if ot % 2 == 0 else psB
                for c in range(NCH):
                    w = []
                    if c == 0:
                        w = [(sW3p[par], st3["w_tok"][par]),
                             (sDV, st3["tr_done"][bpar]), (sDV, st3["ev"])]
                    for half in range(2):
                        sched.op("tensor", lambda t, c=c, par=par, half=half, pp=pp, bpar=bpar: t.matmul(
                            pp[:, half * 512:(half + 1) * 512], wol[par][:, c],
                            yblk[bpar][:, c, half * 512:(half + 1) * 512],
                            start=(c == 0), stop=(c == NCH - 1)),
                            waits=w if half == 0 else (),
                            inc=sPE if (c == NCH - 1 and half == 1) else None)
                omm = sPE.v
                st3["w_free"][par] = omm
                ev = sched.op("vector", lambda v, pp=pp: v.tensor_add(outf3[:], pp[:, :], rt3[:]),
                              waits=[(sPE, omm), (sXR, rtt), (sRS, st3["store"])], inc=sDV)
                st3["ev"] = ev
                st3["rtf"] = ev
                st3["store"] = sched.op("sync", lambda e, ot=ot, bsl=bsl: e.dma_start(
                    out=res3_v[:, ot, bsl], in_=outf3[:]),
                    waits=[(sDV, ev)], inc=sRS, inc_n=16)
            st3["yblk_free"][bpar] = sPE.v
        arena.pop()
        arena.pop()

    def copy_out(src_v, ncols):
        arena.push()
        cp = arena.take(F32, 1024)
        sched.op("sync", None, waits=[(sRS, sRS.v), (sYS, sYS.v)])
        prev = 0
        for c in range(NCH):
            for blk in range(ncols // 1024):
                bsl = slice(blk * 1024, (blk + 1) * 1024)
                tok = sched.op("sync", lambda e, c=c, bsl=bsl: e.dma_start(
                    out=cp[:], in_=src_v[:, c, bsl]),
                    waits=[(sRS, prev)], inc=sXFp[0], inc_n=16)
                prev = sched.op("sync", lambda e, c=c, bsl=bsl: e.dma_start(
                    out=outT_v[:, c, bsl], in_=cp[:]),
                    waits=[(sXFp[0], tok)], inc=sRS, inc_n=16)
        sched.op("sync", None, waits=[(sRS, sRS.v)])
        arena.pop()

    init_consts()
    barrier("init")
    if stop_after != "res1":
        p0_shard()
    wb_emit = make_p0_wb(wb_wbl, wb_wstg)
    p1_pass1(wb_emit=wb_emit)
    sched.op("gpsimd", lambda g: g.collective_compute(
        "AllGather", OP.bypass, ins=[ckvI[:]], outs=[ckvO[:]],
        replica_groups=[[0, 1], [2, 3], [4, 5], [6, 7]]),
        waits=[(sKV, 32)], inc=sCC2)
    sched.op("gpsimd", lambda g: g.collective_compute(
        "AllGather", OP.bypass, ins=[cwI[:]], outs=[cwO[:]],
        replica_groups=[[0, 1, 2, 3, 4, 5, 6, 7]]),
        waits=[(sWSp[0], sWSp[0].v), (sWSp[1], sWSp[1].v)], inc=sCCW)
    barrier("p1a")
    p1_pass2()
    barrier("p1b")

    arena.pop()

    if stop_after == "res1":
        copy_out(res1_v, T)
        barrier("end")
    elif stop_after == "res2":
        ffn_phase(res1_v, res2_v, T)
        barrier("p2")
        copy_out(res2_v, T)
        barrier("end")
    elif stop_after == "ffndbg":
        ffn_phase(res1_v, res2_v, T, dump=True)
        barrier("end")
    elif stop_after == "y2dump":
        ffn_phase(res1_v, res2_v, T)
        barrier("p2")
        p3_attnb()
        barrier("p3")
        sched.op("sync", None, waits=[(sYS, sYS.v), (sRS, sRS.v)])
        ob = outT.bitcast(BF16)
        for g in range(16):
            sched.op("sync", lambda e, g=g: e.dma_start(
                out=ob[:, g * 128:(g + 1) * 128], in_=Yall2[g]),
                inc=sRS, inc_n=16)
        sched.op("sync", None, waits=[(sRS, sRS.v)])
        barrier("end")
    elif stop_after == "res3":
        ffn_phase(res1_v, res2_v, T)
        barrier("p2")
        p3_attnb()
        barrier("p3")
        copy_out(res3_v, T)
        barrier("end")
    elif stop_after in ("full", "full_dbg"):
        ffn_phase(res1_v, res2_v, T)
        barrier("p2")
        p3_attnb()
        barrier("p3")
        if stop_after == "full_dbg":
            dbg_v = dbgT.rearrange("(c p) t -> p c t", p=128)
            arena.push()
            cpd = arena.take(F32, 1024)
            prevd = 0
            sched.op("sync", None, waits=[(sRS, sRS.v)])
            for c in range(NCH):
                for blk in range(T // 1024):
                    bsl = slice(blk * 1024, (blk + 1) * 1024)
                    tokd = sched.op("sync", lambda e, c=c, bsl=bsl: e.dma_start(
                        out=cpd[:], in_=res3_v[:, c, bsl]),
                        waits=[(sRS, prevd)], inc=sXFp[0], inc_n=16)
                    prevd = sched.op("sync", lambda e, c=c, bsl=bsl: e.dma_start(
                        out=dbg_v[:, c, bsl], in_=cpd[:]),
                        waits=[(sXFp[0], tokd)], inc=sRS, inc_n=16)
            arena.pop()
            barrier("dbgc")
        ffn_phase(res3_v, outT_v, T)
        barrier("p4")
        sched.op("sync", None, waits=[(sRS, sRS.v)])
        barrier("end")
    else:
        raise NotImplementedError(stop_after)

    with stack, nc.Block() as block:
        @block.sync
        def _(e):
            sched.run("sync", e)

        @block.gpsimd
        def _(e):
            sched.run("gpsimd", e)

        @block.vector
        def _(e):
            sched.run("vector", e)

        @block.scalar
        def _(e):
            sched.run("scalar", e)

        @block.tensor
        def _(e):
            sched.run("tensor", e)

    return nc


def _shard_inputs(inputs):
    x = np.asarray(inputs["L0_attn_a_x"], np.float32)
    common = {
        "Wq_a": np.ascontiguousarray(np.asarray(inputs["L0_attn_a_Wq"], np.float32)),
        "bq_a": np.ascontiguousarray(np.asarray(inputs["L0_attn_a_bq"], np.float32).reshape(NCH, 128).T),
        "Wk_a": np.ascontiguousarray(np.asarray(inputs["L0_attn_a_Wk"], np.float32)),
        "bk_a": np.asarray(inputs["L0_attn_a_bk"], np.float32).reshape(DA, 1),
        "Wv_a": np.ascontiguousarray(np.asarray(inputs["L0_attn_a_Wv"], np.float32)),
        "bv_a": np.asarray(inputs["L0_attn_a_bv"], np.float32).reshape(1, DA),
        "Wo_a": np.ascontiguousarray(np.asarray(inputs["L0_attn_a_Wo"], np.float32)),
        "bo_a": np.ascontiguousarray(np.asarray(inputs["L0_attn_a_bo"], np.float32).reshape(NCH, 128).T),
        "Wq_b": np.ascontiguousarray(np.asarray(inputs["L0_attn_b_Wq"], np.float32)),
        "Wk_b": np.ascontiguousarray(np.asarray(inputs["L0_attn_b_Wk"], np.float32)),
        "Wv_b": np.ascontiguousarray(np.asarray(inputs["L0_attn_b_Wv"], np.float32)),
        "Wo_b": np.ascontiguousarray(np.asarray(inputs["L0_attn_b_Wo"], np.float32)),
        "Wfq": np.ascontiguousarray(np.asarray(inputs["L0_attn_b_Wfq"], np.float32)),
        "Wfk": np.ascontiguousarray(np.asarray(inputs["L0_attn_b_Wfk"], np.float32)),
    }
    gup = np.asarray(inputs["L0_ffn_gate_up_packed"], np.int32)
    gus = np.asarray(inputs["L0_ffn_gate_up_scales"], np.float16)
    dp = np.asarray(inputs["L0_ffn_down_packed"], np.int32)
    dsc = np.asarray(inputs["L0_ffn_down_scales"], np.float16)
    per_core = []
    for r in range(8):
        b, h = r // 2, r % 2
        own = x[b, h * T:(h + 1) * T]
        par = x[b, (1 - h) * T:(2 - h) * T]
        m = dict(common)
        m["xT"] = np.ascontiguousarray(np.concatenate([own, par], 0).T)
        onehot = np.zeros((128, 8), np.float32)
        if h == 1:
            onehot[:, r - 1] = 1.0
        m["pfx8"] = onehot
        m["gup_sh"] = np.ascontiguousarray(gup[r * 1408:(r + 1) * 1408])
        m["gus_sh"] = np.ascontiguousarray(gus[:, r * 1408:(r + 1) * 1408])
        m["dp_sh"] = np.ascontiguousarray(dp[r * 256:(r + 1) * 256])
        m["dsc_sh"] = np.ascontiguousarray(dsc[:, r * 256:(r + 1) * 256])
        per_core.append(m)
    return per_core


def run_cores(inputs, stop_after="full"):
    nc = build(stop_after=stop_after)
    in_maps = _shard_inputs(inputs)
    return run_bass_kernel_spmd(nc, in_maps, core_ids=list(range(8)))


def kernel(**inputs):
    res = run_cores(inputs, stop_after="full")
    out = np.empty((4, S, D), np.float32)
    for r in range(8):
        b, h = r // 2, r % 2
        out[b, h * T:(h + 1) * T] = np.asarray(res.results[r]["outT"]).T
    return out
